# version 02
# baseline (speedup 1.0000x reference)
import numpy as np
import ml_dtypes
from contextlib import ExitStack

import concourse.bass as bass
import concourse.mybir as mybir
from concourse.bass_utils import run_bass_kernel_spmd

F32 = mybir.dt.float32
BF16 = mybir.dt.bfloat16
AF = mybir.ActivationFunctionType
ALU = mybir.AluOpType
AX = mybir.AxisListType

D = 2048
T = 4096
NCH = 16
TT = 512
NT = T // TT
HPAD = 5504
NHC = 43
H0REAL = 5461
LNEPS = 1e-5

BF = ml_dtypes.bfloat16



def tile_lhsT(wt):
    K, O = wt.shape
    kc, oc = K // 128, O // 128
    r = wt.reshape(kc, 128, oc, 128).transpose(2, 1, 0, 3).reshape(oc, 128, K)
    return np.ascontiguousarray(r)


def pad_to(w, rows=None, cols=None):
    r = rows if rows is not None else w.shape[0]
    c = cols if cols is not None else w.shape[1]
    if (r, c) == w.shape:
        return w
    out = np.zeros((r, c), dtype=w.dtype)
    out[:w.shape[0], :w.shape[1]] = w
    return out



class Sem:
    __slots__ = ("h", "n")

    def __init__(self, h):
        self.h = h
        self.n = 0


class Prog:
    ENGINES = ("sync", "gpsimd", "vector", "scalar", "tensor")

    def __init__(self, nc, es):
        self.nc = nc
        self.es = es
        self.ops = {e: [] for e in self.ENGINES}
        self.s = {}
        for name in ("ld", "ld2", "cast", "st", "te", "ve", "ac", "gp"):
            self.s[name] = Sem(es.enter_context(nc.semaphore("s_" + name)))
        self.dmaq = {self.s[k] for k in ("ld", "ld2", "cast", "st")}

    def step(self, eng, waits, fn, inc=None, inc_k=1, dma_pre=None):
        tgt = None
        if inc is not None and inc in self.dmaq:
            waits = list(waits) + [(inc, dma_pre if dma_pre is not None
                                    else inc.n)]
        if inc is not None:
            inc.n += inc_k
            tgt = inc.n
        waits = [(s.h, v) for (s, v) in waits if v > 0]

        def run(e, waits=waits, fn=fn, inc=inc, inc_k=inc_k):
            for h, v in waits:
                e.wait_ge(h, v)
            ins = fn(e)
            if inc is not None:
                ins.then_inc(inc.h, inc_k)
        self.ops[eng].append(run)
        return tgt

    def raw(self, eng, fn):
        self.ops[eng].append(fn)

    def barrier(self):
        vals = [(s.h, s.n) for s in self.s.values() if s.n > 0]
        for eng in self.ENGINES:
            def run(e, vals=vals):
                for h, v in vals:
                    e.wait_ge(h, v)
            self.ops[eng].append(run)

    def finalize(self):
        block = self.es.enter_context(self.nc.Block())
        for eng in self.ENGINES:
            fns = self.ops[eng]
            getattr(block, eng)(lambda e, fns=fns: [f(e) for f in fns])


_uid = [0]


def _uniq(name):
    _uid[0] += 1
    return f"{name}_{_uid[0]}"


def sbuf(p, name, shape, dtype, es=None):
    return (es or p.es).enter_context(p.nc.sbuf_tensor(_uniq(name), shape, dtype))


def psum(p, name, shape, dtype=F32, es=None):
    return (es or p.es).enter_context(p.nc.psum_tensor(_uniq(name), shape, dtype))



class NS:
    pass


def emit_consts(p):
    c = NS()
    c.ident_bf = sbuf(p, "ident_bf", [128, 128], BF16)
    c.ident_f = sbuf(p, "ident_f", [128, 128], F32)
    c.ones_col_bf = sbuf(p, "ones_col_bf", [128, 1], BF16)
    c.ones_col_f = sbuf(p, "ones_col_f", [128, 1], F32)
    c.ones_row_f = sbuf(p, "ones_row_f", [1, 128], F32)
    c.ones_row_bf = sbuf(p, "ones_row_bf", [1, 128], BF16)
    c.scale = sbuf(p, "scale_sb", [128, 16], F32)
    c.eps1 = sbuf(p, "eps1", [1, 1], F32)
    gp = p.s["gp"]

    def fn(e):
        e.memset(c.ident_bf[:], 0.0)
        e.affine_select(out=c.ident_bf[:], in_=c.ident_bf[:],
                        compare_op=ALU.not_equal, fill=1.0, base=0,
                        pattern=[[-1, 128]], channel_multiplier=1)
        e.memset(c.ident_f[:], 0.0)
        e.affine_select(out=c.ident_f[:], in_=c.ident_f[:],
                        compare_op=ALU.not_equal, fill=1.0, base=0,
                        pattern=[[-1, 128]], channel_multiplier=1)
        e.memset(c.ones_col_bf[:], 1.0)
        e.memset(c.ones_col_f[:], 1.0)
        e.memset(c.ones_row_f[:], 1.0)
        e.memset(c.eps1[:], LNEPS)
        return e.memset(c.ones_row_bf[:], 1.0)
    p.step("gpsimd", [], fn, inc=gp)
    p.barrier()
    return c



def emit_convert(p, jobs):
    if not jobs:
        return
    MAXM = max(src.shape[2] for src, _ in jobs)
    es = ExitStack()
    fb = [sbuf(p, f"cv_f{i}", [128, MAXM], F32, es) for i in range(2)]
    bb = [sbuf(p, f"cv_b{i}", [128, MAXM], BF16, es) for i in range(2)]
    ld, ve, st = p.s["ld"], p.s["ve"], p.s["st"]
    slabs = [(src, dst, g, src.shape[2])
             for src, dst in jobs for g in range(src.shape[0])]
    casts = []
    stores = []
    loads = []
    for i, (src, dst, g, M) in enumerate(slabs):
        w = [(ve, casts[i - 2])] if i >= 2 else []
        loads.append(p.step(
            "sync", w,
            lambda e, src=src, g=g, M=M, i=i: e.dma_start(
                out=fb[i % 2][:, 0:M], in_=src[g]), inc=ld, inc_k=16))
        w = [(ld, loads[i])] + ([(st, stores[i - 2])] if i >= 2 else [])
        casts.append(p.step(
            "vector", w,
            lambda e, M=M, i=i: e.tensor_copy(bb[i % 2][:, 0:M], fb[i % 2][:, 0:M]),
            inc=ve))
        stores.append(p.step(
            "scalar", [(ve, casts[i])],
            lambda e, dst=dst, g=g, M=M, i=i: e.dma_start(
                out=dst[g], in_=bb[i % 2][:, 0:M]), inc=st, inc_k=16))
    p.barrier()
    es.close()



def emit_quant(p, c, jobs):
    MAXM = max(j[0].shape[2] for j in jobs)
    es = ExitStack()
    fb = [sbuf(p, f"qt_f{i}", [128, MAXM], F32, es) for i in range(2)]
    tb = [sbuf(p, f"qt_t{i}", [128, MAXM], BF16, es) for i in range(2)]
    ab = [sbuf(p, f"qt_a{i}", [128, MAXM], F32, es) for i in range(2)]
    sg = [sbuf(p, f"qt_s{i}", [128, MAXM], F32, es) for i in range(2)]
    parts = sbuf(p, "qt_parts", [128, 64], F32, es)
    tot = sbuf(p, "qt_tot", [1, 2], F32, es)
    ps_t = psum(p, "qt_pst", [1, 2], F32, es)
    ps_b = psum(p, "qt_psb", [128, 1], F32, es)
    ld, ve, ac, te, st = (p.s[k] for k in ("ld", "ve", "ac", "te", "st"))

    for src, dst, count, col in jobs:
        G, _, M = src.shape
        loads, reds = [], []
        for g in range(G):
            w = [(ve, reds[g - 2])] if g >= 2 else []
            loads.append(p.step(
                "sync", w, lambda e, src=src, g=g, M=M: e.dma_start(
                    out=fb[g % 2][:, 0:M], in_=src[g]), inc=ld, inc_k=16))
            reds.append(p.step(
                "vector", [(ld, loads[g])],
                lambda e, g=g, M=M: e.tensor_reduce(
                    parts[:, g:g + 1], fb[g % 2][:, 0:M], AX.X, ALU.add,
                    apply_absolute_value=True), inc=ve))
        t0 = p.step("vector", [(ve, reds[-1])],
                    lambda e, G=G: e.tensor_reduce(
                        parts[:, 63:64], parts[:, 0:G], AX.X, ALU.add), inc=ve)
        t1 = p.step("tensor", [(ve, t0)],
                    lambda e: e.matmul(ps_t[:, 0:1], c.ones_col_f[:],
                                       parts[:, 63:64], start=True, stop=True),
                    inc=te)
        t2 = p.step("vector", [(te, t1)],
                    lambda e, count=count: e.tensor_scalar(
                        tot[:, 0:1], ps_t[:, 0:1], 1.0 / count, None, ALU.mult),
                    inc=ve)
        t3 = p.step("tensor", [(ve, t2)],
                    lambda e: e.matmul(ps_b[:], c.ones_row_f[:], tot[:, 0:1],
                                       start=True, stop=True), inc=te)
        t4 = p.step("vector", [(te, t3)],
                    lambda e, col=col: e.tensor_copy(
                        c.scale[:, col:col + 1], ps_b[:]), inc=ve)
        t5 = p.step("vector", [(ve, t4)],
                    lambda e, col=col: e.tensor_mul(
                        c.scale[:, col + 8:col + 9], c.scale[:, col:col + 1],
                        c.scale[:, col:col + 1]), inc=ve)
        t5 = p.step("vector", [(ve, t5)],
                    lambda e, col=col: e.tensor_scalar(
                        c.scale[:, col + 8:col + 9],
                        c.scale[:, col + 8:col + 9], 0.25, None,
                        ALU.mult), inc=ve)
        p.barrier()

        loads, sgns, muls, stores = [], [], [], []
        for g in range(G):
            w = []
            if g >= 2:
                w = [(ve, muls[g - 2]), (ac, sgns[g - 2])]
            loads.append(p.step(
                "sync", w, lambda e, src=src, g=g, M=M: e.dma_start(
                    out=fb[g % 2][:, 0:M], in_=src[g]), inc=ld, inc_k=16))
            w = [(ld, loads[g])]
            if g >= 2:
                w.append((ve, muls[g - 2]))
            sgns.append(p.step(
                "scalar", w, lambda e, g=g, M=M: e.activation(
                    sg[g % 2][:, 0:M], fb[g % 2][:, 0:M], AF.Sign), inc=ac))

            def mk_mask(e, g=g, M=M, col=col):
                e.tensor_mul(ab[g % 2][:, 0:M], fb[g % 2][:, 0:M],
                             fb[g % 2][:, 0:M])
                return e.tensor_scalar(ab[g % 2][:, 0:M], ab[g % 2][:, 0:M],
                                       c.scale[:, col + 8:col + 9], None,
                                       ALU.is_gt)
            w = [(ld, loads[g])]
            if g >= 2:
                w.append((st, stores[g - 2]))
            mk = p.step("vector", w, mk_mask, inc=ve)
            muls.append(p.step(
                "vector", [(ac, sgns[g])],
                lambda e, g=g, M=M: e.tensor_mul(
                    tb[g % 2][:, 0:M], sg[g % 2][:, 0:M], ab[g % 2][:, 0:M]),
                inc=ve))
            stores.append(p.step(
                "sync", [(ve, muls[g])],
                lambda e, dst=dst, g=g, M=M: e.dma_start(
                    out=dst[g], in_=tb[g % 2][:, 0:M]), inc=st, inc_k=16))
        p.barrier()
    es.close()



def emit_ln(p, c, x_ap, xn_ap, xt32_ap=None):
    es = ExitStack()
    xs = [sbuf(p, f"ln_x{i}", [128, NCH * TT], F32, es) for i in range(2)]
    xb = [sbuf(p, f"ln_b{i}", [128, NCH * TT], BF16, es) for i in range(2)]
    sq = sbuf(p, "ln_q", [128, 2 * TT], BF16, es)
    rows = sbuf(p, "ln_rows", [1, 4 * TT], F32, es)
    tmp = sbuf(p, "ln_tmp", [128, 2 * TT], F32, es)
    xnb = sbuf(p, "ln_xn", [128, 2 * TT], BF16, es)
    xf = sbuf(p, "ln_xf", [128, 2 * TT], F32, es) if xt32_ap is not None else None
    ps_m = psum(p, "ln_psm", [1, TT], F32, es)
    ps_s = psum(p, "ln_pss", [1, TT], F32, es)
    ps_bm = psum(p, "ln_bm", [128, TT], F32, es)
    ps_br = psum(p, "ln_br", [128, TT], F32, es)
    ld, ve, ac, te, st = (p.s[k] for k in ("ld", "ve", "ac", "te", "st"))

    norm_last = {0: 0, 1: 0}
    mm_last = {0: 0, 1: 0}
    mm_prev_tile = 0
    stats_done = 0
    norm_prev_tile = 0

    for t in range(NT):
        par = t % 2
        loads, casts, sqs, mms = [], [], [], []
        for ci in range(NCH):
            w = [(ve, norm_last[par])] if ci == 0 else []
            loads.append(p.step(
                "sync", w, lambda e, ci=ci, t=t, par=par: e.dma_start(
                    out=xs[par][:, ci * TT:(ci + 1) * TT],
                    in_=x_ap[ci * 128:(ci + 1) * 128, t * TT:(t + 1) * TT]),
                inc=ld, inc_k=16))
        for ci in range(NCH):
            w = [(ld, loads[ci])]
            if ci == 0:
                w.append((te, mm_last[par]))
            casts.append(p.step(
                "vector", w, lambda e, ci=ci, par=par: e.tensor_copy(
                    xb[par][:, ci * TT:(ci + 1) * TT],
                    xs[par][:, ci * TT:(ci + 1) * TT]), inc=ve))
            w = [(ve, casts[ci])]
            w.append((te, mms[ci - 2] if ci >= 2 else mm_prev_tile))
            sqs.append(p.step(
                "scalar", w, lambda e, ci=ci, par=par: e.activation(
                    sq[:, (ci % 2) * TT:(ci % 2 + 1) * TT],
                    xb[par][:, ci * TT:(ci + 1) * TT], AF.Square), inc=ac))

            def mk_mm(e, ci=ci, par=par):
                e.matmul(ps_m[:], c.ones_col_bf[:],
                         xb[par][:, ci * TT:(ci + 1) * TT],
                         start=(ci == 0), stop=(ci == NCH - 1))
                return e.matmul(ps_s[:], c.ones_col_bf[:],
                                sq[:, (ci % 2) * TT:(ci % 2 + 1) * TT],
                                start=(ci == 0), stop=(ci == NCH - 1))
            w = [(ve, casts[ci]), (ac, sqs[ci])]
            if ci == 0 and t > 0:
                w.append((ve, stats_done))
            mms.append(p.step("tensor", w, mk_mm, inc=te))
        mm_last[par] = mms[-1]
        mm_prev_tile = mms[-1]

        s0 = p.step("vector", [(te, mms[-1])],
                    lambda e: e.tensor_scalar(rows[:, 0:TT], ps_m[:], 1.0 / D,
                                              None, ALU.mult), inc=ve)
        s1 = p.step("vector", [],
                    lambda e: e.tensor_mul(rows[:, TT:2 * TT], rows[:, 0:TT],
                                           rows[:, 0:TT]), inc=ve)
        s2 = p.step("vector", [],
                    lambda e: e.scalar_tensor_tensor(
                        rows[:, TT:2 * TT], ps_s[:], 1.0 / D,
                        rows[:, TT:2 * TT], ALU.mult, ALU.subtract), inc=ve)
        s3 = p.step("scalar", [(ve, s2)],
                    lambda e: e.activation(rows[:, 2 * TT:3 * TT],
                                           rows[:, TT:2 * TT], AF.Sqrt,
                                           bias=c.eps1[:, 0:1]), inc=ac)
        s4 = p.step("vector", [(ac, s3)],
                    lambda e: e.reciprocal(rows[:, 3 * TT:4 * TT],
                                           rows[:, 2 * TT:3 * TT]), inc=ve)
        stats_done = s4

        def mk_bc(e):
            e.matmul(ps_bm[:], c.ones_row_f[:], rows[:, 0:TT],
                     start=True, stop=True)
            return e.matmul(ps_br[:], c.ones_row_f[:], rows[:, 3 * TT:4 * TT],
                            start=True, stop=True)
        bc = p.step("tensor", [(ve, s4), (ve, norm_prev_tile)], mk_bc, inc=te)

        norms, stores = [], []
        for ci in range(NCH):
            def mk_n(e, ci=ci, par=par):
                sl = slice((ci % 2) * TT, (ci % 2 + 1) * TT)
                e.tensor_sub(tmp[:, sl], xs[par][:, ci * TT:(ci + 1) * TT],
                             ps_bm[:])
                ins = e.tensor_mul(xnb[:, sl], tmp[:, sl], ps_br[:])
                if xf is not None:
                    ins = e.tensor_mul(xf[:, sl], tmp[:, sl], ps_br[:])
                return ins
            w = [(te, bc)]
            if ci >= 2:
                w.append((st, stores[ci - 2]))
            norms.append(p.step("vector", w, mk_n, inc=ve))

            def mk_st(e, ci=ci, t=t):
                sl = slice((ci % 2) * TT, (ci % 2 + 1) * TT)
                if xt32_ap is not None:
                    e.dma_start(
                        out=xt32_ap[ci * 128:(ci + 1) * 128, t * TT:(t + 1) * TT],
                        in_=xf[:, sl]).then_inc(st.h, 16)
                return e.dma_start(
                    out=xn_ap[ci * 128:(ci + 1) * 128, t * TT:(t + 1) * TT],
                    in_=xnb[:, sl])
            pre = st.n
            if xt32_ap is not None:
                st.n += 16
            sval = p.step("scalar", [(ve, norms[ci])], mk_st, inc=st,
                          inc_k=16, dma_pre=pre)
            stores.append(sval)
        norm_last[par] = norms[-1]
        norm_prev_tile = norms[-1]
    p.barrier()
    es.close()



def emit_l0ffn(p, c, x_src, x_dst, xn_ap, g_ap, w1q_ap, w3q_ap, w2q_ap):
    ld, ld2, ve, ac, te, st = (p.s[k] for k in ("ld", "ld2", "ve", "ac", "te", "st"))

    p.step("vector", [], lambda e: e.tensor_mul(
        c.scale[:, 3:4], c.scale[:, 1:2], c.scale[:, 2:3]), inc=ve)
    p.barrier()

    esA = ExitStack()
    xnres = sbuf(p, "ff_xn", [128, NCH * T], BF16, esA)
    wb = {(j, i): sbuf(p, f"ff_w{j}{i}", [128, 2048], BF16, esA)
          for j in (1, 3) for i in range(2)}
    hs = [sbuf(p, f"ff_h{i}", [128, TT], F32, esA) for i in range(2)]
    gb = [sbuf(p, f"ff_g{i}", [128, TT], BF16, esA) for i in range(2)]
    ps1 = [psum(p, f"ff_p1{i}", [128, TT], F32, esA) for i in range(2)]
    ps3 = [psum(p, f"ff_p3{i}", [128, TT], F32, esA) for i in range(2)]

    xn_loads = []
    for ci in range(NCH):
        xn_loads.append(p.step(
            "sync", [], lambda e, ci=ci: e.dma_start(
                out=xnres[:, ci * T:(ci + 1) * T],
                in_=xn_ap[ci * 128:(ci + 1) * 128, :]), inc=ld, inc_k=16))
    xn_ready = xn_loads[-1]

    wloads = []
    silu_t, gmul_t, store_t, mm_t = {}, {}, {}, {}
    for hc in range(NHC):
        wpar = hc % 2

        def mk_wl(e, hc=hc, wpar=wpar):
            e.dma_start(out=wb[(1, wpar)][:], in_=w1q_ap[hc]).then_inc(ld2.h, 16)
            return e.dma_start(out=wb[(3, wpar)][:], in_=w3q_ap[hc])
        w = [(te, mm_t.get((hc - 2, NT - 1), 0))]
        pre = ld2.n
        ld2.n += 16
        wloads.append(p.step("scalar", w, mk_wl, inc=ld2, inc_k=16,
                             dma_pre=pre))

        for t in range(NT):
            par = t % 2

            def mk_mm(e, hc=hc, t=t, par=par, wpar=wpar):
                ins = None
                for ci in range(NCH):
                    e.matmul(ps1[par][:], wb[(1, wpar)][:, ci * 128:(ci + 1) * 128],
                             xnres[:, ci * T + t * TT:ci * T + (t + 1) * TT],
                             start=(ci == 0), stop=(ci == NCH - 1))
                for ci in range(NCH):
                    ins = e.matmul(ps3[par][:], wb[(3, wpar)][:, ci * 128:(ci + 1) * 128],
                                   xnres[:, ci * T + t * TT:ci * T + (t + 1) * TT],
                                   start=(ci == 0), stop=(ci == NCH - 1))
                return ins
            w = [(ld2, wloads[hc]), (ld, xn_ready)]
            key_prev = (hc, t - 2) if t >= 2 else (hc - 1, NT - 2 + t)
            if key_prev in silu_t:
                w += [(ac, silu_t[key_prev]), (ve, gmul_t[key_prev])]
            mm_t[(hc, t)] = p.step("tensor", w, mk_mm, inc=te)

            silu_t[(hc, t)] = p.step(
                "scalar", [(te, mm_t[(hc, t)])],
                lambda e, par=par: e.activation(
                    hs[par][:], ps1[par][:], AF.Silu,
                    scale=c.scale[:, 0:1]), inc=ac)

            w = [(ac, silu_t[(hc, t)])]
            key_prev = (hc, t - 2) if t >= 2 else (hc - 1, NT - 2 + t)
            if key_prev in store_t:
                w.append((st, store_t[key_prev]))
            gmul_t[(hc, t)] = p.step(
                "vector", w, lambda e, par=par: e.tensor_mul(
                    gb[par][:], hs[par][:], ps3[par][:]), inc=ve)

            store_t[(hc, t)] = p.step(
                "sync", [(ve, gmul_t[(hc, t)])],
                lambda e, hc=hc, t=t, par=par: e.dma_start(
                    out=g_ap[hc][:, t * TT:(t + 1) * TT], in_=gb[par][:]),
                inc=st, inc_k=16)
    p.barrier()
    esA.close()

    esB = ExitStack()
    gres = [sbuf(p, f"fb_g{i}", [128, NHC * TT], BF16, esB) for i in range(2)]
    w2b = [sbuf(p, f"fb_w{i}", [128, HPAD], BF16, esB) for i in range(2)]
    xo = [sbuf(p, f"fb_x{i}", [128, TT], F32, esB) for i in range(2)]
    xnew = [sbuf(p, f"fb_n{i}", [128, TT], F32, esB) for i in range(2)]
    psy = [psum(p, f"fb_y{i}", [128, TT], F32, esB) for i in range(2)]

    gload_t = {}
    w2load_t = {}
    mmB_t = {}
    epi_t = {}
    stB_t = {}
    xload_t = {}
    for t in range(NT):
        gpar = t % 2

        def mk_gl(e, t=t, gpar=gpar):
            ins = None
            for hc in range(NHC):
                if ins is not None:
                    ins.then_inc(ld.h, 16)
                ins = e.dma_start(out=gres[gpar][:, hc * TT:(hc + 1) * TT],
                                  in_=g_ap[hc][:, t * TT:(t + 1) * TT])
            return ins
        w = []
        if t >= 2:
            w.append((te, mmB_t[(t - 2, NCH - 1)]))
        pre = ld.n
        ld.n += 16 * (NHC - 1)
        gload_t[t] = p.step("sync", w, mk_gl, inc=ld, inc_k=16, dma_pre=pre)

        for oc in range(NCH):
            i = t * NCH + oc
            wpar = i % 2
            w = []
            if i >= 2:
                w.append((te, mmB_t[((i - 2) // NCH, (i - 2) % NCH)]))
            w2load_t[i] = p.step(
                "scalar", w, lambda e, oc=oc, wpar=wpar: e.dma_start(
                    out=w2b[wpar][:], in_=w2q_ap[oc]), inc=ld2, inc_k=16)
            xload_t[i] = p.step(
                "sync", [(ve, epi_t.get(i - 2, 0))],
                lambda e, oc=oc, t=t, wpar=wpar: e.dma_start(
                    out=xo[wpar][:], in_=x_src[oc * 128:(oc + 1) * 128,
                                              t * TT:(t + 1) * TT]),
                inc=ld, inc_k=16)

            def mk_mmB(e, t=t, oc=oc, gpar=gpar, wpar=wpar):
                ins = None
                for hc in range(NHC):
                    ins = e.matmul(psy[wpar][:],
                                   w2b[wpar][:, hc * 128:(hc + 1) * 128],
                                   gres[gpar][:, hc * TT:(hc + 1) * TT],
                                   start=(hc == 0), stop=(hc == NHC - 1))
                return ins
            w = [(ld2, w2load_t[i]), (ld, gload_t[t])]
            if i >= 2:
                w.append((ve, epi_t[i - 2]))
            mmB_t[(t, oc)] = p.step("tensor", w, mk_mmB, inc=te)

            w = [(te, mmB_t[(t, oc)]), (ld, xload_t[i])]
            if i >= 2:
                w.append((st, stB_t[i - 2]))
            epi_t[i] = p.step(
                "vector", w, lambda e, wpar=wpar: e.scalar_tensor_tensor(
                    xnew[wpar][:], psy[wpar][:], c.scale[:, 3:4], xo[wpar][:],
                    ALU.mult, ALU.add), inc=ve)
            stB_t[i] = p.step(
                "scalar", [(ve, epi_t[i])],
                lambda e, oc=oc, t=t, wpar=wpar: e.dma_start(
                    out=x_dst[oc * 128:(oc + 1) * 128, t * TT:(t + 1) * TT],
                    in_=xnew[wpar][:]), inc=st, inc_k=16)
    p.barrier()
    esB.close()



def prep_l0ffn(inp):
    w1 = pad_to(np.asarray(inp["L0_ffn_w1"]), rows=HPAD)
    w3 = pad_to(np.asarray(inp["L0_ffn_w3"]), rows=HPAD)
    w2 = pad_to(np.asarray(inp["L0_ffn_w2"]), cols=HPAD)
    return {
        "w1t": tile_lhsT(np.ascontiguousarray(w1.T)),
        "w3t": tile_lhsT(np.ascontiguousarray(w3.T)),
        "w2t": tile_lhsT(np.ascontiguousarray(w2.T)),
    }



def build_dev_l0ffn():
    _uid[0] = 0
    nc = bass.Bass("TRN2", target_bir_lowering=False, debug=False)
    es = ExitStack()
    x_in = nc.dram_tensor("x", [D, T], F32, kind="ExternalInput").ap()
    w1t = nc.dram_tensor("w1t", [NHC, 128, 2048], F32, kind="ExternalInput").ap()
    w3t = nc.dram_tensor("w3t", [NHC, 128, 2048], F32, kind="ExternalInput").ap()
    w2t = nc.dram_tensor("w2t", [NCH, 128, HPAD], F32, kind="ExternalInput").ap()
    x_out = nc.dram_tensor("x_out", [D, T], F32, kind="ExternalOutput").ap()
    XN = nc.dram_tensor("XN", [D, T], BF16, kind="Internal").ap()
    G = nc.dram_tensor("G", [NHC, 128, T], BF16, kind="Internal").ap()
    W1Q = nc.dram_tensor("W1Q", [NHC, 128, 2048], BF16, kind="Internal").ap()
    W3Q = nc.dram_tensor("W3Q", [NHC, 128, 2048], BF16, kind="Internal").ap()
    W2Q = nc.dram_tensor("W2Q", [NCH, 128, HPAD], BF16, kind="Internal").ap()

    p = Prog(nc, es)
    c = emit_consts(p)
    emit_quant(p, c, [(w1t, W1Q, H0REAL * D, 0), (w3t, W3Q, H0REAL * D, 1),
                      (w2t, W2Q, H0REAL * D, 2)])
    emit_ln(p, c, x_in, XN)
    emit_l0ffn(p, c, x_in, x_out, XN, G, W1Q, W3Q, W2Q)
    p.step("sync", [(p.s["st"], p.s["st"].n)], lambda e: None)
    p.finalize()
    return nc, es


def emit_resident(p, src_ap, buf, rows=D):
    ld = p.s["ld"]
    tgt = 0
    n = rows // 128
    for ci in range(n):
        tgt = p.step("sync", [], lambda e, ci=ci: e.dma_start(
            out=buf[:, ci * T:(ci + 1) * T],
            in_=src_ap[ci * 128:(ci + 1) * 128, :]), inc=ld, inc_k=16)
    return tgt


def emit_vec_cols(p, c, vec_ap, dst_sb, dst_col0, G):
    es = ExitStack()
    tmp = sbuf(p, "vc_tmp", [128, 128], F32, es)
    ps = psum(p, "vc_ps", [128, 128], F32, es)
    ld, te, ve = p.s["ld"], p.s["te"], p.s["ve"]
    l0 = p.step("sync", [], lambda e: e.dma_start(
        out=tmp[0:G, :], in_=vec_ap), inc=ld, inc_k=16)
    t0 = p.step("tensor", [(ld, l0), (p.s["gp"], 1)],
                lambda e: e.transpose(ps[:, 0:G], tmp[0:G, :], c.ident_f[0:G, 0:G]),
                inc=te)
    p.step("vector", [(te, t0)],
           lambda e: e.tensor_copy(dst_sb[:, dst_col0:dst_col0 + G],
                                   ps[:, 0:G]), inc=ve)
    p.barrier()
    es.close()


def emit_projection(p, c, xnres, xn_ready, w_ap, OC, KCH, epi, name="pj"):
    es = ExitStack()
    wb = [sbuf(p, f"{name}_w{i}", [128, KCH * 128], BF16, es) for i in range(2)]
    ps = [psum(p, f"{name}_p{i}", [128, TT], F32, es) for i in range(2)]
    ld2, te = p.s["ld2"], p.s["te"]
    mmt = {}
    freed = {}
    for oc in range(OC):
        wpar = oc % 2
        w = [(te, mmt[(oc - 2, NT - 1)])] if oc >= 2 else []
        wl = p.step("scalar", w, lambda e, oc=oc, wpar=wpar: e.dma_start(
            out=wb[wpar][:], in_=w_ap[oc]), inc=ld2, inc_k=16)
        for t in range(NT):
            i = oc * NT + t
            par = i % 2

            def mk(e, oc=oc, t=t, par=par, wpar=wpar):
                ins = None
                for kc in range(KCH):
                    ins = e.matmul(ps[par][:],
                                   wb[wpar][:, kc * 128:(kc + 1) * 128],
                                   xnres[:, kc * T + t * TT:kc * T + (t + 1) * TT],
                                   start=(kc == 0), stop=(kc == KCH - 1))
                return ins
            w = [(ld2, wl), (p.s["ld"], xn_ready)]
            if i >= 2:
                w += freed[i - 2]
            mmt[(oc, t)] = p.step("tensor", w, mk, inc=te)
            freed[i] = epi(oc, t, par, mmt[(oc, t)], ps[par])
    p.barrier()
    es.close()



NHEAD = 16
HD = 128
SCL = 1.0 / np.sqrt(128.0)


def emit_causal_attn_core(p, c, q_ap, k_ap, vt_ap, out_ap, mode="softmax",
                          l1aa=False):
    es = ExitStack()
    qh = [sbuf(p, f"at_q{i}", [128, T], BF16, es) for i in range(2)]
    kh = [sbuf(p, f"at_k{i}", [128, T], BF16, es) for i in range(2)]
    vt = [sbuf(p, f"at_v{i}", [128, T], BF16, es) for i in range(2)]
    prow = [sbuf(p, f"at_p{j}", [128, T], BF16, es) for j in range(4)]
    ptr = [sbuf(p, f"at_t{i}", [128, TT], BF16, es) for i in range(2)]
    obuf = [sbuf(p, f"at_o{i}", [128, TT], BF16, es) for i in range(2)]
    cols = sbuf(p, "at_c", [128, 16], F32, es)
    tri = [sbuf(p, f"at_tri{j}", [128, TT], F32, es) for j in range(4)]
    sA = psum(p, "at_sA", [128, 3 * TT], F32, es)
    sB = psum(p, "at_sB", [128, 2 * TT], F32, es)
    pv = psum(p, "at_pv", [128, TT], F32, es)
    trp = [psum(p, f"at_tr{i}", [128, TT], BF16, es) for i in range(2)]
    ld, ve, ac, te, gp, st = (p.s[k] for k in ("ld", "ve", "ac", "te", "gp", "st"))

    def mk_tri(e):
        ins = None
        for j in range(4):
            e.memset(tri[j][:], 0.0)
            ins = e.affine_select(
                out=tri[j][:], in_=tri[j][:], compare_op=ALU.is_ge,
                fill=-1e9, base=j * 128, pattern=[[-1, TT]],
                channel_multiplier=1)
        return ins
    tri_done = p.step("gpsimd", [], mk_tri, inc=gp)

    def ldhead(h, par):
        t1 = p.step("sync", [], lambda e, h=h, par=par: e.dma_start(
            out=qh[par][:], in_=q_ap[h * 128:(h + 1) * 128, :]), inc=ld, inc_k=16)
        t2 = p.step("sync", [], lambda e, h=h, par=par: e.dma_start(
            out=kh[par][:], in_=k_ap[h * 128:(h + 1) * 128, :]), inc=ld, inc_k=16)
        t3 = p.step("sync", [], lambda e, h=h, par=par: e.dma_start(
            out=vt[par].rearrange("p (s k) -> p s k", k=128),
            in_=vt_ap[h].transpose([1, 0, 2])), inc=ld, inc_k=16)
        return t3

    hload = {0: ldhead(0, 0)}
    prev_head = {}
    for h in range(NHEAD):
        par = h % 2
        if h + 1 < NHEAD:
            hload[h + 1] = ldhead(h + 1, 1 - par)
        hl = [(ld, hload[h]), (gp, tri_done)]

        exp_done_all = []
        scale_done = {}
        for i in range(32):
            j = i % 4
            nb = i // 4 + 1
            chunks = []
            b0 = 0
            for cap, reg in ((3, sA), (2, sB), (2, sA), (1, sB)):
                if b0 >= nb:
                    break
                ln = min(cap, nb - b0)
                chunks.append((b0, ln, reg))
                b0 += ln
            off = 0
            lastexp = 0
            for cidx, (b0, ln, reg) in enumerate(chunks):
                def mkqk(e, i=i, b0=b0, ln=ln, reg=reg, par=par):
                    ins = None
                    for bi in range(ln):
                        ins = e.matmul(reg[:, bi * TT:(bi + 1) * TT],
                                       qh[par][:, i * 128:(i + 1) * 128],
                                       kh[par][:, (b0 + bi) * TT:(b0 + bi + 1) * TT],
                                       start=True, stop=True)
                    return ins
                w = list(hl)
                key = (reg is sA, "exp")
                if key in scale_done:
                    w.append((ac, scale_done[key]))
                qk = p.step("tensor", w, mkqk, inc=te)
                if b0 + ln == nb:
                    qk2 = p.step("vector", [(te, qk)],
                                 lambda e, ln=ln, reg=reg, j=j: e.tensor_add(
                                     reg[:, (ln - 1) * TT:ln * TT],
                                     reg[:, (ln - 1) * TT:ln * TT],
                                     tri[j][:]), inc=ve)
                else:
                    qk2 = None
                w = [(te, qk)] + ([(ve, qk2)] if qk2 else [])
                mc = p.step("vector", w, lambda e, ln=ln, reg=reg: e.reduce_max(
                    cols[:, 2:3], reg[:, 0:ln * TT], axis=AX.X), inc=ve)
                ms = p.step("vector", [(ve, mc)],
                            lambda e: e.tensor_scalar(cols[:, 3:4], cols[:, 2:3],
                                                      SCL, None, ALU.mult), inc=ve)
                if cidx == 0:
                    cp = p.step("vector", [(ve, ms)], lambda e: e.tensor_copy(
                        cols[:, 0:1], cols[:, 3:4]), inc=ve)
                    ng = p.step("vector", [(ve, cp)], lambda e: e.tensor_scalar(
                        cols[:, 1:2], cols[:, 0:1], -1.0, None, ALU.mult), inc=ve)
                else:
                    mn = p.step("vector", [(ve, ms)], lambda e: e.tensor_max(
                        cols[:, 3:4], cols[:, 3:4], cols[:, 0:1]), inc=ve)
                    ng = p.step("vector", [(ve, mn)], lambda e: e.tensor_scalar(
                        cols[:, 4:5], cols[:, 3:4], -1.0, None, ALU.mult), inc=ve)
                    al = p.step("scalar", [(ve, ng)], lambda e: e.activation(
                        cols[:, 5:6], cols[:, 0:1], AF.Exp,
                        bias=cols[:, 4:5]), inc=ac)
                    rs = p.step("vector", [(ac, al), (ve, lastexp)],
                                lambda e, j=j, off=off: e.tensor_scalar(
                                    prow[j][:, 0:off], prow[j][:, 0:off],
                                    cols[:, 5:6], None, ALU.mult), inc=ve)
                    rl2 = p.step("vector", [(ve, rs)],
                                 lambda e: e.tensor_mul(cols[:, 6:7],
                                                        cols[:, 6:7],
                                                        cols[:, 5:6]), inc=ve)
                    cm = p.step("vector", [(ve, rl2)], lambda e: e.tensor_copy(
                        cols[:, 0:1], cols[:, 3:4]), inc=ve)
                    ng = p.step("vector", [(ve, cm)], lambda e: e.tensor_copy(
                        cols[:, 1:2], cols[:, 4:5]), inc=ve)
                ex = p.step("scalar", [(ve, ng)],
                            lambda e, j=j, off=off, ln=ln, reg=reg: e.activation(
                                prow[j][:, off:off + ln * TT], reg[:, 0:ln * TT],
                                AF.Exp, scale=SCL, bias=cols[:, 1:2],
                                accum_out=cols[:, 7:8]), inc=ac)
                scale_done[(reg is sA, "exp")] = ex
                lastexp = 0
                if cidx == 0:
                    lc = p.step("vector", [(ac, ex)], lambda e: e.tensor_copy(
                        cols[:, 6:7], cols[:, 7:8]), inc=ve)
                else:
                    lc = p.step("vector", [(ac, ex)], lambda e: e.tensor_add(
                        cols[:, 6:7], cols[:, 6:7], cols[:, 7:8]), inc=ve)
                lastexp = lc
                off += ln * TT
            rl = p.step("vector", [(ve, lastexp)], lambda e: e.reciprocal(
                cols[:, 8:9], cols[:, 6:7]), inc=ve)
            L = nb * TT
            nz = p.step("vector", [(ve, rl)],
                        lambda e, j=j, L=L: e.tensor_scalar(
                            prow[j][:, 0:L], prow[j][:, 0:L], cols[:, 8:9],
                            None, ALU.mult), inc=ve)
            scale_done[(i, "norm")] = nz

            if j == 3:
                sup = i // 4
                nsi = 4 * sup + 4
                pvmm = 0
                evacs = {}
                for si in range(nsi):
                    spar = si % 2
                    jmin = max(0, si - 4 * sup)

                    def mktr(e, si=si, spar=spar, jmin=jmin):
                        ins = None
                        for jj in range(jmin, 4):
                            ins = e.transpose(
                                trp[spar][:, jj * 128:(jj + 1) * 128],
                                prow[jj][:, si * 128:(si + 1) * 128],
                                c.ident_bf[:])
                        return ins
                    w = [(ve, scale_done[(4 * sup + 3, "norm")])]
                    for jj in range(jmin, 4):
                        w.append((ve, scale_done[(4 * sup + jj, "norm")]))
                    if si >= 2:
                        w.append(evacs[si - 2])
                    tr = p.step("tensor", w, mktr, inc=te)
                    eng = "vector" if si % 2 == 0 else "scalar"
                    evsem = ve if eng == "vector" else ac
                    if eng == "vector":
                        evfn = (lambda e, spar=spar, jmin=jmin: e.tensor_copy(
                            ptr[spar][:, jmin * 128:TT],
                            trp[spar][:, jmin * 128:TT]))
                    else:
                        evfn = (lambda e, spar=spar, jmin=jmin: e.copy(
                            ptr[spar][:, jmin * 128:TT],
                            trp[spar][:, jmin * 128:TT]))
                    ev = p.step(eng, [(te, tr)], evfn, inc=evsem)
                    evacs[si] = (evsem, ev)

                    def mkpv(e, si=si, spar=spar, jmin=jmin, par=par, sup=sup,
                             first=(si == 0), last=(si == nsi - 1)):
                        return e.matmul(pv[:, jmin * 128:TT],
                                        vt[par][:, si * 128:(si + 1) * 128],
                                        ptr[spar][:, jmin * 128:TT],
                                        start=first, stop=last)
                    w = [evacs[si]]
                    if si == 0:
                        w.append((ac, scale_done.get((sup, "oevac"), 0)))
                    pvmm = p.step("tensor", w, mkpv, inc=te)
                opar = sup % 2
                ow = [(te, pvmm)]
                if (sup - 2, "odma") in scale_done:
                    ow.append((st, scale_done[(sup - 2, "odma")]))
                oe = p.step("scalar", ow,
                            lambda e, opar=opar: e.activation(
                                obuf[opar][:], pv[:], AF.Copy), inc=ac)
                scale_done[(sup + 1, "oevac")] = oe
                scale_done[(sup, "odma")] = p.step(
                    "sync", [(ac, oe)],
                    lambda e, h=h, sup=sup, opar=opar: e.dma_start(
                        out=out_ap[h * 128:(h + 1) * 128,
                                   sup * TT:(sup + 1) * TT],
                        in_=obuf[opar][:]), inc=st, inc_k=16)
        p.barrier()
    p.barrier()
    es.close()


def emit_l1ab(p, c, x_src, x_dst, xn_ap, q_ap, k_ap, vt_ap, ao_ap,
              cattn_ap, cproj_ap, cab_ap, cpb_ap):
    ld, ld2, ve, ac, te, st, gp = (p.s[k] for k in
                                   ("ld", "ld2", "ve", "ac", "te", "st", "gp"))
    esb = ExitStack()
    bias = sbuf(p, "ab_bias", [128, 64], F32, esb)
    emit_vec_cols(p, c, cab_ap, bias, 0, 48)
    emit_vec_cols(p, c, cpb_ap, bias, 48, 16)

    es1 = ExitStack()
    xnres = sbuf(p, "ab_xn", [128, NCH * T], BF16, es1)
    qb = [sbuf(p, f"ab_qb{i}", [128, TT], BF16, es1) for i in range(2)]
    vb = [sbuf(p, f"ab_vb{i}", [128, TT], BF16, es1) for i in range(2)]
    trv = [psum(p, f"ab_trv{i}", [128, TT], BF16, es1) for i in range(2)]
    xn_ready = emit_resident(p, xn_ap, xnres)

    qbfree = {}
    vdma = {}
    vevac = {}

    def epi(oc, t, par, mmt, ps):
        i = oc * NT + t
        w = [(te, mmt)]
        if (i - 2) in qbfree:
            w.append(qbfree[i - 2])
        a1 = p.step("scalar", w, lambda e, par=par, oc=oc: e.activation(
            qb[par][:], ps[:], AF.Identity, bias=bias[:, oc:oc + 1]), inc=ac)
        if oc < 32:
            dst = q_ap if oc < 16 else k_ap
            r0 = (oc % 16) * 128
            d1 = p.step("sync", [(ac, a1)],
                        lambda e, par=par, dst=dst, r0=r0, t=t: e.dma_start(
                            out=dst[r0:r0 + 128, t * TT:(t + 1) * TT],
                            in_=qb[par][:]), inc=st, inc_k=16)
            qbfree[i] = (st, d1)
            return [(ac, a1)]
        h = oc - 32

        def mktr(e, par=par):
            ins = None
            for j in range(4):
                ins = e.transpose(trv[par][:, j * 128:(j + 1) * 128],
                                  qb[par][:, j * 128:(j + 1) * 128],
                                  c.ident_bf[:])
            return ins
        w = [(ac, a1)]
        if (i - 2) in vevac:
            w.append((ve, vevac[i - 2]))
        tr = p.step("tensor", w, mktr, inc=te)
        qbfree[i] = (te, tr)
        w = [(te, tr)]
        if (i - 2) in vdma:
            w.append((st, vdma[i - 2]))
        ev = p.step("vector", w, lambda e, par=par: e.tensor_copy(
            vb[par][:], trv[par][:]), inc=ve)
        vevac[i] = ev
        vdma[i] = p.step("sync", [(ve, ev)],
                        lambda e, par=par, h=h, t=t: e.dma_start(
                            out=vt_ap[h, 4 * t:4 * t + 4].transpose([1, 0, 2]),
                            in_=vb[par].rearrange("p (s k) -> p s k", k=128)),
                        inc=st, inc_k=16)
        return [(ac, a1), (te, tr)]

    emit_projection(p, c, xnres, xn_ready, cattn_ap, 48, NCH, epi, name="abq")
    es1.close()

    emit_causal_attn_core(p, c, q_ap, k_ap, vt_ap, ao_ap)

    es2 = ExitStack()
    aores = sbuf(p, "ab_ao", [128, NCH * T], BF16, es2)
    xo = [sbuf(p, f"ab_xo{i}", [128, TT], F32, es2) for i in range(2)]
    xnew = [sbuf(p, f"ab_xw{i}", [128, TT], F32, es2) for i in range(2)]
    ao_ready = emit_resident(p, ao_ap, aores)
    xl = {}
    ep = {}
    sd = {}

    def epi2(oc, t, par, mmt, ps):
        i = oc * NT + t
        xl[i] = p.step("sync", [(ve, ep.get(i - 2, 0))],
                       lambda e, par=par, oc=oc, t=t: e.dma_start(
                           out=xo[par][:], in_=x_src[oc * 128:(oc + 1) * 128,
                                                     t * TT:(t + 1) * TT]),
                       inc=ld, inc_k=16)
        w = [(te, mmt), (ld, xl[i])]
        if (i - 2) in sd:
            w.append((st, sd[i - 2]))
        ep[i] = p.step("vector", w,
                       lambda e, par=par, oc=oc: e.scalar_tensor_tensor(
                           xnew[par][:], ps[:], bias[:, 48 + oc:49 + oc],
                           xo[par][:], ALU.add, ALU.add), inc=ve)
        sd[i] = p.step("scalar", [(ve, ep[i])],
                       lambda e, par=par, oc=oc, t=t: e.dma_start(
                           out=x_dst[oc * 128:(oc + 1) * 128,
                                     t * TT:(t + 1) * TT],
                           in_=xnew[par][:]), inc=st, inc_k=16)
        return [(ve, ep[i])]

    emit_projection(p, c, aores, ao_ready, cproj_ap, NCH, NCH, epi2, name="abo")
    es2.close()
    esb.close()


def prep_l1ab(inp):
    caw = np.asarray(inp["L1_attn_b_c_attn_w"])
    cpw = np.asarray(inp["L1_attn_b_c_proj_w"])
    cab = np.asarray(inp["L1_attn_b_c_attn_b"])
    cpb = np.asarray(inp["L1_attn_b_c_proj_b"])
    return {
        "cattn_t": tile_lhsT(caw),
        "cproj_t": tile_lhsT(cpw),
        "cab": np.ascontiguousarray(cab.reshape(48, 128)),
        "cpb": np.ascontiguousarray(cpb.reshape(16, 128)),
    }


def build_dev_l1ab():
    _uid[0] = 0
    nc = bass.Bass("TRN2", target_bir_lowering=False, debug=False)
    es = ExitStack()
    x_in = nc.dram_tensor("x", [D, T], F32, kind="ExternalInput").ap()
    cattn = nc.dram_tensor("cattn_t", [48, 128, 2048], F32, kind="ExternalInput").ap()
    cproj = nc.dram_tensor("cproj_t", [16, 128, 2048], F32, kind="ExternalInput").ap()
    cab = nc.dram_tensor("cab", [48, 128], F32, kind="ExternalInput").ap()
    cpb = nc.dram_tensor("cpb", [16, 128], F32, kind="ExternalInput").ap()
    x_out = nc.dram_tensor("x_out", [D, T], F32, kind="ExternalOutput").ap()
    XN = nc.dram_tensor("XN", [D, T], BF16, kind="Internal").ap()
    QF = nc.dram_tensor("QF", [D, T], BF16, kind="Internal").ap()
    KF = nc.dram_tensor("KF", [D, T], BF16, kind="Internal").ap()
    VT = nc.dram_tensor("VT", [NHEAD, 32, 128, 128], BF16, kind="Internal").ap()
    AO = nc.dram_tensor("AO", [D, T], BF16, kind="Internal").ap()
    CATB = nc.dram_tensor("CATB", [48, 128, 2048], BF16, kind="Internal").ap()
    CPRB = nc.dram_tensor("CPRB", [16, 128, 2048], BF16, kind="Internal").ap()

    p = Prog(nc, es)
    c = emit_consts(p)
    emit_convert(p, [(cattn, CATB), (cproj, CPRB)])
    emit_ln(p, c, x_in, XN)
    emit_l1ab(p, c, x_in, x_out, XN, QF, KF, VT, AO, CATB, CPRB, cab, cpb)
    p.step("sync", [(p.s["st"], p.s["st"].n)], lambda e: None)
    p.finalize()
    return nc, es


def emit_l0ab(p, c, x_src, x_dst, xn_ap, qm_ap, km_ap, vt_ap, ao_ap,
              wqkv_ap, wo_ap, bqkv_ap, bo_ap, wmq_ap, wmk_ap):
    ld, ld2, ve, ac, te, st, gp = (p.s[k] for k in
                                   ("ld", "ld2", "ve", "ac", "te", "st", "gp"))
    esb = ExitStack()
    bias = sbuf(p, "lb_bias", [128, 64], F32, esb)
    wm = sbuf(p, "lb_wm", [128, 128], BF16, esb)
    emit_vec_cols(p, c, bqkv_ap, bias, 0, 48)
    emit_vec_cols(p, c, bo_ap, bias, 48, 16)
    l1 = p.step("sync", [], lambda e: e.dma_start(out=wm[:, 0:64],
                                                  in_=wmq_ap[0]), inc=ld, inc_k=16)
    l2 = p.step("sync", [], lambda e: e.dma_start(out=wm[:, 64:128],
                                                  in_=wmk_ap[0]), inc=ld, inc_k=16)
    p.barrier()

    es1 = ExitStack()
    xnres = sbuf(p, "lb_xn", [128, NCH * T], BF16, es1)
    qb = [sbuf(p, f"lb_qb{i}", [128, TT], BF16, es1) for i in range(2)]
    el = [sbuf(p, f"lb_el{i}", [64, 3 * TT], F32, es1) for i in range(2)]
    qf = [sbuf(p, f"lb_qf{i}", [64, TT], BF16, es1) for i in range(2)]
    vb = [sbuf(p, f"lb_vb{i}", [128, TT], BF16, es1) for i in range(2)]
    trv = [psum(p, f"lb_trv{i}", [128, TT], BF16, es1) for i in range(2)]
    qmp = [psum(p, f"lb_qm{i}", [64, TT], F32, es1) for i in range(2)]
    xn_ready = emit_resident(p, xn_ap, xnres)

    qbfree, vdma, vevac, track = {}, {}, {}, {}

    def epi(oc, t, par, mmt, ps):
        i = oc * NT + t
        w = [(te, mmt)]
        if (i - 2) in qbfree:
            w.append(qbfree[i - 2])
        a1 = p.step("scalar", w, lambda e, par=par, oc=oc: e.activation(
            qb[par][:], ps[:], AF.Identity, bias=bias[:, oc:oc + 1]), inc=ac)
        if oc < 32:
            col0 = 0 if oc < 16 else 64
            w = [(ac, a1)]
            if (i - 2) in track:
                w.append(track[i - 2][0])
            m1 = p.step("tensor", w, lambda e, par=par, col0=col0: e.matmul(
                qmp[par][:], wm[:, col0:col0 + 64], qb[par][:],
                start=True, stop=True), inc=te)
            qbfree[i] = (te, m1)

            def mke1(e, par=par):
                e.tensor_scalar(el[par][:, 0:TT], qmp[par][:], 0.0, None,
                                ALU.min)
                return e.tensor_scalar(el[par][:, TT:2 * TT], qmp[par][:], 0.0,
                                       None, ALU.max)
            w = [(te, m1)]
            if (i - 2) in track:
                w.append(track[i - 2][1])
            e1 = p.step("vector", w, mke1, inc=ve)
            e2 = p.step("scalar", [(ve, e1)], lambda e, par=par: e.activation(
                el[par][:, 2 * TT:3 * TT], el[par][:, 0:TT], AF.Exp), inc=ac)
            w = [(ac, e2)]
            if (i - 2) in track:
                w.append(track[i - 2][2])
            e3 = p.step("vector", w, lambda e, par=par: e.tensor_add(
                qf[par][:], el[par][:, TT:2 * TT], el[par][:, 2 * TT:3 * TT]),
                inc=ve)
            dst = qm_ap if oc < 16 else km_ap
            r0 = (oc % 16) * 64
            d1 = p.step("sync", [(ve, e3)],
                        lambda e, par=par, dst=dst, r0=r0, t=t: e.dma_start(
                            out=dst[r0:r0 + 64, t * TT:(t + 1) * TT],
                            in_=qf[par][:]), inc=st, inc_k=16)
            track[i] = [(te, m1), (ve, e3), (st, d1)]
            return [(ac, a1)]
        h = oc - 32

        def mktr(e, par=par):
            ins = None
            for j in range(4):
                ins = e.transpose(trv[par][:, j * 128:(j + 1) * 128],
                                  qb[par][:, j * 128:(j + 1) * 128],
                                  c.ident_bf[:])
            return ins
        w = [(ac, a1)]
        if (i - 2) in vevac:
            w.append((ve, vevac[i - 2]))
        tr = p.step("tensor", w, mktr, inc=te)
        qbfree[i] = (te, tr)
        w = [(te, tr)]
        if (i - 2) in vdma:
            w.append((st, vdma[i - 2]))
        ev = p.step("vector", w, lambda e, par=par: e.tensor_copy(
            vb[par][:], trv[par][:]), inc=ve)
        vevac[i] = ev
        vdma[i] = p.step("sync", [(ve, ev)],
                        lambda e, par=par, h=h, t=t: e.dma_start(
                            out=vt_ap[h, 4 * t:4 * t + 4].transpose([1, 0, 2]),
                            in_=vb[par].rearrange("p (s k) -> p s k", k=128)),
                        inc=st, inc_k=16)
        return [(ac, a1)]

    emit_projection(p, c, xnres, xn_ready, wqkv_ap, 48, NCH, epi, name="lbq")
    es1.close()
    p.barrier()

    es2 = ExitStack()
    qm = [sbuf(p, f"lb_q{i}", [64, T], BF16, es2) for i in range(2)]
    km = [sbuf(p, f"lb_k{i}", [64, T], BF16, es2) for i in range(2)]
    vt = [sbuf(p, f"lb_v{i}", [128, T], BF16, es2) for i in range(2)]
    ktm = sbuf(p, "lb_ktm", [128, 32 * 64], BF16, es2)
    kvb = sbuf(p, "lb_kvb", [64, 128], BF16, es2)
    ksb = sbuf(p, "lb_ksb", [64, 1], BF16, es2)
    nr = sbuf(p, "lb_nr", [1, 2 * TT], F32, es2)
    of32 = [sbuf(p, f"lb_of{i}", [128, TT], F32, es2) for i in range(2)]
    ob = [sbuf(p, f"lb_ob{i}", [128, TT], BF16, es2) for i in range(2)]
    trk = [psum(p, f"lb_trk{i}", [128, 64], BF16, es2) for i in range(2)]
    kvp = psum(p, "lb_kvp", [64, 128], F32, es2)
    ksp = psum(p, "lb_ksp", [64, 1], F32, es2)
    outp = [psum(p, f"lb_out{i}", [128, TT], F32, es2) for i in range(2)]
    bcp = psum(p, "lb_bc", [128, TT], F32, es2)
    nrp = psum(p, "lb_nrp", [1, TT], F32, es2)

    def ldhead2(h, par):
        p.step("sync", [], lambda e, h=h, par=par: e.dma_start(
            out=qm[par][:], in_=qm_ap[h * 64:(h + 1) * 64, :]), inc=ld, inc_k=16)
        p.step("sync", [], lambda e, h=h, par=par: e.dma_start(
            out=km[par][:], in_=km_ap[h * 64:(h + 1) * 64, :]), inc=ld, inc_k=16)
        return p.step("sync", [], lambda e, h=h, par=par: e.dma_start(
            out=vt[par].rearrange("p (s k) -> p s k", k=128),
            in_=vt_ap[h].transpose([1, 0, 2])), inc=ld, inc_k=16)

    h2load = {0: ldhead2(0, 0)}
    for h in range(NHEAD):
        par = h % 2
        if h + 1 < NHEAD:
            h2load[h + 1] = ldhead2(h + 1, 1 - par)
        hl = [(ld, h2load[h])]
        tr_t, ev_t = {}, {}
        for si in range(32):
            spar = si % 2
            w = list(hl)
            if si >= 2:
                w.append((ve, ev_t[si - 2]))
            tr_t[si] = p.step("tensor", w, lambda e, si=si, spar=spar, par=par:
                              e.transpose(trk[spar][:],
                                          km[par][:, si * 128:(si + 1) * 128],
                                          c.ident_bf[0:64, 0:64]), inc=te)
            ev_t[si] = p.step("vector", [(te, tr_t[si])],
                              lambda e, si=si, spar=spar: e.tensor_copy(
                                  ktm[:, si * 64:(si + 1) * 64], trk[spar][:]),
                              inc=ve)

        def mkkv(e, par=par):
            ins = None
            for si in range(32):
                e.matmul(kvp[:], ktm[:, si * 64:(si + 1) * 64],
                         vt[par][:, si * 128:(si + 1) * 128],
                         start=(si == 0), stop=(si == 31))
                ins = e.matmul(ksp[:], ktm[:, si * 64:(si + 1) * 64],
                               c.ones_col_bf[:], start=(si == 0),
                               stop=(si == 31))
            return ins
        kv_mm = p.step("tensor", [(ve, ev_t[31])], mkkv, inc=te)
        kv_ev = p.step("vector", [(te, kv_mm)], lambda e: e.tensor_copy(
            kvb[:], kvp[:]), inc=ve)
        ks_ev = p.step("vector", [(ve, kv_ev)], lambda e: e.tensor_copy(
            ksb[:], ksp[:]), inc=ve)

        prev = {}
        prevn = {}
        prevb = {}
        for t in range(NT):
            tp = t % 2
            w = [(ve, ks_ev)]
            if (t - 2) in prev:
                w.append(prev[t - 2])
            o1 = p.step("tensor", w, lambda e, t=t, tp=tp, par=par: e.matmul(
                outp[tp][:], kvb[:], qm[par][:, t * TT:(t + 1) * TT],
                start=True, stop=True), inc=te)
            w = [(te, o1)]
            if (t - 1) in prevn:
                w.append(prevn[t - 1])
            n1 = p.step("tensor", w, lambda e, t=t, tp=tp, par=par: e.matmul(
                nrp[:], ksb[:], qm[par][:, t * TT:(t + 1) * TT],
                start=True, stop=True), inc=te)
            n2 = p.step("vector", [(te, n1)], lambda e, tp=tp: e.tensor_scalar(
                nr[:, tp * TT:(tp + 1) * TT], nrp[:], 1e-6, None, ALU.add),
                inc=ve)
            prevn[t] = (ve, n2)
            n3 = p.step("vector", [(ve, n2)], lambda e, tp=tp: e.reciprocal(
                nr[:, tp * TT:(tp + 1) * TT], nr[:, tp * TT:(tp + 1) * TT]),
                inc=ve)
            w = [(ve, n3)]
            if (t - 1) in prevb:
                w.append(prevb[t - 1])
            b1 = p.step("tensor", w, lambda e, tp=tp: e.matmul(
                bcp[:], c.ones_row_f[:], nr[:, tp * TT:(tp + 1) * TT],
                start=True, stop=True), inc=te)
            c1 = p.step("scalar", [(te, o1)], lambda e, tp=tp: e.copy(
                of32[tp][:], outp[tp][:]), inc=ac)
            m1 = p.step("vector", [(ac, c1), (te, b1)],
                        lambda e, tp=tp: e.tensor_mul(ob[tp][:], of32[tp][:],
                                                      bcp[:]), inc=ve)
            prevb[t] = (ve, m1)
            d1 = p.step("sync", [(ve, m1)],
                        lambda e, h=h, t=t, tp=tp: e.dma_start(
                            out=ao_ap[h * 128:(h + 1) * 128,
                                      t * TT:(t + 1) * TT],
                            in_=ob[tp][:]), inc=st, inc_k=16)
            prev[t] = (st, d1)
        p.barrier()
    es2.close()

    es3 = ExitStack()
    aores = sbuf(p, "lb_ao", [128, NCH * T], BF16, es3)
    xo = [sbuf(p, f"lb_xo{i}", [128, TT], F32, es3) for i in range(2)]
    xnew = [sbuf(p, f"lb_xw{i}", [128, TT], F32, es3) for i in range(2)]
    ao_ready = emit_resident(p, ao_ap, aores)
    xl, ep, sd = {}, {}, {}

    def epi2(oc, t, par, mmt, ps):
        i = oc * NT + t
        xl[i] = p.step("sync", [(ve, ep.get(i - 2, 0))],
                       lambda e, par=par, oc=oc, t=t: e.dma_start(
                           out=xo[par][:], in_=x_src[oc * 128:(oc + 1) * 128,
                                                     t * TT:(t + 1) * TT]),
                       inc=ld, inc_k=16)
        w = [(te, mmt), (ld, xl[i])]
        if (i - 2) in sd:
            w.append((st, sd[i - 2]))
        ep[i] = p.step("vector", w,
                       lambda e, par=par, oc=oc: e.scalar_tensor_tensor(
                           xnew[par][:], ps[:], bias[:, 48 + oc:49 + oc],
                           xo[par][:], ALU.add, ALU.add), inc=ve)
        sd[i] = p.step("scalar", [(ve, ep[i])],
                       lambda e, par=par, oc=oc, t=t: e.dma_start(
                           out=x_dst[oc * 128:(oc + 1) * 128,
                                     t * TT:(t + 1) * TT],
                           in_=xnew[par][:]), inc=st, inc_k=16)
        return [(ve, ep[i])]

    emit_projection(p, c, aores, ao_ready, wo_ap, NCH, NCH, epi2, name="lbo")
    es3.close()
    esb.close()

def prep_l0ab(inp):
    wq = np.asarray(inp["L0_attn_b_wq"])
    wk = np.asarray(inp["L0_attn_b_wk"])
    wv = np.asarray(inp["L0_attn_b_wv"])
    wo = np.asarray(inp["L0_attn_b_wo"])
    wqkv = np.concatenate([wq.T, wk.T, wv.T], axis=1)
    bqkv = np.concatenate([np.asarray(inp["L0_attn_b_bq"]),
                           np.asarray(inp["L0_attn_b_bk"]),
                           np.asarray(inp["L0_attn_b_bv"])])
    return {
        "wqkv_t": tile_lhsT(np.ascontiguousarray(wqkv)),
        "wo_t": tile_lhsT(np.ascontiguousarray(wo.T)),
        "bqkv": np.ascontiguousarray(bqkv.reshape(48, 128)),
        "bo": np.ascontiguousarray(np.asarray(inp["L0_attn_b_bo"]).reshape(16, 128)),
        "wmq_t": np.ascontiguousarray(
            np.asarray(inp["L0_attn_b_wmap_q"]).T.reshape(1, 128, 64)),
        "wmk_t": np.ascontiguousarray(
            np.asarray(inp["L0_attn_b_wmap_k"]).T.reshape(1, 128, 64)),
    }


def build_dev_l0ab():
    _uid[0] = 0
    nc = bass.Bass("TRN2", target_bir_lowering=False, debug=False)
    es = ExitStack()
    x_in = nc.dram_tensor("x", [D, T], F32, kind="ExternalInput").ap()
    wqkv = nc.dram_tensor("wqkv_t", [48, 128, 2048], F32, kind="ExternalInput").ap()
    wo = nc.dram_tensor("wo_t", [16, 128, 2048], F32, kind="ExternalInput").ap()
    bqkv = nc.dram_tensor("bqkv", [48, 128], F32, kind="ExternalInput").ap()
    bo = nc.dram_tensor("bo", [16, 128], F32, kind="ExternalInput").ap()
    wmq = nc.dram_tensor("wmq_t", [1, 128, 64], F32, kind="ExternalInput").ap()
    wmk = nc.dram_tensor("wmk_t", [1, 128, 64], F32, kind="ExternalInput").ap()
    x_out = nc.dram_tensor("x_out", [D, T], F32, kind="ExternalOutput").ap()
    XN = nc.dram_tensor("XN", [D, T], BF16, kind="Internal").ap()
    QM = nc.dram_tensor("QM", [1024, T], BF16, kind="Internal").ap()
    KM = nc.dram_tensor("KM", [1024, T], BF16, kind="Internal").ap()
    VT = nc.dram_tensor("VT", [NHEAD, 32, 128, 128], BF16, kind="Internal").ap()
    AO = nc.dram_tensor("AO", [D, T], BF16, kind="Internal").ap()
    WQKVB = nc.dram_tensor("WQKVB", [48, 128, 2048], BF16, kind="Internal").ap()
    WOB = nc.dram_tensor("WOB", [16, 128, 2048], BF16, kind="Internal").ap()
    WMQB = nc.dram_tensor("WMQB", [1, 128, 64], BF16, kind="Internal").ap()
    WMKB = nc.dram_tensor("WMKB", [1, 128, 64], BF16, kind="Internal").ap()

    p = Prog(nc, es)
    c = emit_consts(p)
    emit_convert(p, [(wqkv, WQKVB), (wo, WOB), (wmq, WMQB), (wmk, WMKB)])
    emit_ln(p, c, x_in, XN)
    emit_l0ab(p, c, x_in, x_out, XN, QM, KM, VT, AO, WQKVB, WOB, bqkv, bo,
              WMQB, WMKB)
    p.step("sync", [(p.s["st"], p.s["st"].n)], lambda e: None)
    p.finalize()
    return nc, es


LN1E4 = float(np.log(10000.0))
C1_2PI = 6.28125
C2_2PI = float(2 * np.pi - 6.28125)


def emit_rope_tables(p, c, es):
    c.cosF = sbuf(p, "rp_cos", [128, T], F32, es)
    c.sinF = sbuf(p, "rp_sin", [128, T], F32, es)
    c.perm = sbuf(p, "rp_perm", [128, 128], BF16, es)
    es_scratch = ExitStack()
    ang = sbuf(p, "rp_ang", [128, T], F32, es_scratch)
    wrk = sbuf(p, "rp_wrk", [128, T], F32, es_scratch)
    wki = sbuf(p, "rp_wki", [128, T], mybir.dt.int32, es_scratch)
    colj = sbuf(p, "rp_col", [128, 4], F32, es_scratch)
    gp, ve, ac, te = (p.s[k] for k in ("gp", "ve", "ac", "te"))

    def mkgp(e):
        e.memset(c.perm[:], 0.0)
        e.affine_select(out=c.perm[:], in_=c.perm[:], compare_op=ALU.not_equal,
                        fill=-1.0, base=-64, pattern=[[-1, 128]],
                        channel_multiplier=1)
        e.affine_select(out=c.perm[:], in_=c.perm[:], compare_op=ALU.not_equal,
                        fill=1.0, base=64, pattern=[[-1, 128]],
                        channel_multiplier=1)
        e.iota(colj[:, 0:1], pattern=[[0, 1]], base=0, channel_multiplier=1,
               allow_small_or_imprecise_dtypes=True)
        return e.iota(ang[:], pattern=[[1, T]], base=0, channel_multiplier=0,
                      allow_small_or_imprecise_dtypes=True)
    g1 = p.step("gpsimd", [], mkgp, inc=gp)

    j1 = p.step("vector", [(gp, g1)], lambda e: e.tensor_scalar(
        colj[:, 1:2], colj[:, 0:1], 64.0, None, ALU.is_ge), inc=ve)
    j2 = p.step("vector", [(ve, j1)], lambda e: e.scalar_tensor_tensor(
        colj[:, 1:2], colj[:, 1:2], -64.0, colj[:, 0:1], ALU.mult, ALU.add),
        inc=ve)
    th = p.step("scalar", [(ve, j2)], lambda e: e.activation(
        colj[:, 2:3], colj[:, 1:2], AF.Exp, scale=-LN1E4 / 64.0), inc=ac)
    a1 = p.step("vector", [(ac, th)], lambda e: e.tensor_scalar(
        ang[:], ang[:], colj[:, 2:3], None, ALU.mult), inc=ve)

    def reduce_and_sin(dst, shift, dep):
        s1 = p.step("vector", [dep], lambda e, shift=shift: e.tensor_scalar(
            wrk[:], ang[:], 1.0 / (2 * np.pi), shift, ALU.mult, ALU.add),
            inc=ve)
        s2 = p.step("vector", [(ve, s1)], lambda e: e.tensor_copy(
            wki[:], wrk[:]), inc=ve)
        s3 = p.step("vector", [(ve, s2)], lambda e: e.tensor_copy(
            wrk[:], wki[:]), inc=ve)
        sh2 = shift * 2 * np.pi
        s4 = p.step("vector", [(ve, s3)], lambda e, sh2=sh2: e.scalar_tensor_tensor(
            dst[:], wrk[:], -C1_2PI, ang[:], ALU.mult, ALU.add), inc=ve)
        s5 = p.step("vector", [(ve, s4)], lambda e, sh2=sh2: e.scalar_tensor_tensor(
            dst[:], wrk[:], -C2_2PI, dst[:], ALU.mult, ALU.add), inc=ve)
        if shift != 0.0:
            s5 = p.step("vector", [(ve, s5)], lambda e, sh2=sh2: e.tensor_scalar(
                dst[:], dst[:], np.pi / 2, None, ALU.add), inc=ve)
        return p.step("scalar", [(ve, s5)], lambda e: e.activation(
            dst[:], dst[:], AF.Sin), inc=ac)
    d1 = reduce_and_sin(c.sinF, 0.0, (ve, a1))
    d2 = reduce_and_sin(c.cosF, 0.25, (ac, d1))
    p.barrier()
    es_scratch.close()


def rope_epi_factory(p, c, q_ap, k_ap, vt_ap, bufs):
    ld, ve, ac, te, st = (p.s[k] for k in ("ld", "ve", "ac", "te", "st"))
    qb, rb, tmpa, qr, vb = bufs["qb"], bufs["rb"], bufs["tmpa"], bufs["qr"], bufs["vb"]
    trv, trr = bufs["trv"], bufs["trr"]
    qbfree, vdma, vevac, rfree, qrfree = {}, {}, {}, {}, {}

    def epi(oc, t, par, mmt, ps):
        i = oc * NT + t
        w = [(te, mmt)]
        if (i - 2) in qbfree:
            w.append(qbfree[i - 2])
        a1 = p.step("scalar", w, lambda e, par=par: e.activation(
            qb[par][:], ps[:], AF.Copy), inc=ac)
        if oc < 32:
            w = [(ac, a1)]
            if (i - 2) in rfree:
                w.append(rfree[i - 2])
            r1 = p.step("tensor", w, lambda e, par=par: e.matmul(
                trr[par][:], c.perm[:], qb[par][:], start=True, stop=True),
                inc=te)
            r2 = p.step("vector", [(te, r1)], lambda e, par=par: e.tensor_copy(
                rb[par][:], trr[par][:]), inc=ve)
            rfree[i] = (ve, r2)

            def mkrope(e, par=par, t=t):
                sl = slice(t * TT, (t + 1) * TT)
                e.tensor_mul(tmpa[par][:], qb[par][:], c.cosF[:, sl])
                e.tensor_mul(rb[par][:], rb[par][:], c.sinF[:, sl])
                return e.tensor_add(qr[par][:], tmpa[par][:], rb[par][:])
            w = [(ve, r2)]
            if (i - 2) in qrfree:
                w.append(qrfree[i - 2])
            r3 = p.step("vector", w, mkrope, inc=ve)
            qbfree[i] = (ve, r3)
            dst = q_ap if oc < 16 else k_ap
            r0 = (oc % 16) * 128
            d1 = p.step("sync", [(ve, r3)],
                        lambda e, par=par, dst=dst, r0=r0, t=t: e.dma_start(
                            out=dst[r0:r0 + 128, t * TT:(t + 1) * TT],
                            in_=qr[par][:]), inc=st, inc_k=16)
            qrfree[i] = (st, d1)
            return [(ac, a1)]
        h = oc - 32

        def mktr(e, par=par):
            ins = None
            for j in range(4):
                ins = e.transpose(trv[par][:, j * 128:(j + 1) * 128],
                                  qb[par][:, j * 128:(j + 1) * 128],
                                  c.ident_bf[:])
            return ins
        w = [(ac, a1)]
        if (i - 2) in vevac:
            w.append((ve, vevac[i - 2]))
        tr = p.step("tensor", w, mktr, inc=te)
        qbfree[i] = (te, tr)
        w = [(te, tr)]
        if (i - 2) in vdma:
            w.append((st, vdma[i - 2]))
        ev = p.step("vector", w, lambda e, par=par: e.tensor_copy(
            vb[par][:], trv[par][:]), inc=ve)
        vevac[i] = ev
        vdma[i] = p.step("sync", [(ve, ev)],
                        lambda e, par=par, h=h, t=t: e.dma_start(
                            out=vt_ap[h, 4 * t:4 * t + 4].transpose([1, 0, 2]),
                            in_=vb[par].rearrange("p (s k) -> p s k", k=128)),
                        inc=st, inc_k=16)
        return [(ac, a1)]
    return epi


def emit_poly_attn_core(p, c, q_ap, k_ap, vt_ap, out_ap):
    es = ExitStack()
    qh = [sbuf(p, f"pa_q{i}", [128, T], BF16, es) for i in range(2)]
    kh = [sbuf(p, f"pa_k{i}", [128, T], BF16, es) for i in range(2)]
    vt = [sbuf(p, f"pa_v{i}", [128, T], BF16, es) for i in range(2)]
    prow = [sbuf(p, f"pa_p{j}", [128, T], BF16, es) for j in range(4)]
    s2 = [sbuf(p, f"pa_s2{i}", [128, TT], F32, es) for i in range(2)]
    ptr = [sbuf(p, f"pa_t{i}", [128, TT], BF16, es) for i in range(2)]
    obuf = [sbuf(p, f"pa_o{i}", [128, TT], BF16, es) for i in range(2)]
    cols = sbuf(p, "pa_c", [128, 16], F32, es)
    tri01 = [sbuf(p, f"pa_tri{j}", [128, TT], F32, es) for j in range(4)]
    sp = [psum(p, f"pa_s{i}", [128, TT], F32, es) for i in range(2)]
    pv = psum(p, "pa_pv", [128, TT], F32, es)
    trp = [psum(p, f"pa_tr{i}", [128, TT], BF16, es) for i in range(2)]
    ld, ve, ac, te, gp, st = (p.s[k] for k in ("ld", "ve", "ac", "te", "gp", "st"))

    def mk_tri(e):
        ins = None
        for j in range(4):
            e.memset(tri01[j][:], 1.0)
            ins = e.affine_select(
                out=tri01[j][:], in_=tri01[j][:], compare_op=ALU.is_ge,
                fill=0.0, base=j * 128, pattern=[[-1, TT]],
                channel_multiplier=1)
        return ins
    tri_done = p.step("gpsimd", [], mk_tri, inc=gp)

    def ldhead(h, par):
        p.step("sync", [], lambda e, h=h, par=par: e.dma_start(
            out=qh[par][:], in_=q_ap[h * 128:(h + 1) * 128, :]), inc=ld, inc_k=16)
        p.step("sync", [], lambda e, h=h, par=par: e.dma_start(
            out=kh[par][:], in_=k_ap[h * 128:(h + 1) * 128, :]), inc=ld, inc_k=16)
        return p.step("sync", [], lambda e, h=h, par=par: e.dma_start(
            out=vt[par].rearrange("p (s k) -> p s k", k=128),
            in_=vt_ap[h].transpose([1, 0, 2])), inc=ld, inc_k=16)

    hload = {0: ldhead(0, 0)}
    for h in range(NHEAD):
        par = h % 2
        if h + 1 < NHEAD:
            hload[h + 1] = ldhead(h + 1, 1 - par)
        hl = [(ld, hload[h]), (gp, tri_done)]
        sd = {}
        sq1_t, sq2_t = {}, {}
        sp_last = {0: 0, 1: 0}
        dn_prev = 0
        for i in range(32):
            j = i % 4
            nb = i // 4 + 1
            for b in range(nb):
                bpar = b % 2
                w = list(hl)
                if sp_last[bpar]:
                    w.append((ac, sp_last[bpar]))
                qk = p.step("tensor", w, lambda e, i=i, b=b, bpar=bpar, par=par:
                            e.matmul(sp[bpar][:],
                                     qh[par][:, i * 128:(i + 1) * 128],
                                     kh[par][:, b * TT:(b + 1) * TT],
                                     start=True, stop=True), inc=te)
                w = [(te, qk)]
                sq1 = p.step("scalar", w, lambda e, bpar=bpar: e.activation(
                    s2[bpar][:], sp[bpar][:], AF.Square), inc=ac)
                sq1_t[(i, b)] = sq1
                sp_last[bpar] = sq1
                dep = (ac, sq1)
                if b == nb - 1:
                    dm = p.step("vector", [dep], lambda e, bpar=bpar, j=j:
                                e.tensor_mul(s2[bpar][:], s2[bpar][:],
                                             tri01[j][:]), inc=ve)
                    dep = (ve, dm)
                w = [dep, (ve, dn_prev)]
                sq2_t[(i, b)] = p.step(
                    "scalar", w, lambda e, i=i, j=j, b=b, bpar=bpar:
                    e.activation(prow[j][:, b * TT:(b + 1) * TT], s2[bpar][:],
                                 AF.Square, accum_out=cols[:, b:b + 1]),
                    inc=ac)
            d1 = p.step("vector", [(ac, sq2_t[(i, nb - 1)])],
                        lambda e, nb=nb: e.tensor_reduce(
                            cols[:, 8:9], cols[:, 0:nb], AX.X, ALU.add), inc=ve)
            d2 = p.step("vector", [(ve, d1)], lambda e: e.tensor_scalar(
                cols[:, 8:9], cols[:, 8:9], 1e-6, None, ALU.max), inc=ve)
            d3 = p.step("vector", [(ve, d2)], lambda e: e.reciprocal(
                cols[:, 9:10], cols[:, 8:9]), inc=ve)
            L = nb * TT
            nz = p.step("vector", [(ve, d3)], lambda e, j=j, L=L: e.tensor_scalar(
                prow[j][:, 0:L], prow[j][:, 0:L], cols[:, 9:10], None,
                ALU.mult), inc=ve)
            dn_prev = nz
            sd[(i, "norm")] = nz

            if j == 3:
                sup = i // 4
                nsi = 4 * sup + 4
                pvmm = 0
                evacs = {}
                for si in range(nsi):
                    spar = si % 2
                    jmin = max(0, si - 4 * sup)

                    def mktr(e, si=si, spar=spar, jmin=jmin):
                        ins = None
                        for jj in range(jmin, 4):
                            ins = e.transpose(
                                trp[spar][:, jj * 128:(jj + 1) * 128],
                                prow[jj][:, si * 128:(si + 1) * 128],
                                c.ident_bf[:])
                        return ins
                    w = [(ve, sd[(4 * sup + 3, "norm")])]
                    if si >= 2:
                        w.append(evacs[si - 2])
                    tr = p.step("tensor", w, mktr, inc=te)
                    eng = "vector" if si % 2 == 0 else "scalar"
                    evsem = ve if eng == "vector" else ac
                    if eng == "vector":
                        evfn = (lambda e, spar=spar, jmin=jmin: e.tensor_copy(
                            ptr[spar][:, jmin * 128:TT],
                            trp[spar][:, jmin * 128:TT]))
                    else:
                        evfn = (lambda e, spar=spar, jmin=jmin: e.copy(
                            ptr[spar][:, jmin * 128:TT],
                            trp[spar][:, jmin * 128:TT]))
                    ev = p.step(eng, [(te, tr)], evfn, inc=evsem)
                    evacs[si] = (evsem, ev)

                    def mkpv(e, si=si, spar=spar, jmin=jmin, par=par,
                             first=(si == 0), last=(si == nsi - 1)):
                        return e.matmul(pv[:, jmin * 128:TT],
                                        vt[par][:, si * 128:(si + 1) * 128],
                                        ptr[spar][:, jmin * 128:TT],
                                        start=first, stop=last)
                    w = [evacs[si]]
                    if si == 0:
                        w.append((ac, sd.get((sup, "oevac"), 0)))
                    pvmm = p.step("tensor", w, mkpv, inc=te)
                opar = sup % 2
                ow = [(te, pvmm)]
                if (sup - 2, "odma") in sd:
                    ow.append((st, sd[(sup - 2, "odma")]))
                oe = p.step("scalar", ow, lambda e, opar=opar: e.activation(
                    obuf[opar][:], pv[:], AF.Copy), inc=ac)
                sd[(sup + 1, "oevac")] = oe
                sd[(sup, "odma")] = p.step(
                    "sync", [(ac, oe)],
                    lambda e, h=h, sup=sup, opar=opar: e.dma_start(
                        out=out_ap[h * 128:(h + 1) * 128,
                                   sup * TT:(sup + 1) * TT],
                        in_=obuf[opar][:]), inc=st, inc_k=16)
        p.barrier()
    p.barrier()
    es.close()


def i0nb(i):
    return (i - 1) // 4 + 1


def emit_l0aa(p, c, x_src, x_dst, xn_ap, q_ap, k_ap, vt_ap, ao_ap,
              wqkv_ap, wo_ap):
    ld, ld2, ve, ac, te, st = (p.s[k] for k in ("ld", "ld2", "ve", "ac", "te", "st"))
    es1 = ExitStack()
    emit_rope_tables(p, c, es1)
    xnres = sbuf(p, "aa_xn", [128, NCH * T], BF16, es1)
    bufs = {
        "qb": [sbuf(p, f"aa_qb{i}", [128, TT], BF16, es1) for i in range(2)],
        "rb": [sbuf(p, f"aa_rb{i}", [128, TT], F32, es1) for i in range(2)],
        "tmpa": [sbuf(p, f"aa_ta{i}", [128, TT], F32, es1) for i in range(2)],
        "qr": [sbuf(p, f"aa_qr{i}", [128, TT], BF16, es1) for i in range(2)],
        "vb": [sbuf(p, f"aa_vb{i}", [128, TT], BF16, es1) for i in range(2)],
        "trv": [psum(p, f"aa_trv{i}", [128, TT], BF16, es1) for i in range(2)],
        "trr": [psum(p, f"aa_trr{i}", [128, TT], F32, es1) for i in range(2)],
    }
    xn_ready = emit_resident(p, xn_ap, xnres)
    epi = rope_epi_factory(p, c, q_ap, k_ap, vt_ap, bufs)
    emit_projection(p, c, xnres, xn_ready, wqkv_ap, 48, NCH, epi, name="aaq")
    es1.close()
    p.barrier()

    emit_poly_attn_core(p, c, q_ap, k_ap, vt_ap, ao_ap)

    es3 = ExitStack()
    aores = sbuf(p, "aa_ao", [128, NCH * T], BF16, es3)
    xo = [sbuf(p, f"aa_xo{i}", [128, TT], F32, es3) for i in range(2)]
    xnew = [sbuf(p, f"aa_xw{i}", [128, TT], F32, es3) for i in range(2)]
    ao_ready = emit_resident(p, ao_ap, aores)
    xl, ep, sd2 = {}, {}, {}

    def epi2(oc, t, par, mmt, ps):
        i = oc * NT + t
        xl[i] = p.step("sync", [(ve, ep.get(i - 2, 0))],
                       lambda e, par=par, oc=oc, t=t: e.dma_start(
                           out=xo[par][:], in_=x_src[oc * 128:(oc + 1) * 128,
                                                     t * TT:(t + 1) * TT]),
                       inc=ld, inc_k=16)
        w = [(te, mmt), (ld, xl[i])]
        if (i - 2) in sd2:
            w.append((st, sd2[i - 2]))
        ep[i] = p.step("vector", w, lambda e, par=par: e.tensor_add(
            xnew[par][:], xo[par][:], ps[:]), inc=ve)
        sd2[i] = p.step("scalar", [(ve, ep[i])],
                        lambda e, par=par, oc=oc, t=t: e.dma_start(
                            out=x_dst[oc * 128:(oc + 1) * 128,
                                      t * TT:(t + 1) * TT],
                            in_=xnew[par][:]), inc=st, inc_k=16)
        return [(ve, ep[i])]

    emit_projection(p, c, aores, ao_ready, wo_ap, NCH, NCH, epi2, name="aao")
    es3.close()


def prep_l0aa(inp):
    wq = np.asarray(inp["L0_attn_a_Wq"])
    wk = np.asarray(inp["L0_attn_a_Wk"])
    wv = np.asarray(inp["L0_attn_a_Wv"])
    wo = np.asarray(inp["L0_attn_a_Wo"])
    wqkv = np.concatenate([wq.T, wk.T, wv.T], axis=1)
    return {
        "awqkv_t": tile_lhsT(np.ascontiguousarray(wqkv)),
        "awo_t": tile_lhsT(np.ascontiguousarray(wo.T)),
    }


def build_dev_l0aa(dbg=False):
    _uid[0] = 0
    nc = bass.Bass("TRN2", target_bir_lowering=False, debug=False)
    es = ExitStack()
    kind = "ExternalOutput" if dbg else "Internal"
    x_in = nc.dram_tensor("x", [D, T], F32, kind="ExternalInput").ap()
    wqkv = nc.dram_tensor("awqkv_t", [48, 128, 2048], F32, kind="ExternalInput").ap()
    wo = nc.dram_tensor("awo_t", [16, 128, 2048], F32, kind="ExternalInput").ap()
    x_out = nc.dram_tensor("x_out", [D, T], F32, kind="ExternalOutput").ap()
    XN = nc.dram_tensor("XN", [D, T], BF16, kind=kind).ap()
    QF = nc.dram_tensor("QF", [D, T], BF16, kind=kind).ap()
    KF = nc.dram_tensor("KF", [D, T], BF16, kind=kind).ap()
    VT = nc.dram_tensor("VT", [NHEAD, 32, 128, 128], BF16, kind=kind).ap()
    AO = nc.dram_tensor("AO", [D, T], BF16, kind=kind).ap()
    WQKVB = nc.dram_tensor("WQKVB", [48, 128, 2048], BF16, kind="Internal").ap()
    WOB = nc.dram_tensor("WOB", [16, 128, 2048], BF16, kind="Internal").ap()

    p = Prog(nc, es)
    c = emit_consts(p)
    emit_convert(p, [(wqkv, WQKVB), (wo, WOB)])
    emit_ln(p, c, x_in, XN)
    emit_l0aa(p, c, x_in, x_out, XN, QF, KF, VT, AO, WQKVB, WOB)
    p.step("sync", [(p.s["st"], p.s["st"].n)], lambda e: None)
    p.finalize()
    return nc, es


def emit_sparse_attn_core(p, c, q_ap, k_ap, vt_ap, out_ap):
    es = ExitStack()
    qh = [sbuf(p, f"sa_q{i}", [128, T], BF16, es) for i in range(2)]
    kh = [sbuf(p, f"sa_k{i}", [128, T], BF16, es) for i in range(2)]
    vt = [sbuf(p, f"sa_v{i}", [128, T], BF16, es) for i in range(2)]
    prow = [sbuf(p, f"sa_p{j}", [128, T], BF16, es) for j in range(4)]
    ptr = [sbuf(p, f"sa_t{i}", [128, TT], BF16, es) for i in range(2)]
    obuf = [sbuf(p, f"sa_o{i}", [128, TT], BF16, es) for i in range(2)]
    cols = sbuf(p, "sa_c", [128, 16], F32, es)
    sA = psum(p, "sa_sA", [128, 3 * TT], F32, es)
    sB = psum(p, "sa_sB", [128, 2 * TT], F32, es)
    pv = psum(p, "sa_pv", [128, TT], F32, es)
    trp = [psum(p, f"sa_tr{i}", [128, TT], BF16, es) for i in range(2)]
    ld, ve, ac, te, gp, st = (p.s[k] for k in ("ld", "ve", "ac", "te", "gp", "st"))

    def ldhead(h, par):
        p.step("sync", [], lambda e, h=h, par=par: e.dma_start(
            out=qh[par][:], in_=q_ap[h * 128:(h + 1) * 128, :]), inc=ld, inc_k=16)
        p.step("sync", [], lambda e, h=h, par=par: e.dma_start(
            out=kh[par][:], in_=k_ap[h * 128:(h + 1) * 128, :]), inc=ld, inc_k=16)
        return p.step("sync", [], lambda e, h=h, par=par: e.dma_start(
            out=vt[par].rearrange("p (s k) -> p s k", k=128),
            in_=vt_ap[h].transpose([1, 0, 2])), inc=ld, inc_k=16)

    def emit_mask(i, j, nb, dep, dep2):
        tq0 = i * 128
        nb64 = nb * 8
        row = prow[j]
        tgt = None

        def mk1(e, nb64=nb64, tq0=tq0, row=row):
            v = row[:, 0:nb64 * 64].rearrange("p (b k) -> p b k", k=64)
            return e.affine_select(out=v, in_=v, compare_op=ALU.is_ge,
                                   fill=0.0, base=tq0,
                                   pattern=[[-64, nb64], [0, 64]],
                                   channel_multiplier=1)
        tgt = p.step("gpsimd", [dep, dep2], mk1, inc=gp)

        Bend = min(max(2, 2 * i - 16), nb64)
        sing = []
        if Bend > 2:
            for kb in (2, 3):
                if kb < Bend:
                    sing.append(kb)
            a0 = 4
            aend = (Bend // 4) * 4
            if aend > a0:
                def mk2(e, a0=a0, aend=aend, row=row):
                    na = (aend - a0) // 4
                    v = row[:, a0 * 64:aend * 64].rearrange(
                        "p (a b k) -> p a b k", b=4, k=64)
                    return e.affine_select(out=v, in_=v, compare_op=ALU.is_ge,
                                           fill=0.0, base=0,
                                           pattern=[[0, na], [-1, 4], [0, 64]],
                                           channel_multiplier=0)
                tgt = p.step("gpsimd", [(gp, tgt)], mk2, inc=gp)
            for kb in range(aend, Bend):
                if kb % 4 != 0:
                    sing.append(kb)
        bkb = 2 * i - 16
        bsel = (bkb >= 2 and bkb % 4 != 0 and bkb < nb64)
        if sing or bsel:
            def mk3(e, sing=tuple(sing), bsel=bsel, bkb=bkb, row=row):
                ins = None
                for kb in sing:
                    ins = e.memset(row[:, kb * 64:(kb + 1) * 64], 0.0)
                if bsel:
                    v = row[:, bkb * 64:(bkb + 1) * 64]
                    ins = e.affine_select(out=v, in_=v, compare_op=ALU.is_ge,
                                          fill=0.0, base=63,
                                          pattern=[[0, 64]],
                                          channel_multiplier=-1)
                return ins
            tgt = p.step("gpsimd", [(gp, tgt)], mk3, inc=gp)
        return tgt

    hload = {0: ldhead(0, 0)}
    for h in range(NHEAD):
        par = h % 2
        if h + 1 < NHEAD:
            hload[h + 1] = ldhead(h + 1, 1 - par)
        hl = [(ld, hload[h])]

        scale_done = {}
        for i in range(32):
            j = i % 4
            nb = i // 4 + 1
            chunks = []
            b0 = 0
            for cap, reg in ((3, sA), (2, sB), (2, sA), (1, sB)):
                if b0 >= nb:
                    break
                ln = min(cap, nb - b0)
                chunks.append((b0, ln, reg))
                b0 += ln
            off = 0
            lastexp = None
            for cidx, (b0, ln, reg) in enumerate(chunks):
                def mkqk(e, i=i, b0=b0, ln=ln, reg=reg, par=par):
                    ins = None
                    for bi in range(ln):
                        ins = e.matmul(reg[:, bi * TT:(bi + 1) * TT],
                                       qh[par][:, i * 128:(i + 1) * 128],
                                       kh[par][:, (b0 + bi) * TT:(b0 + bi + 1) * TT],
                                       start=True, stop=True)
                    return ins
                w = list(hl)
                key = (reg is sA, "exp")
                if key in scale_done:
                    w.append((ac, scale_done[key]))
                qk = p.step("tensor", w, mkqk, inc=te)
                mc = p.step("vector", [(te, qk)], lambda e, ln=ln, reg=reg:
                            e.reduce_max(cols[:, 2:3], reg[:, 0:ln * TT],
                                         axis=AX.X), inc=ve)
                ms = p.step("vector", [(ve, mc)],
                            lambda e: e.tensor_scalar(cols[:, 3:4], cols[:, 2:3],
                                                      SCL, None, ALU.mult), inc=ve)
                if cidx == 0:
                    cp = p.step("vector", [(ve, ms)], lambda e: e.tensor_copy(
                        cols[:, 0:1], cols[:, 3:4]), inc=ve)
                    ng = p.step("vector", [(ve, cp)], lambda e: e.tensor_scalar(
                        cols[:, 1:2], cols[:, 0:1], -1.0, None, ALU.mult), inc=ve)
                else:
                    mn = p.step("vector", [(ve, ms)], lambda e: e.tensor_max(
                        cols[:, 3:4], cols[:, 3:4], cols[:, 0:1]), inc=ve)
                    ng = p.step("vector", [(ve, mn)], lambda e: e.tensor_scalar(
                        cols[:, 4:5], cols[:, 3:4], -1.0, None, ALU.mult), inc=ve)
                    al = p.step("scalar", [(ve, ng)], lambda e: e.activation(
                        cols[:, 5:6], cols[:, 0:1], AF.Exp,
                        bias=cols[:, 4:5]), inc=ac)
                    rsw = [(ac, al)]
                    if lastexp:
                        rsw.append(lastexp)
                    rs = p.step("vector", rsw,
                                lambda e, j=j, off=off: e.tensor_scalar(
                                    prow[j][:, 0:off], prow[j][:, 0:off],
                                    cols[:, 5:6], None, ALU.mult), inc=ve)
                    cm = p.step("vector", [(ve, rs)], lambda e: e.tensor_copy(
                        cols[:, 0:1], cols[:, 3:4]), inc=ve)
                    ng = p.step("vector", [(ve, cm)], lambda e: e.tensor_copy(
                        cols[:, 1:2], cols[:, 4:5]), inc=ve)
                ex = p.step("scalar", [(ve, ng)],
                            lambda e, j=j, off=off, ln=ln, reg=reg: e.activation(
                                prow[j][:, off:off + ln * TT], reg[:, 0:ln * TT],
                                AF.Exp, scale=SCL, bias=cols[:, 1:2]), inc=ac)
                scale_done[(reg is sA, "exp")] = ex
                lastexp = (ac, ex)
                off += ln * TT
            mk = emit_mask(i, j, nb, lastexp,
                           (ve, scale_done.get((i - 1, "norm"), 0)))
            L = nb * TT
            sm = p.step("vector", [(gp, mk)], lambda e, j=j, L=L:
                        e.tensor_reduce(cols[:, 6:7], prow[j][:, 0:L], AX.X,
                                        ALU.add), inc=ve)
            rl = p.step("vector", [(ve, sm)], lambda e: e.reciprocal(
                cols[:, 8:9], cols[:, 6:7]), inc=ve)
            nz = p.step("vector", [(ve, rl)],
                        lambda e, j=j, L=L: e.tensor_scalar(
                            prow[j][:, 0:L], prow[j][:, 0:L], cols[:, 8:9],
                            None, ALU.mult), inc=ve)
            scale_done[(i, "norm")] = nz

            if j == 3:
                sup = i // 4
                nsi = 4 * sup + 4
                pvmm = 0
                evacs = {}
                for si in range(nsi):
                    spar = si % 2
                    jmin = max(0, si - 4 * sup)

                    def mktr(e, si=si, spar=spar, jmin=jmin):
                        ins = None
                        for jj in range(jmin, 4):
                            ins = e.transpose(
                                trp[spar][:, jj * 128:(jj + 1) * 128],
                                prow[jj][:, si * 128:(si + 1) * 128],
                                c.ident_bf[:])
                        return ins
                    w = [(ve, scale_done[(4 * sup + 3, "norm")])]
                    for jj in range(jmin, 4):
                        w.append((ve, scale_done[(4 * sup + jj, "norm")]))
                    if si >= 2:
                        w.append(evacs[si - 2])
                    tr = p.step("tensor", w, mktr, inc=te)
                    eng = "vector" if si % 2 == 0 else "scalar"
                    evsem = ve if eng == "vector" else ac
                    if eng == "vector":
                        evfn = (lambda e, spar=spar, jmin=jmin: e.tensor_copy(
                            ptr[spar][:, jmin * 128:TT],
                            trp[spar][:, jmin * 128:TT]))
                    else:
                        evfn = (lambda e, spar=spar, jmin=jmin: e.copy(
                            ptr[spar][:, jmin * 128:TT],
                            trp[spar][:, jmin * 128:TT]))
                    ev = p.step(eng, [(te, tr)], evfn, inc=evsem)
                    evacs[si] = (evsem, ev)

                    def mkpv(e, si=si, spar=spar, jmin=jmin, par=par,
                             first=(si == 0), last=(si == nsi - 1)):
                        return e.matmul(pv[:, jmin * 128:TT],
                                        vt[par][:, si * 128:(si + 1) * 128],
                                        ptr[spar][:, jmin * 128:TT],
                                        start=first, stop=last)
                    w = [evacs[si]]
                    if si == 0:
                        w.append((ac, scale_done.get((sup, "oevac"), 0)))
                    pvmm = p.step("tensor", w, mkpv, inc=te)
                opar = sup % 2
                ow = [(te, pvmm)]
                if (sup - 2, "odma") in scale_done:
                    ow.append((st, scale_done[(sup - 2, "odma")]))
                oe = p.step("scalar", ow, lambda e, opar=opar: e.activation(
                    obuf[opar][:], pv[:], AF.Copy), inc=ac)
                scale_done[(sup + 1, "oevac")] = oe
                scale_done[(sup, "odma")] = p.step(
                    "sync", [(ac, oe)],
                    lambda e, h=h, sup=sup, opar=opar: e.dma_start(
                        out=out_ap[h * 128:(h + 1) * 128,
                                   sup * TT:(sup + 1) * TT],
                        in_=obuf[opar][:]), inc=st, inc_k=16)
        p.barrier()
    p.barrier()
    es.close()


def emit_l1aa(p, c, x_src, x_dst, xn_ap, q_ap, k_ap, vt_ap, ao_ap,
              wqkv_ap, wo_ap):
    ld, ld2, ve, ac, te, st = (p.s[k] for k in ("ld", "ld2", "ve", "ac", "te", "st"))
    es1 = ExitStack()
    emit_rope_tables(p, c, es1)
    xnres = sbuf(p, "sb_xn", [128, NCH * T], BF16, es1)
    bufs = {
        "qb": [sbuf(p, f"sb_qb{i}", [128, TT], BF16, es1) for i in range(2)],
        "rb": [sbuf(p, f"sb_rb{i}", [128, TT], F32, es1) for i in range(2)],
        "tmpa": [sbuf(p, f"sb_ta{i}", [128, TT], F32, es1) for i in range(2)],
        "qr": [sbuf(p, f"sb_qr{i}", [128, TT], BF16, es1) for i in range(2)],
        "vb": [sbuf(p, f"sb_vb{i}", [128, TT], BF16, es1) for i in range(2)],
        "trv": [psum(p, f"sb_trv{i}", [128, TT], BF16, es1) for i in range(2)],
        "trr": [psum(p, f"sb_trr{i}", [128, TT], F32, es1) for i in range(2)],
    }
    xn_ready = emit_resident(p, xn_ap, xnres)
    epi = rope_epi_factory(p, c, q_ap, k_ap, vt_ap, bufs)
    emit_projection(p, c, xnres, xn_ready, wqkv_ap, 48, NCH, epi, name="sbq")
    es1.close()
    p.barrier()

    emit_sparse_attn_core(p, c, q_ap, k_ap, vt_ap, ao_ap)

    es3 = ExitStack()
    aores = sbuf(p, "sb_ao", [128, NCH * T], BF16, es3)
    xo = [sbuf(p, f"sb_xo{i}", [128, TT], F32, es3) for i in range(2)]
    xnew = [sbuf(p, f"sb_xw{i}", [128, TT], F32, es3) for i in range(2)]
    ao_ready = emit_resident(p, ao_ap, aores)
    xl, ep, sd2 = {}, {}, {}

    def epi2(oc, t, par, mmt, ps):
        i = oc * NT + t
        xl[i] = p.step("sync", [(ve, ep.get(i - 2, 0))],
                       lambda e, par=par, oc=oc, t=t: e.dma_start(
                           out=xo[par][:], in_=x_src[oc * 128:(oc + 1) * 128,
                                                     t * TT:(t + 1) * TT]),
                       inc=ld, inc_k=16)
        w = [(te, mmt), (ld, xl[i])]
        if (i - 2) in sd2:
            w.append((st, sd2[i - 2]))
        ep[i] = p.step("vector", w, lambda e, par=par: e.tensor_add(
            xnew[par][:], xo[par][:], ps[:]), inc=ve)
        sd2[i] = p.step("scalar", [(ve, ep[i])],
                        lambda e, par=par, oc=oc, t=t: e.dma_start(
                            out=x_dst[oc * 128:(oc + 1) * 128,
                                      t * TT:(t + 1) * TT],
                            in_=xnew[par][:]), inc=st, inc_k=16)
        return [(ve, ep[i])]

    emit_projection(p, c, aores, ao_ready, wo_ap, NCH, NCH, epi2, name="sbo")
    es3.close()


def prep_l1aa(inp):
    wq = np.asarray(inp["L1_attn_a_w_q"])
    wk = np.asarray(inp["L1_attn_a_w_k"])
    wv = np.asarray(inp["L1_attn_a_w_v"])
    wo = np.asarray(inp["L1_attn_a_w_o"])
    wqkv = np.concatenate([wq.T, wk.T, wv.T], axis=1)
    return {
        "swqkv_t": tile_lhsT(np.ascontiguousarray(wqkv)),
        "swo_t": tile_lhsT(np.ascontiguousarray(wo.T)),
    }


def build_dev_l1aa(dbg=False):
    _uid[0] = 0
    nc = bass.Bass("TRN2", target_bir_lowering=False, debug=False)
    es = ExitStack()
    kind = "ExternalOutput" if dbg else "Internal"
    x_in = nc.dram_tensor("x", [D, T], F32, kind="ExternalInput").ap()
    wqkv = nc.dram_tensor("swqkv_t", [48, 128, 2048], F32, kind="ExternalInput").ap()
    wo = nc.dram_tensor("swo_t", [16, 128, 2048], F32, kind="ExternalInput").ap()
    x_out = nc.dram_tensor("x_out", [D, T], F32, kind="ExternalOutput").ap()
    XN = nc.dram_tensor("XN", [D, T], BF16, kind="Internal").ap()
    QF = nc.dram_tensor("QF", [D, T], BF16, kind=kind).ap()
    KF = nc.dram_tensor("KF", [D, T], BF16, kind="Internal").ap()
    VT = nc.dram_tensor("VT", [NHEAD, 32, 128, 128], BF16, kind="Internal").ap()
    AO = nc.dram_tensor("AO", [D, T], BF16, kind=kind).ap()
    WQKVB = nc.dram_tensor("WQKVB", [48, 128, 2048], BF16, kind="Internal").ap()
    WOB = nc.dram_tensor("WOB", [16, 128, 2048], BF16, kind="Internal").ap()

    p = Prog(nc, es)
    c = emit_consts(p)
    emit_convert(p, [(wqkv, WQKVB), (wo, WOB)])
    emit_ln(p, c, x_in, XN)
    emit_l1aa(p, c, x_in, x_out, XN, QF, KF, VT, AO, WQKVB, WOB)
    p.step("sync", [(p.s["st"], p.s["st"].n)], lambda e: None)
    p.finalize()
    return nc, es


def emit_l1ffn(p, c, x_src, x_dst, xn_ap, xt32_ap, xt1_ap,
               wg_ap, bg_ap, wup_ap, wsw_ap, wdn_ap, bdn_ap,
               wsu_ap, wsa_ap, wsd_ap, bias_aps):
    marks = {}
    ld, ld2, ve, ac, te, st, gp = (p.s[k] for k in
                                   ("ld", "ld2", "ve", "ac", "te", "st", "gp"))
    es = ExitStack()
    bias = sbuf(p, "mf_bias", [128, 144], F32, es)
    emit_vec_cols(p, c, bias_aps["bup"], bias, 0, 48)
    emit_vec_cols(p, c, bias_aps["bsw"], bias, 48, 48)
    emit_vec_cols(p, c, bias_aps["bsu"], bias, 96, 16)
    emit_vec_cols(p, c, bias_aps["bsa"], bias, 112, 8)
    emit_vec_cols(p, c, bias_aps["bsd"], bias, 120, 16)

    bdnf = sbuf(p, "mf_bdnf", [6, D], F32, es)
    bdnb = sbuf(p, "mf_bdnb", [6, D], BF16, es)
    bgf = sbuf(p, "mf_bgf", [1, 16], F32, es)
    bgb = sbuf(p, "mf_bgb", [128, 16], F32, es)
    sel = sbuf(p, "mf_sel", [128, 768], BF16, es)
    wg_sb = sbuf(p, "mf_wg", [128, 2 * 16 * 8], BF16, es)
    bcp = psum(p, "mf_bcp", [128, TT], F32, es)

    l1 = p.step("sync", [], lambda e: e.dma_start(out=bdnf[:], in_=bdn_ap),
                inc=ld, inc_k=16)
    l2 = p.step("sync", [], lambda e: e.dma_start(out=bgf[:], in_=bg_ap),
                inc=ld, inc_k=16)

    def mkwg(e):
        ins = None
        for g in range(32):
            if ins is not None:
                ins.then_inc(ld.h, 16)
            ins = e.dma_start(out=wg_sb[:, g * 8:(g + 1) * 8], in_=wg_ap[g])
        return ins
    _pre = ld.n
    ld.n += 16 * 31
    l3 = p.step("sync", [], mkwg, inc=ld, inc_k=16, dma_pre=_pre)
    v1 = p.step("vector", [(ld, l1)], lambda e: e.tensor_copy(bdnb[:], bdnf[:]),
                inc=ve)
    s1 = p.step("gpsimd", [], lambda e: (
        e.memset(sel[:], 0.0),
        e.affine_select(out=sel.rearrange("p (n m) -> p n m", m=128),
                        in_=sel.rearrange("p (n m) -> p n m", m=128),
                        compare_op=ALU.not_equal, fill=1.0, base=0,
                        pattern=[[-1, 6], [0, 128]], channel_multiplier=1))[-1],
        inc=gp)
    for j in range(2):
        b1 = p.step("tensor", [(ld, l2), (p.s["gp"], 1)],
                    lambda e, j=j: e.matmul(bcp[:, 0:8], c.ones_row_f[:],
                                            bgf[:, j * 8:(j + 1) * 8],
                                            start=True, stop=True), inc=te)
        p.step("vector", [(te, b1)], lambda e, j=j: e.tensor_copy(
            bgb[:, j * 8:(j + 1) * 8], bcp[:, 0:8]), inc=ve)
    p.barrier()

    xts = [sbuf(p, f"mf_x{i}", [128, NCH * TT], BF16, es) for i in range(2)]
    gt = sbuf(p, "mf_gt", [128, 32], F32, es)
    t8a = sbuf(p, "mf_t8a", [128, 8], F32, es)
    t8b = sbuf(p, "mf_t8b", [128, 8], F32, es)
    tcols = sbuf(p, "mf_tc", [128, 8], F32, es)
    wbf = sbuf(p, "mf_wbf", [128, 32], BF16, es)
    wr = sbuf(p, "mf_wr", [8, TT], BF16, es)
    wsl = [sbuf(p, f"mf_wu{i}", [128, 2048], BF16, es) for i in range(2)]
    wssl = [sbuf(p, f"mf_ws{i}", [128, 1024], BF16, es) for i in range(2)]
    hres = sbuf(p, "mf_h", [128, 8 * TT], BF16, es)
    ssb = [sbuf(p, f"mf_ss{i}", [128, TT], F32, es) for i in range(2)]
    gtmp = [sbuf(p, f"mf_gt{i}", [128, TT], BF16, es) for i in range(2)]
    g6 = sbuf(p, "mf_g6", [128, 6 * 8 * TT], BF16, es)
    shres = sbuf(p, "mf_sh", [128, 16 * TT], BF16, es)
    gsres = sbuf(p, "mf_gs", [128, 16 * TT], BF16, es)
    wdnsl = [[sbuf(p, f"mf_wd{i}{n}", [128, 1024], BF16, es) for n in range(6)]
             for i in range(2)]
    wsdsl = [sbuf(p, f"mf_wo{i}", [128, 2048], BF16, es) for i in range(2)]
    res32 = [sbuf(p, f"mf_r{i}", [128, TT], F32, es) for i in range(2)]
    xold = [sbuf(p, f"mf_xo{i}", [128, TT], F32, es) for i in range(2)]
    eout = [sbuf(p, f"mf_eo{i}", [128, TT], F32, es) for i in range(2)]
    ebf = [sbuf(p, f"mf_eb{i}", [128, TT], BF16, es) for i in range(2)]
    gps = psum(p, "mf_gps", [128, 32], F32, es)
    psA = [psum(p, f"mf_pA{i}", [128, TT], F32, es) for i in range(2)]
    psO = [psum(p, f"mf_pO{i}", [128, TT], F32, es) for i in range(2)]
    wtr = psum(p, "mf_wtr", [8, 128], BF16, es)

    for rnd in range(2):
        src_ap = xn_ap if rnd == 0 else xt1_ap
        for sup in range(NT):
            spar = sup % 2
            def mkxl(e, src_ap=src_ap, sup=sup, spar=spar):
                ins = None
                for kc in range(NCH):
                    if ins is not None:
                        ins.then_inc(ld.h, 16)
                    ins = e.dma_start(
                        out=xts[spar][:, kc * TT:(kc + 1) * TT],
                        in_=src_ap[kc * 128:(kc + 1) * 128,
                                   sup * TT:(sup + 1) * TT])
                return ins
            _pre = ld.n
            ld.n += 16 * (NCH - 1)
            xlw = []
            if (sup - 2, rnd, "lastmm") in marks:
                xlw.append((te, marks[(sup - 2, rnd, "lastmm")]))
            elif rnd > 0 and (NT - 2 + sup, rnd - 1, "lastmm") in marks:
                xlw.append((te, marks[(NT - 2 + sup, rnd - 1, "lastmm")]))
            xl = p.step("sync", xlw, mkxl, inc=ld, inc_k=16, dma_pre=_pre)

            gdone = []
            for gi in range(4):
                def mkg(e, gi=gi, spar=spar, rnd=rnd):
                    ins = None
                    for kc in range(NCH):
                        ins = e.matmul(
                            gps[:, gi * 8:(gi + 1) * 8],
                            xts[spar][:, kc * TT + gi * 128:kc * TT + gi * 128 + 128],
                            wg_sb[:, (rnd * 16 + kc) * 8:(rnd * 16 + kc + 1) * 8],
                            start=(kc == 0), stop=(kc == NCH - 1))
                    return ins
                w = [(ld, xl)]
                if gi > 0:
                    w.append((ve, gdone[gi - 1][0]))
                elif (sup - 1, rnd, "glast") in marks:
                    w.append((ve, marks[(sup - 1, rnd, "glast")]))
                gm = p.step("tensor", w, mkg, inc=te)
                ga = p.step("vector", [(te, gm)], lambda e, gi=gi, rnd=rnd:
                            e.tensor_add(gt[:, gi * 8:(gi + 1) * 8],
                                         gps[:, gi * 8:(gi + 1) * 8],
                                         bgb[:, rnd * 8:(rnd + 1) * 8]), inc=ve)
                gdone.append((ga, gm))
            wrdone = []
            for gi in range(4):
                sl = slice(gi * 8, (gi + 1) * 8)
                q0 = p.step("vector", [(ve, gdone[gi][0])],
                            lambda e, sl=sl: e.tensor_reduce(
                                tcols[:, 0:1], gt[:, sl], AX.X, ALU.min), inc=ve)
                q1 = p.step("vector", [(ve, q0)], lambda e, sl=sl: e.tensor_scalar(
                    t8a[:], gt[:, sl], tcols[:, 0:1], None, ALU.is_le), inc=ve)
                q2 = p.step("vector", [(ve, q1)], lambda e, sl=sl:
                            e.scalar_tensor_tensor(t8b[:], t8a[:], 1e9,
                                                   gt[:, sl], ALU.mult,
                                                   ALU.add), inc=ve)
                q3 = p.step("vector", [(ve, q2)], lambda e: e.tensor_reduce(
                    tcols[:, 1:2], t8b[:], AX.X, ALU.min), inc=ve)
                q4 = p.step("vector", [(ve, q3)], lambda e, sl=sl: e.tensor_scalar(
                    t8a[:], gt[:, sl], tcols[:, 1:2], None, ALU.is_gt), inc=ve)
                q5 = p.step("vector", [(ve, q4)], lambda e, sl=sl: e.tensor_reduce(
                    tcols[:, 2:3], gt[:, sl], AX.X, ALU.max), inc=ve)
                q6 = p.step("vector", [(ve, q5)], lambda e: e.tensor_scalar(
                    tcols[:, 3:4], tcols[:, 2:3], -1.0, None, ALU.mult), inc=ve)
                q7 = p.step("scalar", [(ve, q6)], lambda e, sl=sl: e.activation(
                    t8b[:], gt[:, sl], AF.Exp, bias=tcols[:, 3:4]), inc=ac)
                q8 = p.step("vector", [(ac, q7)], lambda e: e.tensor_mul(
                    t8b[:], t8b[:], t8a[:]), inc=ve)
                q9 = p.step("vector", [(ve, q8)], lambda e: e.tensor_reduce(
                    tcols[:, 4:5], t8b[:], AX.X, ALU.add), inc=ve)
                qa = p.step("vector", [(ve, q9)], lambda e: e.reciprocal(
                    tcols[:, 5:6], tcols[:, 4:5]), inc=ve)
                qb_ = p.step("vector", [(ve, qa)], lambda e, sl=sl: e.tensor_scalar(
                    wbf[:, sl], t8b[:], tcols[:, 5:6], None, ALU.mult), inc=ve)
                w = [(ve, qb_), (p.s["gp"], 1)]
                if wrdone:
                    w.append((ve, wrdone[-1]))
                tr = p.step("tensor", w, lambda e, sl=sl: e.transpose(
                    wtr[:], wbf[:, sl], c.ident_bf[:]), inc=te)
                ev = p.step("vector", [(te, tr)], lambda e, gi=gi: e.tensor_copy(
                    wr[:, gi * 128:(gi + 1) * 128], wtr[:]), inc=ve)
                wrdone.append(ev)
            marks[(sup, rnd, "glast")] = wrdone[-1]

            pA_free = {0: marks.get((sup - 1, rnd, "pA0"), 0),
                       1: marks.get((sup - 1, rnd, "pA1"), 0)}
            pA_sem = {0: marks.get((sup - 1, rnd, "pA0s"), ac),
                      1: marks.get((sup - 1, rnd, "pA1s"), ac)}
            bcw_free = (ve, 0)
            wl_prev = {}
            glast = 0
            for n in range(6):
                w = [(ve, wrdone[-1])]
                if bcw_free[1]:
                    w.append(bcw_free)
                bc1 = p.step("tensor", w, lambda e, n=n: e.matmul(
                    bcp[:], sel[0:8, n * 128:(n + 1) * 128], wr[:],
                    start=True, stop=True), inc=te)
                glast_prev = glast if n > 0 else marks.get(
                    (sup - 1, rnd, "g6done"), 0)
                glast = 0
                hdone = []
                for ec in range(8):
                    i = n * 8 + ec
                    epar = i % 2
                    w = []
                    if (i - 2) in wl_prev:
                        w.append((te, wl_prev[i - 2]))
                    wlt = p.step("scalar", w, lambda e, n=n, ec=ec, epar=epar:
                                 e.dma_start(out=wsl[epar][:],
                                             in_=wup_ap[n * 8 + ec]),
                                 inc=ld2, inc_k=16)

                    def mkup(e, ec=ec, epar=epar, spar=spar):
                        ins = None
                        for kc in range(NCH):
                            ins = e.matmul(psA[epar][:],
                                           wsl[epar][:, kc * 128:(kc + 1) * 128],
                                           xts[spar][:, kc * TT:(kc + 1) * TT],
                                           start=(kc == 0), stop=(kc == NCH - 1))
                        return ins
                    w = [(ld2, wlt), (ld, xl)]
                    w.append((pA_sem[epar], pA_free[epar]))
                    um = p.step("tensor", w, mkup, inc=te)
                    wl_prev[i] = um
                    he = p.step("scalar", [(te, um), (ve, glast_prev)],
                                lambda e, n=n, ec=ec, epar=epar: e.activation(
                                    hres[:, ec * TT:(ec + 1) * TT], psA[epar][:],
                                    AF.Identity, bias=bias[:, n * 8 + ec:
                                                          n * 8 + ec + 1]),
                                inc=ac)
                    pA_free[epar], pA_sem[epar] = he, ac
                    hdone.append(he)
                for fc in range(8):
                    i = n * 8 + fc + 1000
                    epar = fc % 2
                    w = []
                    if (i - 2) in wl_prev:
                        w.append((te, wl_prev[i - 2]))
                    wlt = p.step("scalar", w, lambda e, n=n, fc=fc, epar=epar:
                                 e.dma_start(out=wssl[epar][:],
                                             in_=wsw_ap[n * 8 + fc]),
                                 inc=ld2, inc_k=16)

                    def mksw(e, fc=fc, epar=epar):
                        ins = None
                        for ec in range(8):
                            ins = e.matmul(psA[epar][:],
                                           wssl[epar][:, ec * 128:(ec + 1) * 128],
                                           hres[:, ec * TT:(ec + 1) * TT],
                                           start=(ec == 0), stop=(ec == 7))
                        return ins
                    w = [(ld2, wlt), (ac, hdone[-1])]
                    w.append((pA_sem[epar], pA_free[epar]))
                    sm_ = p.step("tensor", w, mksw, inc=te)
                    wl_prev[i] = sm_
                    sle = p.step("scalar",
                                 [(te, sm_), (ve, wl_prev.get((n, fc - 2, "g1"), 0))],
                                 lambda e, n=n, fc=fc, epar=epar: e.activation(
                                     ssb[epar][:], psA[epar][:], AF.Silu,
                                     bias=bias[:, 48 + n * 8 + fc:
                                               49 + n * 8 + fc]), inc=ac)
                    pA_free[epar], pA_sem[epar] = sle, ac
                    g1 = p.step("vector", [(ac, sle)],
                                lambda e, fc=fc, epar=epar: e.tensor_mul(
                                    gtmp[epar][:], ssb[epar][:],
                                    hres[:, fc * TT:(fc + 1) * TT]), inc=ve)
                    wl_prev[(n, fc, "g1")] = g1
                    g2 = p.step("vector", [(ve, g1), (te, bc1)],
                                lambda e, n=n, fc=fc, epar=epar: e.tensor_mul(
                                    g6[:, (n * 8 + fc) * TT:(n * 8 + fc + 1) * TT],
                                    gtmp[epar][:], bcp[:]), inc=ve)
                    glast = g2
                bcw_free = (ve, glast)
            marks[(sup, rnd, "g6done")] = glast

            shdone = []
            for oc2 in range(16):
                i = oc2 + 2000
                epar = oc2 % 2
                w = []
                if (i - 2) in wl_prev:
                    w.append((te, wl_prev[i - 2]))
                wlt = p.step("scalar", w, lambda e, oc2=oc2, epar=epar:
                             e.dma_start(out=wsl[epar][:], in_=wsu_ap[oc2]),
                             inc=ld2, inc_k=16)

                def mksu(e, oc2=oc2, epar=epar, spar=spar):
                    ins = None
                    for kc in range(NCH):
                        ins = e.matmul(psA[epar][:],
                                       wsl[epar][:, kc * 128:(kc + 1) * 128],
                                       xts[spar][:, kc * TT:(kc + 1) * TT],
                                       start=(kc == 0), stop=(kc == NCH - 1))
                    return ins
                w = [(ld2, wlt), (ld, xl)]
                w.append((pA_sem[epar], pA_free[epar]))
                um = p.step("tensor", w, mksu, inc=te)
                wl_prev[i] = um
                he = p.step("scalar", [(te, um)],
                            lambda e, oc2=oc2, epar=epar: e.activation(
                                shres[:, oc2 * TT:(oc2 + 1) * TT], psA[epar][:],
                                AF.Identity, bias=bias[:, 96 + oc2:97 + oc2]),
                            inc=ac)
                pA_free[epar], pA_sem[epar] = he, ac
                shdone.append(he)
            gsl = 0
            for cc in range(2):
                for fc2 in range(8):
                    i = cc * 8 + fc2 + 3000
                    epar = (cc * 8 + fc2) % 2
                    w = []
                    if (i - 2) in wl_prev:
                        w.append((te, wl_prev[i - 2]))
                    wlt = p.step("scalar", w, lambda e, fc2=fc2, epar=epar:
                                 e.dma_start(out=wssl[epar][:], in_=wsa_ap[fc2]),
                                 inc=ld2, inc_k=16)

                    def mksa(e, cc=cc, fc2=fc2, epar=epar):
                        ins = None
                        for ec in range(8):
                            ins = e.matmul(
                                psA[epar][:],
                                wssl[epar][:, ec * 128:(ec + 1) * 128],
                                shres[:, (cc * 8 + ec) * TT:(cc * 8 + ec + 1) * TT],
                                start=(ec == 0), stop=(ec == 7))
                        return ins
                    w = [(ld2, wlt), (ac, shdone[-1])]
                    w.append((pA_sem[epar], pA_free[epar]))
                    sm_ = p.step("tensor", w, mksa, inc=te)
                    wl_prev[i] = sm_
                    sle = p.step("scalar",
                                 [(te, sm_),
                                  (ve, wl_prev.get((cc * 8 + fc2 - 2, "gsl"), 0))],
                                 lambda e, fc2=fc2, epar=epar: e.activation(
                                     ssb[epar][:], psA[epar][:], AF.Silu,
                                     bias=bias[:, 112 + fc2:113 + fc2]), inc=ac)
                    pA_free[epar], pA_sem[epar] = sle, ac
                    gsl = p.step("vector", [(ac, sle)],
                                 lambda e, cc=cc, fc2=fc2, epar=epar: e.tensor_mul(
                                     gsres[:, (cc * 8 + fc2) * TT:
                                           (cc * 8 + fc2 + 1) * TT],
                                     ssb[epar][:],
                                     shres[:, (cc * 8 + fc2) * TT:
                                           (cc * 8 + fc2 + 1) * TT]), inc=ve)
                    wl_prev[(cc * 8 + fc2, "gsl")] = gsl
            marks[(sup, rnd, "gsdone")] = gsl

            lastmm = 0
            for oc in range(NCH):
                opar = oc % 2

                def mkdl(e, oc=oc, opar=opar):
                    ins = None
                    for n in range(6):
                        if ins is not None:
                            ins.then_inc(ld2.h, 16)
                        ins = e.dma_start(out=wdnsl[opar][n][:],
                                          in_=wdn_ap[n * 16 + oc])
                    ins.then_inc(ld2.h, 16)
                    return e.dma_start(out=wsdsl[opar][:], in_=wsd_ap[oc])
                w = []
                if (oc - 2, "dl") in wl_prev:
                    w.append((te, wl_prev[(oc - 2, "dl")]))
                elif (sup - 1, rnd, "lastmm") in marks:
                    w.append((te, marks[(sup - 1, rnd, "lastmm")]))
                elif rnd > 0 and (NT - 1, rnd - 1, "lastmm") in marks:
                    w.append((te, marks[(NT - 1, rnd - 1, "lastmm")]))
                _pre = ld2.n
                ld2.n += 16 * 6
                dlt = p.step("scalar", w, mkdl, inc=ld2, inc_k=16, dma_pre=_pre)

                def mkacc(e, oc=oc, opar=opar):
                    first = True
                    for n in range(6):
                        for fc in range(8):
                            e.matmul(psO[opar][:],
                                     wdnsl[opar][n][:, fc * 128:(fc + 1) * 128],
                                     g6[:, (n * 8 + fc) * TT:(n * 8 + fc + 1) * TT],
                                     start=first, stop=False)
                            first = False
                    e.matmul(psO[opar][:], bdnb[0:6, oc * 128:(oc + 1) * 128],
                             wr[0:6, :], start=False, stop=False)
                    ins = None
                    for o2 in range(NCH):
                        ins = e.matmul(psO[opar][:],
                                       wsdsl[opar][:, o2 * 128:(o2 + 1) * 128],
                                       gsres[:, o2 * TT:(o2 + 1) * TT],
                                       start=False, stop=(o2 == NCH - 1))
                    return ins
                w = [(ld2, dlt), (ve, glast), (ve, gsl)]
                if (oc - 2, "epi") in wl_prev:
                    w.append(wl_prev[(oc - 2, "epi")])
                elif (sup - 1, rnd, f"epi{oc % 2}") in marks:
                    w.append(marks[(sup - 1, rnd, f"epi{oc % 2}")])
                elif rnd > 0 and (NT - 1, rnd - 1, f"epi{oc % 2}") in marks:
                    w.append(marks[(NT - 1, rnd - 1, f"epi{oc % 2}")])
                am = p.step("tensor", w, mkacc, inc=te)
                wl_prev[(oc, "dl")] = am
                lastmm = am

                resw = [(ve, wl_prev.get((oc - 2, "res"), 0))]
                if (oc - 2, "res") not in wl_prev:
                    for key in ((sup - 1, rnd, f"res{oc % 2}"),
                                (NT - 1, rnd - 1, f"res{oc % 2}")):
                        if key in marks:
                            resw = [(ve, marks[key])]
                            break
                rl_ = p.step("sync", resw,
                             lambda e, oc=oc, sup=sup, opar=opar: e.dma_start(
                                 out=res32[opar][:],
                                 in_=xt32_ap[oc * 128:(oc + 1) * 128,
                                             sup * TT:(sup + 1) * TT]),
                             inc=ld, inc_k=16)
                stw = wl_prev.get((oc - 2, "st"))
                if stw is None:
                    for key in ((sup - 1, rnd, f"st{oc % 2}"),
                                (NT - 1, rnd - 1, f"st{oc % 2}")):
                        if key in marks:
                            stw = (st, marks[key])
                            break
                    else:
                        stw = (ve, 0)
                if rnd == 0:
                    ep = p.step("vector", [(te, am), (ld, rl_), stw],
                                lambda e, oc=oc, opar=opar:
                                e.scalar_tensor_tensor(
                                    ebf[opar][:], psO[opar][:],
                                    bias[:, 120 + oc:121 + oc], res32[opar][:],
                                    ALU.add, ALU.add), inc=ve)
                    sd_ = p.step("scalar", [(ve, ep)],
                                 lambda e, oc=oc, sup=sup, opar=opar: e.dma_start(
                                     out=xt1_ap[oc * 128:(oc + 1) * 128,
                                                sup * TT:(sup + 1) * TT],
                                     in_=ebf[opar][:]), inc=st, inc_k=16)
                else:
                    xl2 = p.step("sync", resw,
                                 lambda e, oc=oc, sup=sup, opar=opar: e.dma_start(
                                     out=xold[opar][:],
                                     in_=x_src[oc * 128:(oc + 1) * 128,
                                               sup * TT:(sup + 1) * TT]),
                                 inc=ld, inc_k=16)
                    e1 = p.step("vector", [(te, am), (ld, rl_), (ld, xl2),
                                           stw],
                                lambda e, oc=oc, opar=opar:
                                e.scalar_tensor_tensor(
                                    eout[opar][:], psO[opar][:],
                                    bias[:, 120 + oc:121 + oc], res32[opar][:],
                                    ALU.add, ALU.add), inc=ve)
                    ep = p.step("vector", [(ve, e1)],
                                lambda e, opar=opar: e.tensor_add(
                                    eout[opar][:], eout[opar][:],
                                    xold[opar][:]), inc=ve)
                    sd_ = p.step("scalar", [(ve, ep)],
                                 lambda e, oc=oc, sup=sup, opar=opar: e.dma_start(
                                     out=x_dst[oc * 128:(oc + 1) * 128,
                                               sup * TT:(sup + 1) * TT],
                                     in_=eout[opar][:]), inc=st, inc_k=16)
                wl_prev[(oc, "res")] = ep
                wl_prev[(oc, "epi")] = (ve, ep)
                wl_prev[(oc, "st")] = (st, sd_)
                if oc >= NCH - 2:
                    marks[(sup, rnd, f"epi{oc % 2}")] = (ve, ep)
                    marks[(sup, rnd, f"res{oc % 2}")] = ep
                    marks[(sup, rnd, f"st{oc % 2}")] = sd_
            marks[(sup, rnd, "lastmm")] = lastmm
            marks[(sup, rnd, "pA0")] = pA_free[0]
            marks[(sup, rnd, "pA0s")] = pA_sem[0]
            marks[(sup, rnd, "pA1")] = pA_free[1]
            marks[(sup, rnd, "pA1s")] = pA_sem[1]
        p.barrier()
    p.barrier()
    es.close()


marks = {}

def prep_l1ffn(inp):
    Wg = np.asarray(inp["L1_ffn_Wg"])
    Wup = np.asarray(inp["L1_ffn_Wup"])
    Wsw = np.asarray(inp["L1_ffn_Wsw"])
    Wdn = np.asarray(inp["L1_ffn_Wdn"])
    Wsu = np.asarray(inp["L1_ffn_Wsu"])
    Wsa = np.asarray(inp["L1_ffn_Wsa"])
    Wsd = np.asarray(inp["L1_ffn_Wsd"])
    wg_t = np.ascontiguousarray(
        Wg.reshape(2, 16, 128, 8).reshape(32, 128, 8).astype(np.float32))
    wup_t = np.concatenate([tile_lhsT(Wup[n].astype(np.float32))
                            for n in range(6)], axis=0)
    wsw_t = np.concatenate([tile_lhsT(Wsw[n].astype(np.float32))
                            for n in range(6)], axis=0)
    wdn_t = np.concatenate([tile_lhsT(Wdn[n].astype(np.float32))
                            for n in range(6)], axis=0)
    return {
        "wg_t": wg_t, "wup_t": wup_t, "wsw_t": wsw_t, "wdn_t": wdn_t,
        "wsu_t": tile_lhsT(Wsu.astype(np.float32)),
        "wsa_t": tile_lhsT(Wsa.astype(np.float32)),
        "wsd_t": tile_lhsT(Wsd.astype(np.float32)),
        "bg": np.ascontiguousarray(
            np.asarray(inp["L1_ffn_bg"]).reshape(1, 16).astype(np.float32)),
        "bdn": np.ascontiguousarray(np.asarray(inp["L1_ffn_bdn"]).astype(np.float32)),
        "bup": np.ascontiguousarray(
            np.asarray(inp["L1_ffn_bup"]).reshape(48, 128).astype(np.float32)),
        "bsw": np.ascontiguousarray(
            np.asarray(inp["L1_ffn_bsw"]).reshape(48, 128).astype(np.float32)),
        "bsu": np.ascontiguousarray(
            np.asarray(inp["L1_ffn_bsu"]).reshape(16, 128).astype(np.float32)),
        "bsa": np.ascontiguousarray(
            np.asarray(inp["L1_ffn_bsa"]).reshape(8, 128).astype(np.float32)),
        "bsd": np.ascontiguousarray(
            np.asarray(inp["L1_ffn_bsd"]).reshape(16, 128).astype(np.float32)),
    }


def decl_l1ffn(nc, pref=""):
    d = {}
    d["wg_t"] = nc.dram_tensor("wg_t", [32, 128, 8], F32, kind="ExternalInput").ap()
    d["wup_t"] = nc.dram_tensor("wup_t", [48, 128, 2048], F32, kind="ExternalInput").ap()
    d["wsw_t"] = nc.dram_tensor("wsw_t", [48, 128, 1024], F32, kind="ExternalInput").ap()
    d["wdn_t"] = nc.dram_tensor("wdn_t", [96, 128, 1024], F32, kind="ExternalInput").ap()
    d["wsu_t"] = nc.dram_tensor("wsu_t", [16, 128, 2048], F32, kind="ExternalInput").ap()
    d["wsa_t"] = nc.dram_tensor("wsa_t", [8, 128, 1024], F32, kind="ExternalInput").ap()
    d["wsd_t"] = nc.dram_tensor("wsd_t", [16, 128, 2048], F32, kind="ExternalInput").ap()
    d["bg"] = nc.dram_tensor("bg", [1, 16], F32, kind="ExternalInput").ap()
    d["bdn"] = nc.dram_tensor("bdn", [6, 2048], F32, kind="ExternalInput").ap()
    for nm, g in (("bup", 48), ("bsw", 48), ("bsu", 16), ("bsa", 8), ("bsd", 16)):
        d[nm] = nc.dram_tensor(nm, [g, 128], F32, kind="ExternalInput").ap()
    d["WGB"] = nc.dram_tensor("WGB", [32, 128, 8], BF16, kind="Internal").ap()
    d["WUPB"] = nc.dram_tensor("WUPB", [48, 128, 2048], BF16, kind="Internal").ap()
    d["WSWB"] = nc.dram_tensor("WSWB", [48, 128, 1024], BF16, kind="Internal").ap()
    d["WDNB"] = nc.dram_tensor("WDNB", [96, 128, 1024], BF16, kind="Internal").ap()
    d["WSUB"] = nc.dram_tensor("WSUB", [16, 128, 2048], BF16, kind="Internal").ap()
    d["WSAB"] = nc.dram_tensor("WSAB", [8, 128, 1024], BF16, kind="Internal").ap()
    d["WSDB"] = nc.dram_tensor("WSDB", [16, 128, 2048], BF16, kind="Internal").ap()
    return d


def emit_l1ffn_sub(p, c, d, x_src, x_dst, XN, XT32, XT1):
    emit_l1ffn(p, c, x_src, x_dst, XN, XT32, XT1,
               d["WGB"], d["bg"], d["WUPB"], d["WSWB"], d["WDNB"], d["bdn"],
               d["WSUB"], d["WSAB"], d["WSDB"],
               {k: d[k] for k in ("bup", "bsw", "bsu", "bsa", "bsd")})


def build_dev_l1ffn(dbg=False):
    _uid[0] = 0
    nc = bass.Bass("TRN2", target_bir_lowering=False, debug=False)
    es = ExitStack()
    kind = "ExternalOutput" if dbg else "Internal"
    x_in = nc.dram_tensor("x", [D, T], F32, kind="ExternalInput").ap()
    x_out = nc.dram_tensor("x_out", [D, T], F32, kind="ExternalOutput").ap()
    d = decl_l1ffn(nc)
    XN = nc.dram_tensor("XN", [D, T], BF16, kind=kind).ap()
    XT32 = nc.dram_tensor("XT32", [D, T], F32, kind=kind).ap()
    XT1 = nc.dram_tensor("XT1", [D, T], BF16, kind=kind).ap()

    p = Prog(nc, es)
    c = emit_consts(p)
    emit_convert(p, [(d["wg_t"], d["WGB"]), (d["wup_t"], d["WUPB"]),
                     (d["wsw_t"], d["WSWB"]), (d["wdn_t"], d["WDNB"]),
                     (d["wsu_t"], d["WSUB"]), (d["wsa_t"], d["WSAB"]),
                     (d["wsd_t"], d["WSDB"])])
    emit_ln(p, c, x_in, XN, xt32_ap=XT32)
    emit_l1ffn_sub(p, c, d, x_in, x_out, XN, XT32, XT1)
    p.step("sync", [(p.s["st"], p.s["st"].n)], lambda e: None)
    p.finalize()
    return nc, es


def kernel(**inputs):
    xs = np.asarray(inputs["L0_attn_a_x"])
    ncores = xs.shape[0]
    shards = [np.ascontiguousarray(xs[i].T) for i in range(ncores)]
    t_aa = prep_l0aa(inputs)
    t_ff = prep_l0ffn(inputs)
    t_ab = prep_l0ab(inputs)
    t_sa = prep_l1aa(inputs)
    t_cb = prep_l1ab(inputs)
    t_mf = prep_l1ffn(inputs)
    seq = [
        (build_dev_l0aa, t_aa),
        (build_dev_l0ffn, t_ff),
        (build_dev_l0ab, t_ab),
        (build_dev_l0ffn, t_ff),
        (build_dev_l1aa, t_sa),
        (build_dev_l1ffn, t_mf),
        (build_dev_l1ab, t_cb),
        (build_dev_l1ffn, t_mf),
    ]
    cur = shards
    for build, tiles in seq:
        nc, es = build()
        in_maps = [{"x": cur[i], **tiles} for i in range(ncores)]
        res = run_bass_kernel_spmd(nc, in_maps, core_ids=list(range(ncores)))
        cur = [np.ascontiguousarray(res.results[i]["x_out"])
               for i in range(ncores)]
        es.close()
    return np.stack([c.T for c in cur], axis=0)


# version 03 (final)
# speedup vs baseline: 1.0324x; 1.0324x over previous
import numpy as np
import ml_dtypes
from contextlib import ExitStack

import concourse.bass as bass
import concourse.mybir as mybir
from concourse.bass_utils import run_bass_kernel_spmd

F32 = mybir.dt.float32
BF16 = mybir.dt.bfloat16
AF = mybir.ActivationFunctionType
ALU = mybir.AluOpType
AX = mybir.AxisListType

D = 2048
T = 4096
NCH = 16
TT = 512
NT = T // TT
HPAD = 5504
NHC = 43
H0REAL = 5461
LNEPS = 1e-5

BF = ml_dtypes.bfloat16



def tile_lhsT(wt):
    K, O = wt.shape
    kc, oc = K // 128, O // 128
    r = wt.reshape(kc, 128, oc, 128).transpose(2, 1, 0, 3).reshape(oc, 128, K)
    return np.ascontiguousarray(r)


def pad_to(w, rows=None, cols=None):
    r = rows if rows is not None else w.shape[0]
    c = cols if cols is not None else w.shape[1]
    if (r, c) == w.shape:
        return w
    out = np.zeros((r, c), dtype=w.dtype)
    out[:w.shape[0], :w.shape[1]] = w
    return out



class Sem:
    __slots__ = ("h", "n")

    def __init__(self, h):
        self.h = h
        self.n = 0


class Prog:
    ENGINES = ("sync", "gpsimd", "vector", "scalar", "tensor")

    def __init__(self, nc, es):
        self.nc = nc
        self.es = es
        self.ops = {e: [] for e in self.ENGINES}
        self.s = {}
        for name in ("ld", "ld2", "cast", "st", "te", "ve", "ac", "gp"):
            self.s[name] = Sem(es.enter_context(nc.semaphore("s_" + name)))
        self.dmaq = {self.s[k] for k in ("ld", "ld2", "cast", "st")}

    def step(self, eng, waits, fn, inc=None, inc_k=1, dma_pre=None):
        tgt = None
        if inc is not None and inc in self.dmaq:
            waits = list(waits) + [(inc, dma_pre if dma_pre is not None
                                    else inc.n)]
        if inc is not None:
            inc.n += inc_k
            tgt = inc.n
        waits = [(s.h, v) for (s, v) in waits if v > 0]

        def run(e, waits=waits, fn=fn, inc=inc, inc_k=inc_k):
            for h, v in waits:
                e.wait_ge(h, v)
            ins = fn(e)
            if inc is not None:
                ins.then_inc(inc.h, inc_k)
        self.ops[eng].append(run)
        return tgt

    def raw(self, eng, fn):
        self.ops[eng].append(fn)

    def barrier(self):
        vals = [(s.h, s.n) for s in self.s.values() if s.n > 0]
        for eng in self.ENGINES:
            def run(e, vals=vals):
                for h, v in vals:
                    e.wait_ge(h, v)
            self.ops[eng].append(run)

    def finalize(self):
        block = self.es.enter_context(self.nc.Block())
        for eng in self.ENGINES:
            fns = self.ops[eng]
            getattr(block, eng)(lambda e, fns=fns: [f(e) for f in fns])


_uid = [0]


def _uniq(name):
    _uid[0] += 1
    return f"{name}_{_uid[0]}"


def sbuf(p, name, shape, dtype, es=None):
    return (es or p.es).enter_context(p.nc.sbuf_tensor(_uniq(name), shape, dtype))


def psum(p, name, shape, dtype=F32, es=None):
    return (es or p.es).enter_context(p.nc.psum_tensor(_uniq(name), shape, dtype))



class NS:
    pass


def emit_consts(p):
    c = NS()
    c.ident_bf = sbuf(p, "ident_bf", [128, 128], BF16)
    c.ident_f = sbuf(p, "ident_f", [128, 128], F32)
    c.ones_col_bf = sbuf(p, "ones_col_bf", [128, 1], BF16)
    c.ones_col_f = sbuf(p, "ones_col_f", [128, 1], F32)
    c.ones_row_f = sbuf(p, "ones_row_f", [1, 128], F32)
    c.ones_row_bf = sbuf(p, "ones_row_bf", [1, 128], BF16)
    c.scale = sbuf(p, "scale_sb", [128, 16], F32)
    c.eps1 = sbuf(p, "eps1", [1, 1], F32)
    gp = p.s["gp"]

    def fn(e):
        e.memset(c.ident_bf[:], 0.0)
        e.affine_select(out=c.ident_bf[:], in_=c.ident_bf[:],
                        compare_op=ALU.not_equal, fill=1.0, base=0,
                        pattern=[[-1, 128]], channel_multiplier=1)
        e.memset(c.ident_f[:], 0.0)
        e.affine_select(out=c.ident_f[:], in_=c.ident_f[:],
                        compare_op=ALU.not_equal, fill=1.0, base=0,
                        pattern=[[-1, 128]], channel_multiplier=1)
        e.memset(c.ones_col_bf[:], 1.0)
        e.memset(c.ones_col_f[:], 1.0)
        e.memset(c.ones_row_f[:], 1.0)
        e.memset(c.eps1[:], LNEPS)
        return e.memset(c.ones_row_bf[:], 1.0)
    p.step("gpsimd", [], fn, inc=gp)
    p.barrier()
    return c



def emit_convert(p, jobs):
    if not jobs:
        return
    MAXM = max(src.shape[2] for src, _ in jobs)
    es = ExitStack()
    fb = [sbuf(p, f"cv_f{i}", [128, MAXM], F32, es) for i in range(2)]
    bb = [sbuf(p, f"cv_b{i}", [128, MAXM], BF16, es) for i in range(2)]
    ld, ve, st = p.s["ld"], p.s["ve"], p.s["st"]
    slabs = [(src, dst, g, src.shape[2])
             for src, dst in jobs for g in range(src.shape[0])]
    casts = []
    stores = []
    loads = []
    for i, (src, dst, g, M) in enumerate(slabs):
        w = [(ve, casts[i - 2])] if i >= 2 else []
        loads.append(p.step(
            "sync", w,
            lambda e, src=src, g=g, M=M, i=i: e.dma_start(
                out=fb[i % 2][:, 0:M], in_=src[g]), inc=ld, inc_k=16))
        w = [(ld, loads[i])] + ([(st, stores[i - 2])] if i >= 2 else [])
        casts.append(p.step(
            "vector", w,
            lambda e, M=M, i=i: e.tensor_copy(bb[i % 2][:, 0:M], fb[i % 2][:, 0:M]),
            inc=ve))
        stores.append(p.step(
            "scalar", [(ve, casts[i])],
            lambda e, dst=dst, g=g, M=M, i=i: e.dma_start(
                out=dst[g], in_=bb[i % 2][:, 0:M]), inc=st, inc_k=16))
    p.barrier()
    es.close()



def emit_quant(p, c, jobs):
    MAXM = max(j[0].shape[2] for j in jobs)
    es = ExitStack()
    fb = [sbuf(p, f"qt_f{i}", [128, MAXM], F32, es) for i in range(2)]
    tb = [sbuf(p, f"qt_t{i}", [128, MAXM], BF16, es) for i in range(2)]
    ab = [sbuf(p, f"qt_a{i}", [128, MAXM], F32, es) for i in range(2)]
    sg = [sbuf(p, f"qt_s{i}", [128, MAXM], F32, es) for i in range(2)]
    parts = sbuf(p, "qt_parts", [128, 64], F32, es)
    tot = sbuf(p, "qt_tot", [1, 2], F32, es)
    ps_t = psum(p, "qt_pst", [1, 2], F32, es)
    ps_b = psum(p, "qt_psb", [128, 1], F32, es)
    ld, ve, ac, te, st = (p.s[k] for k in ("ld", "ve", "ac", "te", "st"))

    for src, dst, count, col in jobs:
        G, _, M = src.shape
        loads, reds = [], []
        for g in range(G):
            w = [(ve, reds[g - 2])] if g >= 2 else []
            loads.append(p.step(
                "sync", w, lambda e, src=src, g=g, M=M: e.dma_start(
                    out=fb[g % 2][:, 0:M], in_=src[g]), inc=ld, inc_k=16))
            reds.append(p.step(
                "vector", [(ld, loads[g])],
                lambda e, g=g, M=M: e.tensor_reduce(
                    parts[:, g:g + 1], fb[g % 2][:, 0:M], AX.X, ALU.add,
                    apply_absolute_value=True), inc=ve))
        t0 = p.step("vector", [(ve, reds[-1])],
                    lambda e, G=G: e.tensor_reduce(
                        parts[:, 63:64], parts[:, 0:G], AX.X, ALU.add), inc=ve)
        t1 = p.step("tensor", [(ve, t0)],
                    lambda e: e.matmul(ps_t[:, 0:1], c.ones_col_f[:],
                                       parts[:, 63:64], start=True, stop=True),
                    inc=te)
        t2 = p.step("vector", [(te, t1)],
                    lambda e, count=count: e.tensor_scalar(
                        tot[:, 0:1], ps_t[:, 0:1], 1.0 / count, None, ALU.mult),
                    inc=ve)
        t3 = p.step("tensor", [(ve, t2)],
                    lambda e: e.matmul(ps_b[:], c.ones_row_f[:], tot[:, 0:1],
                                       start=True, stop=True), inc=te)
        t4 = p.step("vector", [(te, t3)],
                    lambda e, col=col: e.tensor_copy(
                        c.scale[:, col:col + 1], ps_b[:]), inc=ve)
        t5 = p.step("vector", [(ve, t4)],
                    lambda e, col=col: e.tensor_mul(
                        c.scale[:, col + 8:col + 9], c.scale[:, col:col + 1],
                        c.scale[:, col:col + 1]), inc=ve)
        t5 = p.step("vector", [(ve, t5)],
                    lambda e, col=col: e.tensor_scalar(
                        c.scale[:, col + 8:col + 9],
                        c.scale[:, col + 8:col + 9], 0.25, None,
                        ALU.mult), inc=ve)
        p.barrier()

        loads, sgns, muls, stores = [], [], [], []
        for g in range(G):
            w = []
            if g >= 2:
                w = [(ve, muls[g - 2]), (ac, sgns[g - 2])]
            loads.append(p.step(
                "sync", w, lambda e, src=src, g=g, M=M: e.dma_start(
                    out=fb[g % 2][:, 0:M], in_=src[g]), inc=ld, inc_k=16))
            w = [(ld, loads[g])]
            if g >= 2:
                w.append((ve, muls[g - 2]))
            sgns.append(p.step(
                "scalar", w, lambda e, g=g, M=M: e.activation(
                    sg[g % 2][:, 0:M], fb[g % 2][:, 0:M], AF.Sign), inc=ac))

            def mk_mask(e, g=g, M=M, col=col):
                e.tensor_mul(ab[g % 2][:, 0:M], fb[g % 2][:, 0:M],
                             fb[g % 2][:, 0:M])
                return e.tensor_scalar(ab[g % 2][:, 0:M], ab[g % 2][:, 0:M],
                                       c.scale[:, col + 8:col + 9], None,
                                       ALU.is_gt)
            w = [(ld, loads[g])]
            if g >= 2:
                w.append((st, stores[g - 2]))
            mk = p.step("vector", w, mk_mask, inc=ve)
            muls.append(p.step(
                "vector", [(ac, sgns[g])],
                lambda e, g=g, M=M: e.tensor_mul(
                    tb[g % 2][:, 0:M], sg[g % 2][:, 0:M], ab[g % 2][:, 0:M]),
                inc=ve))
            stores.append(p.step(
                "sync", [(ve, muls[g])],
                lambda e, dst=dst, g=g, M=M: e.dma_start(
                    out=dst[g], in_=tb[g % 2][:, 0:M]), inc=st, inc_k=16))
        p.barrier()
    es.close()



def emit_ln(p, c, x_ap, xn_ap, xt32_ap=None):
    es = ExitStack()
    xs = [sbuf(p, f"ln_x{i}", [128, NCH * TT], F32, es) for i in range(2)]
    xb = [sbuf(p, f"ln_b{i}", [128, NCH * TT], BF16, es) for i in range(2)]
    sq = sbuf(p, "ln_q", [128, 2 * TT], BF16, es)
    rows = sbuf(p, "ln_rows", [1, 4 * TT], F32, es)
    tmp = sbuf(p, "ln_tmp", [128, 2 * TT], F32, es)
    xnb = sbuf(p, "ln_xn", [128, 2 * TT], BF16, es)
    xf = sbuf(p, "ln_xf", [128, 2 * TT], F32, es) if xt32_ap is not None else None
    ps_m = psum(p, "ln_psm", [1, TT], F32, es)
    ps_s = psum(p, "ln_pss", [1, TT], F32, es)
    ps_bm = psum(p, "ln_bm", [128, TT], F32, es)
    ps_br = psum(p, "ln_br", [128, TT], F32, es)
    ld, ve, ac, te, st = (p.s[k] for k in ("ld", "ve", "ac", "te", "st"))

    norm_last = {0: 0, 1: 0}
    mm_last = {0: 0, 1: 0}
    mm_prev_tile = 0
    stats_done = 0
    norm_prev_tile = 0

    for t in range(NT):
        par = t % 2
        loads, casts, sqs, mms = [], [], [], []
        for ci in range(NCH):
            w = [(ve, norm_last[par])] if ci == 0 else []
            loads.append(p.step(
                "sync", w, lambda e, ci=ci, t=t, par=par: e.dma_start(
                    out=xs[par][:, ci * TT:(ci + 1) * TT],
                    in_=x_ap[ci * 128:(ci + 1) * 128, t * TT:(t + 1) * TT]),
                inc=ld, inc_k=16))
        for ci in range(NCH):
            w = [(ld, loads[ci])]
            if ci == 0:
                w.append((te, mm_last[par]))
            casts.append(p.step(
                "vector", w, lambda e, ci=ci, par=par: e.tensor_copy(
                    xb[par][:, ci * TT:(ci + 1) * TT],
                    xs[par][:, ci * TT:(ci + 1) * TT]), inc=ve))
            w = [(ve, casts[ci])]
            w.append((te, mms[ci - 2] if ci >= 2 else mm_prev_tile))
            sqs.append(p.step(
                "scalar", w, lambda e, ci=ci, par=par: e.activation(
                    sq[:, (ci % 2) * TT:(ci % 2 + 1) * TT],
                    xb[par][:, ci * TT:(ci + 1) * TT], AF.Square), inc=ac))

            def mk_mm(e, ci=ci, par=par):
                e.matmul(ps_m[:], c.ones_col_bf[:],
                         xb[par][:, ci * TT:(ci + 1) * TT],
                         start=(ci == 0), stop=(ci == NCH - 1))
                return e.matmul(ps_s[:], c.ones_col_bf[:],
                                sq[:, (ci % 2) * TT:(ci % 2 + 1) * TT],
                                start=(ci == 0), stop=(ci == NCH - 1))
            w = [(ve, casts[ci]), (ac, sqs[ci])]
            if ci == 0 and t > 0:
                w.append((ve, stats_done))
            mms.append(p.step("tensor", w, mk_mm, inc=te))
        mm_last[par] = mms[-1]
        mm_prev_tile = mms[-1]

        s0 = p.step("vector", [(te, mms[-1])],
                    lambda e: e.tensor_scalar(rows[:, 0:TT], ps_m[:], 1.0 / D,
                                              None, ALU.mult), inc=ve)
        s1 = p.step("vector", [],
                    lambda e: e.tensor_mul(rows[:, TT:2 * TT], rows[:, 0:TT],
                                           rows[:, 0:TT]), inc=ve)
        s2 = p.step("vector", [],
                    lambda e: e.scalar_tensor_tensor(
                        rows[:, TT:2 * TT], ps_s[:], 1.0 / D,
                        rows[:, TT:2 * TT], ALU.mult, ALU.subtract), inc=ve)
        s3 = p.step("scalar", [(ve, s2)],
                    lambda e: e.activation(rows[:, 2 * TT:3 * TT],
                                           rows[:, TT:2 * TT], AF.Sqrt,
                                           bias=c.eps1[:, 0:1]), inc=ac)
        s4 = p.step("vector", [(ac, s3)],
                    lambda e: e.reciprocal(rows[:, 3 * TT:4 * TT],
                                           rows[:, 2 * TT:3 * TT]), inc=ve)
        stats_done = s4

        def mk_bc(e):
            e.matmul(ps_bm[:], c.ones_row_f[:], rows[:, 0:TT],
                     start=True, stop=True)
            return e.matmul(ps_br[:], c.ones_row_f[:], rows[:, 3 * TT:4 * TT],
                            start=True, stop=True)
        bc = p.step("tensor", [(ve, s4), (ve, norm_prev_tile)], mk_bc, inc=te)

        norms, stores = [], []
        for ci in range(NCH):
            def mk_n(e, ci=ci, par=par):
                sl = slice((ci % 2) * TT, (ci % 2 + 1) * TT)
                e.tensor_sub(tmp[:, sl], xs[par][:, ci * TT:(ci + 1) * TT],
                             ps_bm[:])
                ins = e.tensor_mul(xnb[:, sl], tmp[:, sl], ps_br[:])
                if xf is not None:
                    ins = e.tensor_mul(xf[:, sl], tmp[:, sl], ps_br[:])
                return ins
            w = [(te, bc)]
            if ci >= 2:
                w.append((st, stores[ci - 2]))
            norms.append(p.step("vector", w, mk_n, inc=ve))

            def mk_st(e, ci=ci, t=t):
                sl = slice((ci % 2) * TT, (ci % 2 + 1) * TT)
                if xt32_ap is not None:
                    e.dma_start(
                        out=xt32_ap[ci * 128:(ci + 1) * 128, t * TT:(t + 1) * TT],
                        in_=xf[:, sl]).then_inc(st.h, 16)
                return e.dma_start(
                    out=xn_ap[ci * 128:(ci + 1) * 128, t * TT:(t + 1) * TT],
                    in_=xnb[:, sl])
            pre = st.n
            if xt32_ap is not None:
                st.n += 16
            sval = p.step("scalar", [(ve, norms[ci])], mk_st, inc=st,
                          inc_k=16, dma_pre=pre)
            stores.append(sval)
        norm_last[par] = norms[-1]
        norm_prev_tile = norms[-1]
    p.barrier()
    es.close()



def emit_l0ffn(p, c, x_src, x_dst, xn_ap, g_ap, w1q_ap, w3q_ap, w2q_ap):
    ld, ld2, ve, ac, te, st = (p.s[k] for k in ("ld", "ld2", "ve", "ac", "te", "st"))

    p.step("vector", [], lambda e: e.tensor_mul(
        c.scale[:, 3:4], c.scale[:, 1:2], c.scale[:, 2:3]), inc=ve)
    p.barrier()

    esA = ExitStack()
    xnres = sbuf(p, "ff_xn", [128, NCH * T], BF16, esA)
    wb = {(j, i): sbuf(p, f"ff_w{j}{i}", [128, 2048], BF16, esA)
          for j in (1, 3) for i in range(2)}
    hs = [sbuf(p, f"ff_h{i}", [128, TT], F32, esA) for i in range(2)]
    gb = [sbuf(p, f"ff_g{i}", [128, TT], BF16, esA) for i in range(2)]
    ps1 = [psum(p, f"ff_p1{i}", [128, TT], F32, esA) for i in range(2)]
    ps3 = [psum(p, f"ff_p3{i}", [128, TT], F32, esA) for i in range(2)]

    xn_loads = []
    for ci in range(NCH):
        xn_loads.append(p.step(
            "sync", [], lambda e, ci=ci: e.dma_start(
                out=xnres[:, ci * T:(ci + 1) * T],
                in_=xn_ap[ci * 128:(ci + 1) * 128, :]), inc=ld, inc_k=16))
    xn_ready = xn_loads[-1]

    wloads = []
    silu_t, gmul_t, store_t, mm_t = {}, {}, {}, {}
    for hc in range(NHC):
        wpar = hc % 2

        def mk_wl(e, hc=hc, wpar=wpar):
            e.dma_start(out=wb[(1, wpar)][:], in_=w1q_ap[hc]).then_inc(ld2.h, 16)
            return e.dma_start(out=wb[(3, wpar)][:], in_=w3q_ap[hc])
        w = [(te, mm_t.get((hc - 2, NT - 1), 0))]
        pre = ld2.n
        ld2.n += 16
        wloads.append(p.step("scalar", w, mk_wl, inc=ld2, inc_k=16,
                             dma_pre=pre))

        for t in range(NT):
            par = t % 2

            def mk_mm(e, hc=hc, t=t, par=par, wpar=wpar):
                ins = None
                for ci in range(NCH):
                    e.matmul(ps1[par][:], wb[(1, wpar)][:, ci * 128:(ci + 1) * 128],
                             xnres[:, ci * T + t * TT:ci * T + (t + 1) * TT],
                             start=(ci == 0), stop=(ci == NCH - 1))
                for ci in range(NCH):
                    ins = e.matmul(ps3[par][:], wb[(3, wpar)][:, ci * 128:(ci + 1) * 128],
                                   xnres[:, ci * T + t * TT:ci * T + (t + 1) * TT],
                                   start=(ci == 0), stop=(ci == NCH - 1))
                return ins
            w = [(ld2, wloads[hc]), (ld, xn_ready)]
            key_prev = (hc, t - 2) if t >= 2 else (hc - 1, NT - 2 + t)
            if key_prev in silu_t:
                w += [(ac, silu_t[key_prev]), (ve, gmul_t[key_prev])]
            mm_t[(hc, t)] = p.step("tensor", w, mk_mm, inc=te)

            silu_t[(hc, t)] = p.step(
                "scalar", [(te, mm_t[(hc, t)])],
                lambda e, par=par: e.activation(
                    hs[par][:], ps1[par][:], AF.Silu,
                    scale=c.scale[:, 0:1]), inc=ac)

            w = [(ac, silu_t[(hc, t)])]
            key_prev = (hc, t - 2) if t >= 2 else (hc - 1, NT - 2 + t)
            if key_prev in store_t:
                w.append((st, store_t[key_prev]))
            gmul_t[(hc, t)] = p.step(
                "vector", w, lambda e, par=par: e.tensor_mul(
                    gb[par][:], hs[par][:], ps3[par][:]), inc=ve)

            store_t[(hc, t)] = p.step(
                "sync", [(ve, gmul_t[(hc, t)])],
                lambda e, hc=hc, t=t, par=par: e.dma_start(
                    out=g_ap[hc][:, t * TT:(t + 1) * TT], in_=gb[par][:]),
                inc=st, inc_k=16)
    p.barrier()
    esA.close()

    esB = ExitStack()
    gres = [sbuf(p, f"fb_g{i}", [128, NHC * TT], BF16, esB) for i in range(2)]
    w2b = [sbuf(p, f"fb_w{i}", [128, HPAD], BF16, esB) for i in range(2)]
    xo = [sbuf(p, f"fb_x{i}", [128, TT], F32, esB) for i in range(2)]
    xnew = [sbuf(p, f"fb_n{i}", [128, TT], F32, esB) for i in range(2)]
    psy = [psum(p, f"fb_y{i}", [128, TT], F32, esB) for i in range(2)]

    gload_t = {}
    w2load_t = {}
    mmB_t = {}
    epi_t = {}
    stB_t = {}
    xload_t = {}
    for t in range(NT):
        gpar = t % 2

        def mk_gl(e, t=t, gpar=gpar):
            ins = None
            for hc in range(NHC):
                if ins is not None:
                    ins.then_inc(ld.h, 16)
                ins = e.dma_start(out=gres[gpar][:, hc * TT:(hc + 1) * TT],
                                  in_=g_ap[hc][:, t * TT:(t + 1) * TT])
            return ins
        w = []
        if t >= 2:
            w.append((te, mmB_t[(t - 2, NCH - 1)]))
        pre = ld.n
        ld.n += 16 * (NHC - 1)
        gload_t[t] = p.step("sync", w, mk_gl, inc=ld, inc_k=16, dma_pre=pre)

        for oc in range(NCH):
            i = t * NCH + oc
            wpar = i % 2
            w = []
            if i >= 2:
                w.append((te, mmB_t[((i - 2) // NCH, (i - 2) % NCH)]))
            w2load_t[i] = p.step(
                "scalar", w, lambda e, oc=oc, wpar=wpar: e.dma_start(
                    out=w2b[wpar][:], in_=w2q_ap[oc]), inc=ld2, inc_k=16)
            xload_t[i] = p.step(
                "sync", [(ve, epi_t.get(i - 2, 0))],
                lambda e, oc=oc, t=t, wpar=wpar: e.dma_start(
                    out=xo[wpar][:], in_=x_src[oc * 128:(oc + 1) * 128,
                                              t * TT:(t + 1) * TT]),
                inc=ld, inc_k=16)

            def mk_mmB(e, t=t, oc=oc, gpar=gpar, wpar=wpar):
                ins = None
                for hc in range(NHC):
                    ins = e.matmul(psy[wpar][:],
                                   w2b[wpar][:, hc * 128:(hc + 1) * 128],
                                   gres[gpar][:, hc * TT:(hc + 1) * TT],
                                   start=(hc == 0), stop=(hc == NHC - 1))
                return ins
            w = [(ld2, w2load_t[i]), (ld, gload_t[t])]
            if i >= 2:
                w.append((ve, epi_t[i - 2]))
            mmB_t[(t, oc)] = p.step("tensor", w, mk_mmB, inc=te)

            w = [(te, mmB_t[(t, oc)]), (ld, xload_t[i])]
            if i >= 2:
                w.append((st, stB_t[i - 2]))
            epi_t[i] = p.step(
                "vector", w, lambda e, wpar=wpar: e.scalar_tensor_tensor(
                    xnew[wpar][:], psy[wpar][:], c.scale[:, 3:4], xo[wpar][:],
                    ALU.mult, ALU.add), inc=ve)
            stB_t[i] = p.step(
                "scalar", [(ve, epi_t[i])],
                lambda e, oc=oc, t=t, wpar=wpar: e.dma_start(
                    out=x_dst[oc * 128:(oc + 1) * 128, t * TT:(t + 1) * TT],
                    in_=xnew[wpar][:]), inc=st, inc_k=16)
    p.barrier()
    esB.close()



def prep_l0ffn(inp):
    w1 = pad_to(np.asarray(inp["L0_ffn_w1"]), rows=HPAD)
    w3 = pad_to(np.asarray(inp["L0_ffn_w3"]), rows=HPAD)
    w2 = pad_to(np.asarray(inp["L0_ffn_w2"]), cols=HPAD)
    return {
        "w1t": tile_lhsT(np.ascontiguousarray(w1.T)),
        "w3t": tile_lhsT(np.ascontiguousarray(w3.T)),
        "w2t": tile_lhsT(np.ascontiguousarray(w2.T)),
    }



def build_dev_l0ffn():
    _uid[0] = 0
    nc = bass.Bass("TRN2", target_bir_lowering=False, debug=False)
    es = ExitStack()
    x_in = nc.dram_tensor("x", [D, T], F32, kind="ExternalInput").ap()
    w1t = nc.dram_tensor("w1t", [NHC, 128, 2048], F32, kind="ExternalInput").ap()
    w3t = nc.dram_tensor("w3t", [NHC, 128, 2048], F32, kind="ExternalInput").ap()
    w2t = nc.dram_tensor("w2t", [NCH, 128, HPAD], F32, kind="ExternalInput").ap()
    x_out = nc.dram_tensor("x_out", [D, T], F32, kind="ExternalOutput").ap()
    XN = nc.dram_tensor("XN", [D, T], BF16, kind="Internal").ap()
    G = nc.dram_tensor("G", [NHC, 128, T], BF16, kind="Internal").ap()
    W1Q = nc.dram_tensor("W1Q", [NHC, 128, 2048], BF16, kind="Internal").ap()
    W3Q = nc.dram_tensor("W3Q", [NHC, 128, 2048], BF16, kind="Internal").ap()
    W2Q = nc.dram_tensor("W2Q", [NCH, 128, HPAD], BF16, kind="Internal").ap()

    p = Prog(nc, es)
    c = emit_consts(p)
    emit_quant(p, c, [(w1t, W1Q, H0REAL * D, 0), (w3t, W3Q, H0REAL * D, 1),
                      (w2t, W2Q, H0REAL * D, 2)])
    emit_ln(p, c, x_in, XN)
    emit_l0ffn(p, c, x_in, x_out, XN, G, W1Q, W3Q, W2Q)
    p.step("sync", [(p.s["st"], p.s["st"].n)], lambda e: None)
    p.finalize()
    return nc, es


def emit_resident(p, src_ap, buf, rows=D):
    ld = p.s["ld"]
    tgt = 0
    n = rows // 128
    for ci in range(n):
        tgt = p.step("sync", [], lambda e, ci=ci: e.dma_start(
            out=buf[:, ci * T:(ci + 1) * T],
            in_=src_ap[ci * 128:(ci + 1) * 128, :]), inc=ld, inc_k=16)
    return tgt


def emit_vec_cols(p, c, vec_ap, dst_sb, dst_col0, G):
    es = ExitStack()
    tmp = sbuf(p, "vc_tmp", [128, 128], F32, es)
    ps = psum(p, "vc_ps", [128, 128], F32, es)
    ld, te, ve = p.s["ld"], p.s["te"], p.s["ve"]
    l0 = p.step("sync", [], lambda e: e.dma_start(
        out=tmp[0:G, :], in_=vec_ap), inc=ld, inc_k=16)
    t0 = p.step("tensor", [(ld, l0), (p.s["gp"], 1)],
                lambda e: e.transpose(ps[:, 0:G], tmp[0:G, :], c.ident_f[0:G, 0:G]),
                inc=te)
    p.step("vector", [(te, t0)],
           lambda e: e.tensor_copy(dst_sb[:, dst_col0:dst_col0 + G],
                                   ps[:, 0:G]), inc=ve)
    p.barrier()
    es.close()


def emit_projection(p, c, xnres, xn_ready, w_ap, OC, KCH, epi, name="pj"):
    es = ExitStack()
    wb = [sbuf(p, f"{name}_w{i}", [128, KCH * 128], BF16, es) for i in range(2)]
    ps = [psum(p, f"{name}_p{i}", [128, TT], F32, es) for i in range(2)]
    ld2, te = p.s["ld2"], p.s["te"]
    mmt = {}
    freed = {}
    for oc in range(OC):
        wpar = oc % 2
        w = [(te, mmt[(oc - 2, NT - 1)])] if oc >= 2 else []
        wl = p.step("scalar", w, lambda e, oc=oc, wpar=wpar: e.dma_start(
            out=wb[wpar][:], in_=w_ap[oc]), inc=ld2, inc_k=16)
        for t in range(NT):
            i = oc * NT + t
            par = i % 2

            def mk(e, oc=oc, t=t, par=par, wpar=wpar):
                ins = None
                for kc in range(KCH):
                    ins = e.matmul(ps[par][:],
                                   wb[wpar][:, kc * 128:(kc + 1) * 128],
                                   xnres[:, kc * T + t * TT:kc * T + (t + 1) * TT],
                                   start=(kc == 0), stop=(kc == KCH - 1))
                return ins
            w = [(ld2, wl), (p.s["ld"], xn_ready)]
            if i >= 2:
                w += freed[i - 2]
            mmt[(oc, t)] = p.step("tensor", w, mk, inc=te)
            freed[i] = epi(oc, t, par, mmt[(oc, t)], ps[par])
    p.barrier()
    es.close()



NHEAD = 16
HD = 128
SCL = 1.0 / np.sqrt(128.0)


def emit_causal_attn_core(p, c, q_ap, k_ap, vt_ap, out_ap, mode="softmax",
                          l1aa=False):
    es = ExitStack()
    qh = [sbuf(p, f"at_q{i}", [128, T], BF16, es) for i in range(2)]
    kh = [sbuf(p, f"at_k{i}", [128, T], BF16, es) for i in range(2)]
    vt = [sbuf(p, f"at_v{i}", [128, T], BF16, es) for i in range(2)]
    prow = [sbuf(p, f"at_p{j}", [128, T], BF16, es) for j in range(4)]
    ptr = [sbuf(p, f"at_t{i}", [128, TT], BF16, es) for i in range(2)]
    obuf = [sbuf(p, f"at_o{i}", [128, TT], BF16, es) for i in range(2)]
    cols = sbuf(p, "at_c", [128, 16], F32, es)
    tri = [sbuf(p, f"at_tri{j}", [128, TT], F32, es) for j in range(4)]
    sA = psum(p, "at_sA", [128, 3 * TT], F32, es)
    sB = psum(p, "at_sB", [128, 2 * TT], F32, es)
    pv = psum(p, "at_pv", [128, TT], F32, es)
    trp = [psum(p, f"at_tr{i}", [128, TT], BF16, es) for i in range(2)]
    ld, ve, ac, te, gp, st = (p.s[k] for k in ("ld", "ve", "ac", "te", "gp", "st"))

    def mk_tri(e):
        ins = None
        for j in range(4):
            e.memset(tri[j][:], 0.0)
            ins = e.affine_select(
                out=tri[j][:], in_=tri[j][:], compare_op=ALU.is_ge,
                fill=-1e9, base=j * 128, pattern=[[-1, TT]],
                channel_multiplier=1)
        return ins
    tri_done = p.step("gpsimd", [], mk_tri, inc=gp)

    def ldhead(h, par):
        t1 = p.step("sync", [], lambda e, h=h, par=par: e.dma_start(
            out=qh[par][:], in_=q_ap[h * 128:(h + 1) * 128, :]), inc=ld, inc_k=16)
        t2 = p.step("sync", [], lambda e, h=h, par=par: e.dma_start(
            out=kh[par][:], in_=k_ap[h * 128:(h + 1) * 128, :]), inc=ld, inc_k=16)
        t3 = p.step("sync", [], lambda e, h=h, par=par: e.dma_start(
            out=vt[par].rearrange("p (s k) -> p s k", k=128),
            in_=vt_ap[h].transpose([1, 0, 2])), inc=ld, inc_k=16)
        return t3

    hload = {0: ldhead(0, 0)}
    prev_head = {}
    for h in range(NHEAD):
        par = h % 2
        if h + 1 < NHEAD:
            hload[h + 1] = ldhead(h + 1, 1 - par)
        hl = [(ld, hload[h]), (gp, tri_done)]

        exp_done_all = []
        scale_done = {}
        for i in range(32):
            j = i % 4
            nb = i // 4 + 1
            chunks = []
            b0 = 0
            for cap, reg in ((3, sA), (2, sB), (2, sA), (1, sB)):
                if b0 >= nb:
                    break
                ln = min(cap, nb - b0)
                chunks.append((b0, ln, reg))
                b0 += ln
            off = 0
            lastexp = 0
            for cidx, (b0, ln, reg) in enumerate(chunks):
                def mkqk(e, i=i, b0=b0, ln=ln, reg=reg, par=par):
                    ins = None
                    for bi in range(ln):
                        ins = e.matmul(reg[:, bi * TT:(bi + 1) * TT],
                                       qh[par][:, i * 128:(i + 1) * 128],
                                       kh[par][:, (b0 + bi) * TT:(b0 + bi + 1) * TT],
                                       start=True, stop=True)
                    return ins
                w = list(hl)
                key = (reg is sA, "exp")
                if key in scale_done:
                    w.append((ac, scale_done[key]))
                qk = p.step("tensor", w, mkqk, inc=te)
                if b0 + ln == nb:
                    qk2 = p.step("vector", [(te, qk)],
                                 lambda e, ln=ln, reg=reg, j=j: e.tensor_add(
                                     reg[:, (ln - 1) * TT:ln * TT],
                                     reg[:, (ln - 1) * TT:ln * TT],
                                     tri[j][:]), inc=ve)
                else:
                    qk2 = None
                w = [(te, qk)] + ([(ve, qk2)] if qk2 else [])
                mc = p.step("vector", w, lambda e, ln=ln, reg=reg: e.reduce_max(
                    cols[:, 2:3], reg[:, 0:ln * TT], axis=AX.X), inc=ve)
                ms = p.step("vector", [(ve, mc)],
                            lambda e: e.tensor_scalar(cols[:, 3:4], cols[:, 2:3],
                                                      SCL, None, ALU.mult), inc=ve)
                if cidx == 0:
                    cp = p.step("vector", [(ve, ms)], lambda e: e.tensor_copy(
                        cols[:, 0:1], cols[:, 3:4]), inc=ve)
                    ng = p.step("vector", [(ve, cp)], lambda e: e.tensor_scalar(
                        cols[:, 1:2], cols[:, 0:1], -1.0, None, ALU.mult), inc=ve)
                else:
                    mn = p.step("vector", [(ve, ms)], lambda e: e.tensor_max(
                        cols[:, 3:4], cols[:, 3:4], cols[:, 0:1]), inc=ve)
                    ng = p.step("vector", [(ve, mn)], lambda e: e.tensor_scalar(
                        cols[:, 4:5], cols[:, 3:4], -1.0, None, ALU.mult), inc=ve)
                    al = p.step("scalar", [(ve, ng)], lambda e: e.activation(
                        cols[:, 5:6], cols[:, 0:1], AF.Exp,
                        bias=cols[:, 4:5]), inc=ac)
                    rs = p.step("vector", [(ac, al), (ve, lastexp)],
                                lambda e, j=j, off=off: e.tensor_scalar(
                                    prow[j][:, 0:off], prow[j][:, 0:off],
                                    cols[:, 5:6], None, ALU.mult), inc=ve)
                    rl2 = p.step("vector", [(ve, rs)],
                                 lambda e: e.tensor_mul(cols[:, 6:7],
                                                        cols[:, 6:7],
                                                        cols[:, 5:6]), inc=ve)
                    cm = p.step("vector", [(ve, rl2)], lambda e: e.tensor_copy(
                        cols[:, 0:1], cols[:, 3:4]), inc=ve)
                    ng = p.step("vector", [(ve, cm)], lambda e: e.tensor_copy(
                        cols[:, 1:2], cols[:, 4:5]), inc=ve)
                ex = p.step("scalar", [(ve, ng)],
                            lambda e, j=j, off=off, ln=ln, reg=reg: e.activation(
                                prow[j][:, off:off + ln * TT], reg[:, 0:ln * TT],
                                AF.Exp, scale=SCL, bias=cols[:, 1:2],
                                accum_out=cols[:, 7:8]), inc=ac)
                scale_done[(reg is sA, "exp")] = ex
                lastexp = 0
                if cidx == 0:
                    lc = p.step("vector", [(ac, ex)], lambda e: e.tensor_copy(
                        cols[:, 6:7], cols[:, 7:8]), inc=ve)
                else:
                    lc = p.step("vector", [(ac, ex)], lambda e: e.tensor_add(
                        cols[:, 6:7], cols[:, 6:7], cols[:, 7:8]), inc=ve)
                lastexp = lc
                off += ln * TT
            rl = p.step("vector", [(ve, lastexp)], lambda e: e.reciprocal(
                cols[:, 8:9], cols[:, 6:7]), inc=ve)
            L = nb * TT
            nz = p.step("vector", [(ve, rl)],
                        lambda e, j=j, L=L: e.tensor_scalar(
                            prow[j][:, 0:L], prow[j][:, 0:L], cols[:, 8:9],
                            None, ALU.mult), inc=ve)
            scale_done[(i, "norm")] = nz

            if j == 3:
                sup = i // 4
                nsi = 4 * sup + 4
                pvmm = 0
                evacs = {}
                for si in range(nsi):
                    spar = si % 2
                    jmin = max(0, si - 4 * sup)

                    def mktr(e, si=si, spar=spar, jmin=jmin):
                        ins = None
                        for jj in range(jmin, 4):
                            ins = e.transpose(
                                trp[spar][:, jj * 128:(jj + 1) * 128],
                                prow[jj][:, si * 128:(si + 1) * 128],
                                c.ident_bf[:])
                        return ins
                    w = [(ve, scale_done[(4 * sup + 3, "norm")])]
                    for jj in range(jmin, 4):
                        w.append((ve, scale_done[(4 * sup + jj, "norm")]))
                    if si >= 2:
                        w.append(evacs[si - 2])
                    tr = p.step("tensor", w, mktr, inc=te)
                    eng = "vector" if si % 2 == 0 else "scalar"
                    evsem = ve if eng == "vector" else ac
                    if eng == "vector":
                        evfn = (lambda e, spar=spar, jmin=jmin: e.tensor_copy(
                            ptr[spar][:, jmin * 128:TT],
                            trp[spar][:, jmin * 128:TT]))
                    else:
                        evfn = (lambda e, spar=spar, jmin=jmin: e.copy(
                            ptr[spar][:, jmin * 128:TT],
                            trp[spar][:, jmin * 128:TT]))
                    ev = p.step(eng, [(te, tr)], evfn, inc=evsem)
                    evacs[si] = (evsem, ev)

                    def mkpv(e, si=si, spar=spar, jmin=jmin, par=par, sup=sup,
                             first=(si == 0), last=(si == nsi - 1)):
                        return e.matmul(pv[:, jmin * 128:TT],
                                        vt[par][:, si * 128:(si + 1) * 128],
                                        ptr[spar][:, jmin * 128:TT],
                                        start=first, stop=last)
                    w = [evacs[si]]
                    if si == 0:
                        w.append((ac, scale_done.get((sup, "oevac"), 0)))
                    pvmm = p.step("tensor", w, mkpv, inc=te)
                opar = sup % 2
                ow = [(te, pvmm)]
                if (sup - 2, "odma") in scale_done:
                    ow.append((st, scale_done[(sup - 2, "odma")]))
                oe = p.step("scalar", ow,
                            lambda e, opar=opar: e.activation(
                                obuf[opar][:], pv[:], AF.Copy), inc=ac)
                scale_done[(sup + 1, "oevac")] = oe
                scale_done[(sup, "odma")] = p.step(
                    "sync", [(ac, oe)],
                    lambda e, h=h, sup=sup, opar=opar: e.dma_start(
                        out=out_ap[h * 128:(h + 1) * 128,
                                   sup * TT:(sup + 1) * TT],
                        in_=obuf[opar][:]), inc=st, inc_k=16)
        p.barrier()
    p.barrier()
    es.close()


def emit_l1ab(p, c, x_src, x_dst, xn_ap, q_ap, k_ap, vt_ap, ao_ap,
              cattn_ap, cproj_ap, cab_ap, cpb_ap):
    ld, ld2, ve, ac, te, st, gp = (p.s[k] for k in
                                   ("ld", "ld2", "ve", "ac", "te", "st", "gp"))
    esb = ExitStack()
    bias = sbuf(p, "ab_bias", [128, 64], F32, esb)
    emit_vec_cols(p, c, cab_ap, bias, 0, 48)
    emit_vec_cols(p, c, cpb_ap, bias, 48, 16)

    es1 = ExitStack()
    xnres = sbuf(p, "ab_xn", [128, NCH * T], BF16, es1)
    qb = [sbuf(p, f"ab_qb{i}", [128, TT], BF16, es1) for i in range(2)]
    vb = [sbuf(p, f"ab_vb{i}", [128, TT], BF16, es1) for i in range(2)]
    trv = [psum(p, f"ab_trv{i}", [128, TT], BF16, es1) for i in range(2)]
    xn_ready = emit_resident(p, xn_ap, xnres)

    qbfree = {}
    vdma = {}
    vevac = {}

    def epi(oc, t, par, mmt, ps):
        i = oc * NT + t
        w = [(te, mmt)]
        if (i - 2) in qbfree:
            w.append(qbfree[i - 2])
        a1 = p.step("scalar", w, lambda e, par=par, oc=oc: e.activation(
            qb[par][:], ps[:], AF.Identity, bias=bias[:, oc:oc + 1]), inc=ac)
        if oc < 32:
            dst = q_ap if oc < 16 else k_ap
            r0 = (oc % 16) * 128
            d1 = p.step("sync", [(ac, a1)],
                        lambda e, par=par, dst=dst, r0=r0, t=t: e.dma_start(
                            out=dst[r0:r0 + 128, t * TT:(t + 1) * TT],
                            in_=qb[par][:]), inc=st, inc_k=16)
            qbfree[i] = (st, d1)
            return [(ac, a1)]
        h = oc - 32

        def mktr(e, par=par):
            ins = None
            for j in range(4):
                ins = e.transpose(trv[par][:, j * 128:(j + 1) * 128],
                                  qb[par][:, j * 128:(j + 1) * 128],
                                  c.ident_bf[:])
            return ins
        w = [(ac, a1)]
        if (i - 2) in vevac:
            w.append((ve, vevac[i - 2]))
        tr = p.step("tensor", w, mktr, inc=te)
        qbfree[i] = (te, tr)
        w = [(te, tr)]
        if (i - 2) in vdma:
            w.append((st, vdma[i - 2]))
        ev = p.step("vector", w, lambda e, par=par: e.tensor_copy(
            vb[par][:], trv[par][:]), inc=ve)
        vevac[i] = ev
        vdma[i] = p.step("sync", [(ve, ev)],
                        lambda e, par=par, h=h, t=t: e.dma_start(
                            out=vt_ap[h, 4 * t:4 * t + 4].transpose([1, 0, 2]),
                            in_=vb[par].rearrange("p (s k) -> p s k", k=128)),
                        inc=st, inc_k=16)
        return [(ac, a1), (te, tr)]

    emit_projection(p, c, xnres, xn_ready, cattn_ap, 48, NCH, epi, name="abq")
    es1.close()

    emit_causal_attn_core(p, c, q_ap, k_ap, vt_ap, ao_ap)

    es2 = ExitStack()
    aores = sbuf(p, "ab_ao", [128, NCH * T], BF16, es2)
    xo = [sbuf(p, f"ab_xo{i}", [128, TT], F32, es2) for i in range(2)]
    xnew = [sbuf(p, f"ab_xw{i}", [128, TT], F32, es2) for i in range(2)]
    ao_ready = emit_resident(p, ao_ap, aores)
    xl = {}
    ep = {}
    sd = {}

    def epi2(oc, t, par, mmt, ps):
        i = oc * NT + t
        xl[i] = p.step("sync", [(ve, ep.get(i - 2, 0))],
                       lambda e, par=par, oc=oc, t=t: e.dma_start(
                           out=xo[par][:], in_=x_src[oc * 128:(oc + 1) * 128,
                                                     t * TT:(t + 1) * TT]),
                       inc=ld, inc_k=16)
        w = [(te, mmt), (ld, xl[i])]
        if (i - 2) in sd:
            w.append((st, sd[i - 2]))
        ep[i] = p.step("vector", w,
                       lambda e, par=par, oc=oc: e.scalar_tensor_tensor(
                           xnew[par][:], ps[:], bias[:, 48 + oc:49 + oc],
                           xo[par][:], ALU.add, ALU.add), inc=ve)
        sd[i] = p.step("scalar", [(ve, ep[i])],
                       lambda e, par=par, oc=oc, t=t: e.dma_start(
                           out=x_dst[oc * 128:(oc + 1) * 128,
                                     t * TT:(t + 1) * TT],
                           in_=xnew[par][:]), inc=st, inc_k=16)
        return [(ve, ep[i])]

    emit_projection(p, c, aores, ao_ready, cproj_ap, NCH, NCH, epi2, name="abo")
    es2.close()
    esb.close()


def prep_l1ab(inp):
    caw = np.asarray(inp["L1_attn_b_c_attn_w"])
    cpw = np.asarray(inp["L1_attn_b_c_proj_w"])
    cab = np.asarray(inp["L1_attn_b_c_attn_b"])
    cpb = np.asarray(inp["L1_attn_b_c_proj_b"])
    return {
        "cattn_t": tile_lhsT(caw),
        "cproj_t": tile_lhsT(cpw),
        "cab": np.ascontiguousarray(cab.reshape(48, 128)),
        "cpb": np.ascontiguousarray(cpb.reshape(16, 128)),
    }


def build_dev_l1ab():
    _uid[0] = 0
    nc = bass.Bass("TRN2", target_bir_lowering=False, debug=False)
    es = ExitStack()
    x_in = nc.dram_tensor("x", [D, T], F32, kind="ExternalInput").ap()
    cattn = nc.dram_tensor("cattn_t", [48, 128, 2048], F32, kind="ExternalInput").ap()
    cproj = nc.dram_tensor("cproj_t", [16, 128, 2048], F32, kind="ExternalInput").ap()
    cab = nc.dram_tensor("cab", [48, 128], F32, kind="ExternalInput").ap()
    cpb = nc.dram_tensor("cpb", [16, 128], F32, kind="ExternalInput").ap()
    x_out = nc.dram_tensor("x_out", [D, T], F32, kind="ExternalOutput").ap()
    XN = nc.dram_tensor("XN", [D, T], BF16, kind="Internal").ap()
    QF = nc.dram_tensor("QF", [D, T], BF16, kind="Internal").ap()
    KF = nc.dram_tensor("KF", [D, T], BF16, kind="Internal").ap()
    VT = nc.dram_tensor("VT", [NHEAD, 32, 128, 128], BF16, kind="Internal").ap()
    AO = nc.dram_tensor("AO", [D, T], BF16, kind="Internal").ap()
    CATB = nc.dram_tensor("CATB", [48, 128, 2048], BF16, kind="Internal").ap()
    CPRB = nc.dram_tensor("CPRB", [16, 128, 2048], BF16, kind="Internal").ap()

    p = Prog(nc, es)
    c = emit_consts(p)
    emit_convert(p, [(cattn, CATB), (cproj, CPRB)])
    emit_ln(p, c, x_in, XN)
    emit_l1ab(p, c, x_in, x_out, XN, QF, KF, VT, AO, CATB, CPRB, cab, cpb)
    p.step("sync", [(p.s["st"], p.s["st"].n)], lambda e: None)
    p.finalize()
    return nc, es


def emit_l0ab(p, c, x_src, x_dst, xn_ap, qm_ap, km_ap, vt_ap, ao_ap,
              wqkv_ap, wo_ap, bqkv_ap, bo_ap, wmq_ap, wmk_ap):
    ld, ld2, ve, ac, te, st, gp = (p.s[k] for k in
                                   ("ld", "ld2", "ve", "ac", "te", "st", "gp"))
    esb = ExitStack()
    bias = sbuf(p, "lb_bias", [128, 64], F32, esb)
    wm = sbuf(p, "lb_wm", [128, 128], BF16, esb)
    emit_vec_cols(p, c, bqkv_ap, bias, 0, 48)
    emit_vec_cols(p, c, bo_ap, bias, 48, 16)
    l1 = p.step("sync", [], lambda e: e.dma_start(out=wm[:, 0:64],
                                                  in_=wmq_ap[0]), inc=ld, inc_k=16)
    l2 = p.step("sync", [], lambda e: e.dma_start(out=wm[:, 64:128],
                                                  in_=wmk_ap[0]), inc=ld, inc_k=16)
    p.barrier()

    es1 = ExitStack()
    xnres = sbuf(p, "lb_xn", [128, NCH * T], BF16, es1)
    qb = [sbuf(p, f"lb_qb{i}", [128, TT], BF16, es1) for i in range(2)]
    el = [sbuf(p, f"lb_el{i}", [64, 3 * TT], F32, es1) for i in range(2)]
    qf = [sbuf(p, f"lb_qf{i}", [64, TT], BF16, es1) for i in range(2)]
    vb = [sbuf(p, f"lb_vb{i}", [128, TT], BF16, es1) for i in range(2)]
    trv = [psum(p, f"lb_trv{i}", [128, TT], BF16, es1) for i in range(2)]
    qmp = [psum(p, f"lb_qm{i}", [64, TT], F32, es1) for i in range(2)]
    xn_ready = emit_resident(p, xn_ap, xnres)

    qbfree, vdma, vevac, track = {}, {}, {}, {}

    def epi(oc, t, par, mmt, ps):
        i = oc * NT + t
        w = [(te, mmt)]
        if (i - 2) in qbfree:
            w.append(qbfree[i - 2])
        a1 = p.step("scalar", w, lambda e, par=par, oc=oc: e.activation(
            qb[par][:], ps[:], AF.Identity, bias=bias[:, oc:oc + 1]), inc=ac)
        if oc < 32:
            col0 = 0 if oc < 16 else 64
            w = [(ac, a1)]
            if (i - 2) in track:
                w.append(track[i - 2][0])
            m1 = p.step("tensor", w, lambda e, par=par, col0=col0: e.matmul(
                qmp[par][:], wm[:, col0:col0 + 64], qb[par][:],
                start=True, stop=True), inc=te)
            qbfree[i] = (te, m1)

            def mke1(e, par=par):
                e.tensor_scalar(el[par][:, 0:TT], qmp[par][:], 0.0, None,
                                ALU.min)
                return e.tensor_scalar(el[par][:, TT:2 * TT], qmp[par][:], 0.0,
                                       None, ALU.max)
            w = [(te, m1)]
            if (i - 2) in track:
                w.append(track[i - 2][1])
            e1 = p.step("vector", w, mke1, inc=ve)
            e2 = p.step("scalar", [(ve, e1)], lambda e, par=par: e.activation(
                el[par][:, 2 * TT:3 * TT], el[par][:, 0:TT], AF.Exp), inc=ac)
            w = [(ac, e2)]
            if (i - 2) in track:
                w.append(track[i - 2][2])
            e3 = p.step("vector", w, lambda e, par=par: e.tensor_add(
                qf[par][:], el[par][:, TT:2 * TT], el[par][:, 2 * TT:3 * TT]),
                inc=ve)
            dst = qm_ap if oc < 16 else km_ap
            r0 = (oc % 16) * 64
            d1 = p.step("sync", [(ve, e3)],
                        lambda e, par=par, dst=dst, r0=r0, t=t: e.dma_start(
                            out=dst[r0:r0 + 64, t * TT:(t + 1) * TT],
                            in_=qf[par][:]), inc=st, inc_k=16)
            track[i] = [(te, m1), (ve, e3), (st, d1)]
            return [(ac, a1)]
        h = oc - 32

        def mktr(e, par=par):
            ins = None
            for j in range(4):
                ins = e.transpose(trv[par][:, j * 128:(j + 1) * 128],
                                  qb[par][:, j * 128:(j + 1) * 128],
                                  c.ident_bf[:])
            return ins
        w = [(ac, a1)]
        if (i - 2) in vevac:
            w.append((ve, vevac[i - 2]))
        tr = p.step("tensor", w, mktr, inc=te)
        qbfree[i] = (te, tr)
        w = [(te, tr)]
        if (i - 2) in vdma:
            w.append((st, vdma[i - 2]))
        ev = p.step("vector", w, lambda e, par=par: e.tensor_copy(
            vb[par][:], trv[par][:]), inc=ve)
        vevac[i] = ev
        vdma[i] = p.step("sync", [(ve, ev)],
                        lambda e, par=par, h=h, t=t: e.dma_start(
                            out=vt_ap[h, 4 * t:4 * t + 4].transpose([1, 0, 2]),
                            in_=vb[par].rearrange("p (s k) -> p s k", k=128)),
                        inc=st, inc_k=16)
        return [(ac, a1)]

    emit_projection(p, c, xnres, xn_ready, wqkv_ap, 48, NCH, epi, name="lbq")
    es1.close()
    p.barrier()

    es2 = ExitStack()
    qm = [sbuf(p, f"lb_q{i}", [64, T], BF16, es2) for i in range(2)]
    km = [sbuf(p, f"lb_k{i}", [64, T], BF16, es2) for i in range(2)]
    vt = [sbuf(p, f"lb_v{i}", [128, T], BF16, es2) for i in range(2)]
    ktm = sbuf(p, "lb_ktm", [128, 32 * 64], BF16, es2)
    kvb = sbuf(p, "lb_kvb", [64, 128], BF16, es2)
    ksb = sbuf(p, "lb_ksb", [64, 1], BF16, es2)
    nr = sbuf(p, "lb_nr", [1, 2 * TT], F32, es2)
    of32 = [sbuf(p, f"lb_of{i}", [128, TT], F32, es2) for i in range(2)]
    ob = [sbuf(p, f"lb_ob{i}", [128, TT], BF16, es2) for i in range(2)]
    trk = [psum(p, f"lb_trk{i}", [128, 64], BF16, es2) for i in range(2)]
    kvp = psum(p, "lb_kvp", [64, 128], F32, es2)
    ksp = psum(p, "lb_ksp", [64, 1], F32, es2)
    outp = [psum(p, f"lb_out{i}", [128, TT], F32, es2) for i in range(2)]
    bcp = psum(p, "lb_bc", [128, TT], F32, es2)
    nrp = psum(p, "lb_nrp", [1, TT], F32, es2)

    def ldhead2(h, par):
        p.step("sync", [], lambda e, h=h, par=par: e.dma_start(
            out=qm[par][:], in_=qm_ap[h * 64:(h + 1) * 64, :]), inc=ld, inc_k=16)
        p.step("sync", [], lambda e, h=h, par=par: e.dma_start(
            out=km[par][:], in_=km_ap[h * 64:(h + 1) * 64, :]), inc=ld, inc_k=16)
        return p.step("sync", [], lambda e, h=h, par=par: e.dma_start(
            out=vt[par].rearrange("p (s k) -> p s k", k=128),
            in_=vt_ap[h].transpose([1, 0, 2])), inc=ld, inc_k=16)

    h2load = {0: ldhead2(0, 0)}
    for h in range(NHEAD):
        par = h % 2
        if h + 1 < NHEAD:
            h2load[h + 1] = ldhead2(h + 1, 1 - par)
        hl = [(ld, h2load[h])]
        tr_t, ev_t = {}, {}
        for si in range(32):
            spar = si % 2
            w = list(hl)
            if si >= 2:
                w.append((ve, ev_t[si - 2]))
            tr_t[si] = p.step("tensor", w, lambda e, si=si, spar=spar, par=par:
                              e.transpose(trk[spar][:],
                                          km[par][:, si * 128:(si + 1) * 128],
                                          c.ident_bf[0:64, 0:64]), inc=te)
            ev_t[si] = p.step("vector", [(te, tr_t[si])],
                              lambda e, si=si, spar=spar: e.tensor_copy(
                                  ktm[:, si * 64:(si + 1) * 64], trk[spar][:]),
                              inc=ve)

        def mkkv(e, par=par):
            ins = None
            for si in range(32):
                e.matmul(kvp[:], ktm[:, si * 64:(si + 1) * 64],
                         vt[par][:, si * 128:(si + 1) * 128],
                         start=(si == 0), stop=(si == 31))
                ins = e.matmul(ksp[:], ktm[:, si * 64:(si + 1) * 64],
                               c.ones_col_bf[:], start=(si == 0),
                               stop=(si == 31))
            return ins
        kv_mm = p.step("tensor", [(ve, ev_t[31])], mkkv, inc=te)
        kv_ev = p.step("vector", [(te, kv_mm)], lambda e: e.tensor_copy(
            kvb[:], kvp[:]), inc=ve)
        ks_ev = p.step("vector", [(ve, kv_ev)], lambda e: e.tensor_copy(
            ksb[:], ksp[:]), inc=ve)

        prev = {}
        prevn = {}
        prevb = {}
        for t in range(NT):
            tp = t % 2
            w = [(ve, ks_ev)]
            if (t - 2) in prev:
                w.append(prev[t - 2])
            o1 = p.step("tensor", w, lambda e, t=t, tp=tp, par=par: e.matmul(
                outp[tp][:], kvb[:], qm[par][:, t * TT:(t + 1) * TT],
                start=True, stop=True), inc=te)
            w = [(te, o1)]
            if (t - 1) in prevn:
                w.append(prevn[t - 1])
            n1 = p.step("tensor", w, lambda e, t=t, tp=tp, par=par: e.matmul(
                nrp[:], ksb[:], qm[par][:, t * TT:(t + 1) * TT],
                start=True, stop=True), inc=te)
            n2 = p.step("vector", [(te, n1)], lambda e, tp=tp: e.tensor_scalar(
                nr[:, tp * TT:(tp + 1) * TT], nrp[:], 1e-6, None, ALU.add),
                inc=ve)
            prevn[t] = (ve, n2)
            n3 = p.step("vector", [(ve, n2)], lambda e, tp=tp: e.reciprocal(
                nr[:, tp * TT:(tp + 1) * TT], nr[:, tp * TT:(tp + 1) * TT]),
                inc=ve)
            w = [(ve, n3)]
            if (t - 1) in prevb:
                w.append(prevb[t - 1])
            b1 = p.step("tensor", w, lambda e, tp=tp: e.matmul(
                bcp[:], c.ones_row_f[:], nr[:, tp * TT:(tp + 1) * TT],
                start=True, stop=True), inc=te)
            c1 = p.step("scalar", [(te, o1)], lambda e, tp=tp: e.copy(
                of32[tp][:], outp[tp][:]), inc=ac)
            m1 = p.step("vector", [(ac, c1), (te, b1)],
                        lambda e, tp=tp: e.tensor_mul(ob[tp][:], of32[tp][:],
                                                      bcp[:]), inc=ve)
            prevb[t] = (ve, m1)
            d1 = p.step("sync", [(ve, m1)],
                        lambda e, h=h, t=t, tp=tp: e.dma_start(
                            out=ao_ap[h * 128:(h + 1) * 128,
                                      t * TT:(t + 1) * TT],
                            in_=ob[tp][:]), inc=st, inc_k=16)
            prev[t] = (st, d1)
        p.barrier()
    es2.close()

    es3 = ExitStack()
    aores = sbuf(p, "lb_ao", [128, NCH * T], BF16, es3)
    xo = [sbuf(p, f"lb_xo{i}", [128, TT], F32, es3) for i in range(2)]
    xnew = [sbuf(p, f"lb_xw{i}", [128, TT], F32, es3) for i in range(2)]
    ao_ready = emit_resident(p, ao_ap, aores)
    xl, ep, sd = {}, {}, {}

    def epi2(oc, t, par, mmt, ps):
        i = oc * NT + t
        xl[i] = p.step("sync", [(ve, ep.get(i - 2, 0))],
                       lambda e, par=par, oc=oc, t=t: e.dma_start(
                           out=xo[par][:], in_=x_src[oc * 128:(oc + 1) * 128,
                                                     t * TT:(t + 1) * TT]),
                       inc=ld, inc_k=16)
        w = [(te, mmt), (ld, xl[i])]
        if (i - 2) in sd:
            w.append((st, sd[i - 2]))
        ep[i] = p.step("vector", w,
                       lambda e, par=par, oc=oc: e.scalar_tensor_tensor(
                           xnew[par][:], ps[:], bias[:, 48 + oc:49 + oc],
                           xo[par][:], ALU.add, ALU.add), inc=ve)
        sd[i] = p.step("scalar", [(ve, ep[i])],
                       lambda e, par=par, oc=oc, t=t: e.dma_start(
                           out=x_dst[oc * 128:(oc + 1) * 128,
                                     t * TT:(t + 1) * TT],
                           in_=xnew[par][:]), inc=st, inc_k=16)
        return [(ve, ep[i])]

    emit_projection(p, c, aores, ao_ready, wo_ap, NCH, NCH, epi2, name="lbo")
    es3.close()
    esb.close()

def prep_l0ab(inp):
    wq = np.asarray(inp["L0_attn_b_wq"])
    wk = np.asarray(inp["L0_attn_b_wk"])
    wv = np.asarray(inp["L0_attn_b_wv"])
    wo = np.asarray(inp["L0_attn_b_wo"])
    wqkv = np.concatenate([wq.T, wk.T, wv.T], axis=1)
    bqkv = np.concatenate([np.asarray(inp["L0_attn_b_bq"]),
                           np.asarray(inp["L0_attn_b_bk"]),
                           np.asarray(inp["L0_attn_b_bv"])])
    return {
        "wqkv_t": tile_lhsT(np.ascontiguousarray(wqkv)),
        "wo_t": tile_lhsT(np.ascontiguousarray(wo.T)),
        "bqkv": np.ascontiguousarray(bqkv.reshape(48, 128)),
        "bo": np.ascontiguousarray(np.asarray(inp["L0_attn_b_bo"]).reshape(16, 128)),
        "wmq_t": np.ascontiguousarray(
            np.asarray(inp["L0_attn_b_wmap_q"]).T.reshape(1, 128, 64)),
        "wmk_t": np.ascontiguousarray(
            np.asarray(inp["L0_attn_b_wmap_k"]).T.reshape(1, 128, 64)),
    }


def build_dev_l0ab():
    _uid[0] = 0
    nc = bass.Bass("TRN2", target_bir_lowering=False, debug=False)
    es = ExitStack()
    x_in = nc.dram_tensor("x", [D, T], F32, kind="ExternalInput").ap()
    wqkv = nc.dram_tensor("wqkv_t", [48, 128, 2048], F32, kind="ExternalInput").ap()
    wo = nc.dram_tensor("wo_t", [16, 128, 2048], F32, kind="ExternalInput").ap()
    bqkv = nc.dram_tensor("bqkv", [48, 128], F32, kind="ExternalInput").ap()
    bo = nc.dram_tensor("bo", [16, 128], F32, kind="ExternalInput").ap()
    wmq = nc.dram_tensor("wmq_t", [1, 128, 64], F32, kind="ExternalInput").ap()
    wmk = nc.dram_tensor("wmk_t", [1, 128, 64], F32, kind="ExternalInput").ap()
    x_out = nc.dram_tensor("x_out", [D, T], F32, kind="ExternalOutput").ap()
    XN = nc.dram_tensor("XN", [D, T], BF16, kind="Internal").ap()
    QM = nc.dram_tensor("QM", [1024, T], BF16, kind="Internal").ap()
    KM = nc.dram_tensor("KM", [1024, T], BF16, kind="Internal").ap()
    VT = nc.dram_tensor("VT", [NHEAD, 32, 128, 128], BF16, kind="Internal").ap()
    AO = nc.dram_tensor("AO", [D, T], BF16, kind="Internal").ap()
    WQKVB = nc.dram_tensor("WQKVB", [48, 128, 2048], BF16, kind="Internal").ap()
    WOB = nc.dram_tensor("WOB", [16, 128, 2048], BF16, kind="Internal").ap()
    WMQB = nc.dram_tensor("WMQB", [1, 128, 64], BF16, kind="Internal").ap()
    WMKB = nc.dram_tensor("WMKB", [1, 128, 64], BF16, kind="Internal").ap()

    p = Prog(nc, es)
    c = emit_consts(p)
    emit_convert(p, [(wqkv, WQKVB), (wo, WOB), (wmq, WMQB), (wmk, WMKB)])
    emit_ln(p, c, x_in, XN)
    emit_l0ab(p, c, x_in, x_out, XN, QM, KM, VT, AO, WQKVB, WOB, bqkv, bo,
              WMQB, WMKB)
    p.step("sync", [(p.s["st"], p.s["st"].n)], lambda e: None)
    p.finalize()
    return nc, es


LN1E4 = float(np.log(10000.0))
C1_2PI = 6.28125
C2_2PI = float(2 * np.pi - 6.28125)


def emit_rope_tables(p, c, es):
    c.cosF = sbuf(p, "rp_cos", [128, T], F32, es)
    c.sinF = sbuf(p, "rp_sin", [128, T], F32, es)
    c.perm = sbuf(p, "rp_perm", [128, 128], BF16, es)
    es_scratch = ExitStack()
    ang = sbuf(p, "rp_ang", [128, T], F32, es_scratch)
    wrk = sbuf(p, "rp_wrk", [128, T], F32, es_scratch)
    wki = sbuf(p, "rp_wki", [128, T], mybir.dt.int32, es_scratch)
    colj = sbuf(p, "rp_col", [128, 4], F32, es_scratch)
    gp, ve, ac, te = (p.s[k] for k in ("gp", "ve", "ac", "te"))

    def mkgp(e):
        e.memset(c.perm[:], 0.0)
        e.affine_select(out=c.perm[:], in_=c.perm[:], compare_op=ALU.not_equal,
                        fill=-1.0, base=-64, pattern=[[-1, 128]],
                        channel_multiplier=1)
        e.affine_select(out=c.perm[:], in_=c.perm[:], compare_op=ALU.not_equal,
                        fill=1.0, base=64, pattern=[[-1, 128]],
                        channel_multiplier=1)
        e.iota(colj[:, 0:1], pattern=[[0, 1]], base=0, channel_multiplier=1,
               allow_small_or_imprecise_dtypes=True)
        return e.iota(ang[:], pattern=[[1, T]], base=0, channel_multiplier=0,
                      allow_small_or_imprecise_dtypes=True)
    g1 = p.step("gpsimd", [], mkgp, inc=gp)

    j1 = p.step("vector", [(gp, g1)], lambda e: e.tensor_scalar(
        colj[:, 1:2], colj[:, 0:1], 64.0, None, ALU.is_ge), inc=ve)
    j2 = p.step("vector", [(ve, j1)], lambda e: e.scalar_tensor_tensor(
        colj[:, 1:2], colj[:, 1:2], -64.0, colj[:, 0:1], ALU.mult, ALU.add),
        inc=ve)
    th = p.step("scalar", [(ve, j2)], lambda e: e.activation(
        colj[:, 2:3], colj[:, 1:2], AF.Exp, scale=-LN1E4 / 64.0), inc=ac)
    a1 = p.step("vector", [(ac, th)], lambda e: e.tensor_scalar(
        ang[:], ang[:], colj[:, 2:3], None, ALU.mult), inc=ve)

    def reduce_and_sin(dst, shift, dep):
        s1 = p.step("vector", [dep], lambda e, shift=shift: e.tensor_scalar(
            wrk[:], ang[:], 1.0 / (2 * np.pi), shift, ALU.mult, ALU.add),
            inc=ve)
        s2 = p.step("vector", [(ve, s1)], lambda e: e.tensor_copy(
            wki[:], wrk[:]), inc=ve)
        s3 = p.step("vector", [(ve, s2)], lambda e: e.tensor_copy(
            wrk[:], wki[:]), inc=ve)
        sh2 = shift * 2 * np.pi
        s4 = p.step("vector", [(ve, s3)], lambda e, sh2=sh2: e.scalar_tensor_tensor(
            dst[:], wrk[:], -C1_2PI, ang[:], ALU.mult, ALU.add), inc=ve)
        s5 = p.step("vector", [(ve, s4)], lambda e, sh2=sh2: e.scalar_tensor_tensor(
            dst[:], wrk[:], -C2_2PI, dst[:], ALU.mult, ALU.add), inc=ve)
        if shift != 0.0:
            s5 = p.step("vector", [(ve, s5)], lambda e, sh2=sh2: e.tensor_scalar(
                dst[:], dst[:], np.pi / 2, None, ALU.add), inc=ve)
        return p.step("scalar", [(ve, s5)], lambda e: e.activation(
            dst[:], dst[:], AF.Sin), inc=ac)
    d1 = reduce_and_sin(c.sinF, 0.0, (ve, a1))
    d2 = reduce_and_sin(c.cosF, 0.25, (ac, d1))
    p.barrier()
    es_scratch.close()


def rope_epi_factory(p, c, q_ap, k_ap, vt_ap, bufs):
    ld, ve, ac, te, st = (p.s[k] for k in ("ld", "ve", "ac", "te", "st"))
    qb, rb, tmpa, qr, vb = bufs["qb"], bufs["rb"], bufs["tmpa"], bufs["qr"], bufs["vb"]
    trv, trr = bufs["trv"], bufs["trr"]
    qbfree, vdma, vevac, rfree, qrfree = {}, {}, {}, {}, {}

    def epi(oc, t, par, mmt, ps):
        i = oc * NT + t
        w = [(te, mmt)]
        if (i - 2) in qbfree:
            w.append(qbfree[i - 2])
        a1 = p.step("scalar", w, lambda e, par=par: e.activation(
            qb[par][:], ps[:], AF.Copy), inc=ac)
        if oc < 32:
            w = [(ac, a1)]
            if (i - 2) in rfree:
                w.append(rfree[i - 2])
            r1 = p.step("tensor", w, lambda e, par=par: e.matmul(
                trr[par][:], c.perm[:], qb[par][:], start=True, stop=True),
                inc=te)
            r2 = p.step("vector", [(te, r1)], lambda e, par=par: e.tensor_copy(
                rb[par][:], trr[par][:]), inc=ve)
            rfree[i] = (ve, r2)

            def mkrope(e, par=par, t=t):
                sl = slice(t * TT, (t + 1) * TT)
                e.tensor_mul(tmpa[par][:], qb[par][:], c.cosF[:, sl])
                e.tensor_mul(rb[par][:], rb[par][:], c.sinF[:, sl])
                return e.tensor_add(qr[par][:], tmpa[par][:], rb[par][:])
            w = [(ve, r2)]
            if (i - 2) in qrfree:
                w.append(qrfree[i - 2])
            r3 = p.step("vector", w, mkrope, inc=ve)
            qbfree[i] = (ve, r3)
            dst = q_ap if oc < 16 else k_ap
            r0 = (oc % 16) * 128
            d1 = p.step("sync", [(ve, r3)],
                        lambda e, par=par, dst=dst, r0=r0, t=t: e.dma_start(
                            out=dst[r0:r0 + 128, t * TT:(t + 1) * TT],
                            in_=qr[par][:]), inc=st, inc_k=16)
            qrfree[i] = (st, d1)
            return [(ac, a1)]
        h = oc - 32

        def mktr(e, par=par):
            ins = None
            for j in range(4):
                ins = e.transpose(trv[par][:, j * 128:(j + 1) * 128],
                                  qb[par][:, j * 128:(j + 1) * 128],
                                  c.ident_bf[:])
            return ins
        w = [(ac, a1)]
        if (i - 2) in vevac:
            w.append((ve, vevac[i - 2]))
        tr = p.step("tensor", w, mktr, inc=te)
        qbfree[i] = (te, tr)
        w = [(te, tr)]
        if (i - 2) in vdma:
            w.append((st, vdma[i - 2]))
        ev = p.step("vector", w, lambda e, par=par: e.tensor_copy(
            vb[par][:], trv[par][:]), inc=ve)
        vevac[i] = ev
        vdma[i] = p.step("sync", [(ve, ev)],
                        lambda e, par=par, h=h, t=t: e.dma_start(
                            out=vt_ap[h, 4 * t:4 * t + 4].transpose([1, 0, 2]),
                            in_=vb[par].rearrange("p (s k) -> p s k", k=128)),
                        inc=st, inc_k=16)
        return [(ac, a1)]
    return epi


def emit_poly_attn_core(p, c, q_ap, k_ap, vt_ap, out_ap):
    es = ExitStack()
    qh = [sbuf(p, f"pa_q{i}", [128, T], BF16, es) for i in range(2)]
    kh = [sbuf(p, f"pa_k{i}", [128, T], BF16, es) for i in range(2)]
    vt = [sbuf(p, f"pa_v{i}", [128, T], BF16, es) for i in range(2)]
    prow = [sbuf(p, f"pa_p{j}", [128, T], BF16, es) for j in range(4)]
    s2 = [sbuf(p, f"pa_s2{i}", [128, TT], F32, es) for i in range(2)]
    ptr = [sbuf(p, f"pa_t{i}", [128, TT], BF16, es) for i in range(2)]
    obuf = [sbuf(p, f"pa_o{i}", [128, TT], BF16, es) for i in range(2)]
    cols = sbuf(p, "pa_c", [128, 16], F32, es)
    tri01 = [sbuf(p, f"pa_tri{j}", [128, TT], F32, es) for j in range(4)]
    sp = [psum(p, f"pa_s{i}", [128, TT], F32, es) for i in range(2)]
    pv = psum(p, "pa_pv", [128, TT], F32, es)
    trp = [psum(p, f"pa_tr{i}", [128, TT], BF16, es) for i in range(2)]
    ld, ve, ac, te, gp, st = (p.s[k] for k in ("ld", "ve", "ac", "te", "gp", "st"))

    def mk_tri(e):
        ins = None
        for j in range(4):
            e.memset(tri01[j][:], 1.0)
            ins = e.affine_select(
                out=tri01[j][:], in_=tri01[j][:], compare_op=ALU.is_ge,
                fill=0.0, base=j * 128, pattern=[[-1, TT]],
                channel_multiplier=1)
        return ins
    tri_done = p.step("gpsimd", [], mk_tri, inc=gp)

    def ldhead(h, par):
        p.step("sync", [], lambda e, h=h, par=par: e.dma_start(
            out=qh[par][:], in_=q_ap[h * 128:(h + 1) * 128, :]), inc=ld, inc_k=16)
        p.step("sync", [], lambda e, h=h, par=par: e.dma_start(
            out=kh[par][:], in_=k_ap[h * 128:(h + 1) * 128, :]), inc=ld, inc_k=16)
        return p.step("sync", [], lambda e, h=h, par=par: e.dma_start(
            out=vt[par].rearrange("p (s k) -> p s k", k=128),
            in_=vt_ap[h].transpose([1, 0, 2])), inc=ld, inc_k=16)

    hload = {0: ldhead(0, 0)}
    for h in range(NHEAD):
        par = h % 2
        if h + 1 < NHEAD:
            hload[h + 1] = ldhead(h + 1, 1 - par)
        hl = [(ld, hload[h]), (gp, tri_done)]
        sd = {}
        sq1_t, sq2_t = {}, {}
        sp_last = {0: 0, 1: 0}
        dn_prev = 0
        for i in range(32):
            j = i % 4
            nb = i // 4 + 1
            for b in range(nb):
                bpar = b % 2
                w = list(hl)
                if sp_last[bpar]:
                    w.append((ac, sp_last[bpar]))
                qk = p.step("tensor", w, lambda e, i=i, b=b, bpar=bpar, par=par:
                            e.matmul(sp[bpar][:],
                                     qh[par][:, i * 128:(i + 1) * 128],
                                     kh[par][:, b * TT:(b + 1) * TT],
                                     start=True, stop=True), inc=te)
                w = [(te, qk)]
                sq1 = p.step("scalar", w, lambda e, bpar=bpar: e.activation(
                    s2[bpar][:], sp[bpar][:], AF.Square), inc=ac)
                sq1_t[(i, b)] = sq1
                sp_last[bpar] = sq1
                dep = (ac, sq1)
                if b == nb - 1:
                    dm = p.step("vector", [dep], lambda e, bpar=bpar, j=j:
                                e.tensor_mul(s2[bpar][:], s2[bpar][:],
                                             tri01[j][:]), inc=ve)
                    dep = (ve, dm)
                w = [dep, (ve, dn_prev)]
                sq2_t[(i, b)] = p.step(
                    "scalar", w, lambda e, i=i, j=j, b=b, bpar=bpar:
                    e.activation(prow[j][:, b * TT:(b + 1) * TT], s2[bpar][:],
                                 AF.Square, accum_out=cols[:, b:b + 1]),
                    inc=ac)
            d1 = p.step("vector", [(ac, sq2_t[(i, nb - 1)])],
                        lambda e, nb=nb: e.tensor_reduce(
                            cols[:, 8:9], cols[:, 0:nb], AX.X, ALU.add), inc=ve)
            d2 = p.step("vector", [(ve, d1)], lambda e: e.tensor_scalar(
                cols[:, 8:9], cols[:, 8:9], 1e-6, None, ALU.max), inc=ve)
            d3 = p.step("vector", [(ve, d2)], lambda e: e.reciprocal(
                cols[:, 9:10], cols[:, 8:9]), inc=ve)
            L = nb * TT
            nz = p.step("vector", [(ve, d3)], lambda e, j=j, L=L: e.tensor_scalar(
                prow[j][:, 0:L], prow[j][:, 0:L], cols[:, 9:10], None,
                ALU.mult), inc=ve)
            dn_prev = nz
            sd[(i, "norm")] = nz

            if j == 3:
                sup = i // 4
                nsi = 4 * sup + 4
                pvmm = 0
                evacs = {}
                for si in range(nsi):
                    spar = si % 2
                    jmin = max(0, si - 4 * sup)

                    def mktr(e, si=si, spar=spar, jmin=jmin):
                        ins = None
                        for jj in range(jmin, 4):
                            ins = e.transpose(
                                trp[spar][:, jj * 128:(jj + 1) * 128],
                                prow[jj][:, si * 128:(si + 1) * 128],
                                c.ident_bf[:])
                        return ins
                    w = [(ve, sd[(4 * sup + 3, "norm")])]
                    if si >= 2:
                        w.append(evacs[si - 2])
                    tr = p.step("tensor", w, mktr, inc=te)
                    eng = "vector" if si % 2 == 0 else "scalar"
                    evsem = ve if eng == "vector" else ac
                    if eng == "vector":
                        evfn = (lambda e, spar=spar, jmin=jmin: e.tensor_copy(
                            ptr[spar][:, jmin * 128:TT],
                            trp[spar][:, jmin * 128:TT]))
                    else:
                        evfn = (lambda e, spar=spar, jmin=jmin: e.copy(
                            ptr[spar][:, jmin * 128:TT],
                            trp[spar][:, jmin * 128:TT]))
                    ev = p.step(eng, [(te, tr)], evfn, inc=evsem)
                    evacs[si] = (evsem, ev)

                    def mkpv(e, si=si, spar=spar, jmin=jmin, par=par,
                             first=(si == 0), last=(si == nsi - 1)):
                        return e.matmul(pv[:, jmin * 128:TT],
                                        vt[par][:, si * 128:(si + 1) * 128],
                                        ptr[spar][:, jmin * 128:TT],
                                        start=first, stop=last)
                    w = [evacs[si]]
                    if si == 0:
                        w.append((ac, sd.get((sup, "oevac"), 0)))
                    pvmm = p.step("tensor", w, mkpv, inc=te)
                opar = sup % 2
                ow = [(te, pvmm)]
                if (sup - 2, "odma") in sd:
                    ow.append((st, sd[(sup - 2, "odma")]))
                oe = p.step("scalar", ow, lambda e, opar=opar: e.activation(
                    obuf[opar][:], pv[:], AF.Copy), inc=ac)
                sd[(sup + 1, "oevac")] = oe
                sd[(sup, "odma")] = p.step(
                    "sync", [(ac, oe)],
                    lambda e, h=h, sup=sup, opar=opar: e.dma_start(
                        out=out_ap[h * 128:(h + 1) * 128,
                                   sup * TT:(sup + 1) * TT],
                        in_=obuf[opar][:]), inc=st, inc_k=16)
        p.barrier()
    p.barrier()
    es.close()


def i0nb(i):
    return (i - 1) // 4 + 1


def emit_l0aa(p, c, x_src, x_dst, xn_ap, q_ap, k_ap, vt_ap, ao_ap,
              wqkv_ap, wo_ap):
    ld, ld2, ve, ac, te, st = (p.s[k] for k in ("ld", "ld2", "ve", "ac", "te", "st"))
    es1 = ExitStack()
    emit_rope_tables(p, c, es1)
    xnres = sbuf(p, "aa_xn", [128, NCH * T], BF16, es1)
    bufs = {
        "qb": [sbuf(p, f"aa_qb{i}", [128, TT], BF16, es1) for i in range(2)],
        "rb": [sbuf(p, f"aa_rb{i}", [128, TT], F32, es1) for i in range(2)],
        "tmpa": [sbuf(p, f"aa_ta{i}", [128, TT], F32, es1) for i in range(2)],
        "qr": [sbuf(p, f"aa_qr{i}", [128, TT], BF16, es1) for i in range(2)],
        "vb": [sbuf(p, f"aa_vb{i}", [128, TT], BF16, es1) for i in range(2)],
        "trv": [psum(p, f"aa_trv{i}", [128, TT], BF16, es1) for i in range(2)],
        "trr": [psum(p, f"aa_trr{i}", [128, TT], F32, es1) for i in range(2)],
    }
    xn_ready = emit_resident(p, xn_ap, xnres)
    epi = rope_epi_factory(p, c, q_ap, k_ap, vt_ap, bufs)
    emit_projection(p, c, xnres, xn_ready, wqkv_ap, 48, NCH, epi, name="aaq")
    es1.close()
    p.barrier()

    emit_poly_attn_core(p, c, q_ap, k_ap, vt_ap, ao_ap)

    es3 = ExitStack()
    aores = sbuf(p, "aa_ao", [128, NCH * T], BF16, es3)
    xo = [sbuf(p, f"aa_xo{i}", [128, TT], F32, es3) for i in range(2)]
    xnew = [sbuf(p, f"aa_xw{i}", [128, TT], F32, es3) for i in range(2)]
    ao_ready = emit_resident(p, ao_ap, aores)
    xl, ep, sd2 = {}, {}, {}

    def epi2(oc, t, par, mmt, ps):
        i = oc * NT + t
        xl[i] = p.step("sync", [(ve, ep.get(i - 2, 0))],
                       lambda e, par=par, oc=oc, t=t: e.dma_start(
                           out=xo[par][:], in_=x_src[oc * 128:(oc + 1) * 128,
                                                     t * TT:(t + 1) * TT]),
                       inc=ld, inc_k=16)
        w = [(te, mmt), (ld, xl[i])]
        if (i - 2) in sd2:
            w.append((st, sd2[i - 2]))
        ep[i] = p.step("vector", w, lambda e, par=par: e.tensor_add(
            xnew[par][:], xo[par][:], ps[:]), inc=ve)
        sd2[i] = p.step("scalar", [(ve, ep[i])],
                        lambda e, par=par, oc=oc, t=t: e.dma_start(
                            out=x_dst[oc * 128:(oc + 1) * 128,
                                      t * TT:(t + 1) * TT],
                            in_=xnew[par][:]), inc=st, inc_k=16)
        return [(ve, ep[i])]

    emit_projection(p, c, aores, ao_ready, wo_ap, NCH, NCH, epi2, name="aao")
    es3.close()


def prep_l0aa(inp):
    wq = np.asarray(inp["L0_attn_a_Wq"])
    wk = np.asarray(inp["L0_attn_a_Wk"])
    wv = np.asarray(inp["L0_attn_a_Wv"])
    wo = np.asarray(inp["L0_attn_a_Wo"])
    wqkv = np.concatenate([wq.T, wk.T, wv.T], axis=1)
    return {
        "awqkv_t": tile_lhsT(np.ascontiguousarray(wqkv)),
        "awo_t": tile_lhsT(np.ascontiguousarray(wo.T)),
    }


def build_dev_l0aa(dbg=False):
    _uid[0] = 0
    nc = bass.Bass("TRN2", target_bir_lowering=False, debug=False)
    es = ExitStack()
    kind = "ExternalOutput" if dbg else "Internal"
    x_in = nc.dram_tensor("x", [D, T], F32, kind="ExternalInput").ap()
    wqkv = nc.dram_tensor("awqkv_t", [48, 128, 2048], F32, kind="ExternalInput").ap()
    wo = nc.dram_tensor("awo_t", [16, 128, 2048], F32, kind="ExternalInput").ap()
    x_out = nc.dram_tensor("x_out", [D, T], F32, kind="ExternalOutput").ap()
    XN = nc.dram_tensor("XN", [D, T], BF16, kind=kind).ap()
    QF = nc.dram_tensor("QF", [D, T], BF16, kind=kind).ap()
    KF = nc.dram_tensor("KF", [D, T], BF16, kind=kind).ap()
    VT = nc.dram_tensor("VT", [NHEAD, 32, 128, 128], BF16, kind=kind).ap()
    AO = nc.dram_tensor("AO", [D, T], BF16, kind=kind).ap()
    WQKVB = nc.dram_tensor("WQKVB", [48, 128, 2048], BF16, kind="Internal").ap()
    WOB = nc.dram_tensor("WOB", [16, 128, 2048], BF16, kind="Internal").ap()

    p = Prog(nc, es)
    c = emit_consts(p)
    emit_convert(p, [(wqkv, WQKVB), (wo, WOB)])
    emit_ln(p, c, x_in, XN)
    emit_l0aa(p, c, x_in, x_out, XN, QF, KF, VT, AO, WQKVB, WOB)
    p.step("sync", [(p.s["st"], p.s["st"].n)], lambda e: None)
    p.finalize()
    return nc, es


def emit_sparse_attn_core(p, c, q_ap, k_ap, vt_ap, out_ap):
    es = ExitStack()
    qh = [sbuf(p, f"sa_q{i}", [128, T], BF16, es) for i in range(2)]
    kh = [sbuf(p, f"sa_k{i}", [128, T], BF16, es) for i in range(2)]
    vt = [sbuf(p, f"sa_v{i}", [128, T], BF16, es) for i in range(2)]
    prow = [sbuf(p, f"sa_p{j}", [128, T], BF16, es) for j in range(4)]
    ptr = [sbuf(p, f"sa_t{i}", [128, TT], BF16, es) for i in range(2)]
    obuf = [sbuf(p, f"sa_o{i}", [128, TT], BF16, es) for i in range(2)]
    cols = sbuf(p, "sa_c", [128, 16], F32, es)
    sA = psum(p, "sa_sA", [128, 3 * TT], F32, es)
    sB = psum(p, "sa_sB", [128, 2 * TT], F32, es)
    pv = psum(p, "sa_pv", [128, TT], F32, es)
    trp = [psum(p, f"sa_tr{i}", [128, TT], BF16, es) for i in range(2)]
    ld, ve, ac, te, gp, st = (p.s[k] for k in ("ld", "ve", "ac", "te", "gp", "st"))

    def ldhead(h, par):
        p.step("sync", [], lambda e, h=h, par=par: e.dma_start(
            out=qh[par][:], in_=q_ap[h * 128:(h + 1) * 128, :]), inc=ld, inc_k=16)
        p.step("sync", [], lambda e, h=h, par=par: e.dma_start(
            out=kh[par][:], in_=k_ap[h * 128:(h + 1) * 128, :]), inc=ld, inc_k=16)
        return p.step("sync", [], lambda e, h=h, par=par: e.dma_start(
            out=vt[par].rearrange("p (s k) -> p s k", k=128),
            in_=vt_ap[h].transpose([1, 0, 2])), inc=ld, inc_k=16)

    def emit_mask(i, j, nb, dep, dep2):
        tq0 = i * 128
        nb64 = nb * 8
        row = prow[j]
        tgt = None

        def mk1(e, nb64=nb64, tq0=tq0, row=row):
            v = row[:, 0:nb64 * 64].rearrange("p (b k) -> p b k", k=64)
            return e.affine_select(out=v, in_=v, compare_op=ALU.is_ge,
                                   fill=0.0, base=tq0,
                                   pattern=[[-64, nb64], [0, 64]],
                                   channel_multiplier=1)
        tgt = p.step("gpsimd", [dep, dep2], mk1, inc=gp)

        Bend = min(max(2, 2 * i - 16), nb64)
        sing = []
        if Bend > 2:
            for kb in (2, 3):
                if kb < Bend:
                    sing.append(kb)
            a0 = 4
            aend = (Bend // 4) * 4
            if aend > a0:
                def mk2(e, a0=a0, aend=aend, row=row):
                    na = (aend - a0) // 4
                    v = row[:, a0 * 64:aend * 64].rearrange(
                        "p (a b k) -> p a b k", b=4, k=64)
                    return e.affine_select(out=v, in_=v, compare_op=ALU.is_ge,
                                           fill=0.0, base=0,
                                           pattern=[[0, na], [-1, 4], [0, 64]],
                                           channel_multiplier=0)
                tgt = p.step("gpsimd", [(gp, tgt)], mk2, inc=gp)
            for kb in range(aend, Bend):
                if kb % 4 != 0:
                    sing.append(kb)
        bkb = 2 * i - 16
        bsel = (bkb >= 2 and bkb % 4 != 0 and bkb < nb64)
        if sing or bsel:
            def mk3(e, sing=tuple(sing), bsel=bsel, bkb=bkb, row=row):
                ins = None
                for kb in sing:
                    ins = e.memset(row[:, kb * 64:(kb + 1) * 64], 0.0)
                if bsel:
                    v = row[:, bkb * 64:(bkb + 1) * 64]
                    ins = e.affine_select(out=v, in_=v, compare_op=ALU.is_ge,
                                          fill=0.0, base=63,
                                          pattern=[[0, 64]],
                                          channel_multiplier=-1)
                return ins
            tgt = p.step("gpsimd", [(gp, tgt)], mk3, inc=gp)
        return tgt

    hload = {0: ldhead(0, 0)}
    for h in range(NHEAD):
        par = h % 2
        if h + 1 < NHEAD:
            hload[h + 1] = ldhead(h + 1, 1 - par)
        hl = [(ld, hload[h])]

        scale_done = {}
        for i in range(32):
            j = i % 4
            nb = i // 4 + 1
            chunks = []
            b0 = 0
            for cap, reg in ((3, sA), (2, sB), (2, sA), (1, sB)):
                if b0 >= nb:
                    break
                ln = min(cap, nb - b0)
                chunks.append((b0, ln, reg))
                b0 += ln
            off = 0
            lastexp = None
            for cidx, (b0, ln, reg) in enumerate(chunks):
                def mkqk(e, i=i, b0=b0, ln=ln, reg=reg, par=par):
                    ins = None
                    for bi in range(ln):
                        ins = e.matmul(reg[:, bi * TT:(bi + 1) * TT],
                                       qh[par][:, i * 128:(i + 1) * 128],
                                       kh[par][:, (b0 + bi) * TT:(b0 + bi + 1) * TT],
                                       start=True, stop=True)
                    return ins
                w = list(hl)
                key = (reg is sA, "exp")
                if key in scale_done:
                    w.append((ac, scale_done[key]))
                qk = p.step("tensor", w, mkqk, inc=te)
                mc = p.step("vector", [(te, qk)], lambda e, ln=ln, reg=reg:
                            e.reduce_max(cols[:, 2:3], reg[:, 0:ln * TT],
                                         axis=AX.X), inc=ve)
                ms = p.step("vector", [(ve, mc)],
                            lambda e: e.tensor_scalar(cols[:, 3:4], cols[:, 2:3],
                                                      SCL, None, ALU.mult), inc=ve)
                if cidx == 0:
                    cp = p.step("vector", [(ve, ms)], lambda e: e.tensor_copy(
                        cols[:, 0:1], cols[:, 3:4]), inc=ve)
                    ng = p.step("vector", [(ve, cp)], lambda e: e.tensor_scalar(
                        cols[:, 1:2], cols[:, 0:1], -1.0, None, ALU.mult), inc=ve)
                else:
                    mn = p.step("vector", [(ve, ms)], lambda e: e.tensor_max(
                        cols[:, 3:4], cols[:, 3:4], cols[:, 0:1]), inc=ve)
                    ng = p.step("vector", [(ve, mn)], lambda e: e.tensor_scalar(
                        cols[:, 4:5], cols[:, 3:4], -1.0, None, ALU.mult), inc=ve)
                    al = p.step("scalar", [(ve, ng)], lambda e: e.activation(
                        cols[:, 5:6], cols[:, 0:1], AF.Exp,
                        bias=cols[:, 4:5]), inc=ac)
                    rsw = [(ac, al)]
                    if lastexp:
                        rsw.append(lastexp)
                    rs = p.step("vector", rsw,
                                lambda e, j=j, off=off: e.tensor_scalar(
                                    prow[j][:, 0:off], prow[j][:, 0:off],
                                    cols[:, 5:6], None, ALU.mult), inc=ve)
                    cm = p.step("vector", [(ve, rs)], lambda e: e.tensor_copy(
                        cols[:, 0:1], cols[:, 3:4]), inc=ve)
                    ng = p.step("vector", [(ve, cm)], lambda e: e.tensor_copy(
                        cols[:, 1:2], cols[:, 4:5]), inc=ve)
                ex = p.step("scalar", [(ve, ng)],
                            lambda e, j=j, off=off, ln=ln, reg=reg: e.activation(
                                prow[j][:, off:off + ln * TT], reg[:, 0:ln * TT],
                                AF.Exp, scale=SCL, bias=cols[:, 1:2]), inc=ac)
                scale_done[(reg is sA, "exp")] = ex
                lastexp = (ac, ex)
                off += ln * TT
            mk = emit_mask(i, j, nb, lastexp,
                           (ve, scale_done.get((i - 1, "norm"), 0)))
            L = nb * TT
            sm = p.step("vector", [(gp, mk)], lambda e, j=j, L=L:
                        e.tensor_reduce(cols[:, 6:7], prow[j][:, 0:L], AX.X,
                                        ALU.add), inc=ve)
            rl = p.step("vector", [(ve, sm)], lambda e: e.reciprocal(
                cols[:, 8:9], cols[:, 6:7]), inc=ve)
            nz = p.step("vector", [(ve, rl)],
                        lambda e, j=j, L=L: e.tensor_scalar(
                            prow[j][:, 0:L], prow[j][:, 0:L], cols[:, 8:9],
                            None, ALU.mult), inc=ve)
            scale_done[(i, "norm")] = nz

            if j == 3:
                sup = i // 4
                nsi = 4 * sup + 4
                pvmm = 0
                evacs = {}
                for si in range(nsi):
                    spar = si % 2
                    jmin = max(0, si - 4 * sup)

                    def mktr(e, si=si, spar=spar, jmin=jmin):
                        ins = None
                        for jj in range(jmin, 4):
                            ins = e.transpose(
                                trp[spar][:, jj * 128:(jj + 1) * 128],
                                prow[jj][:, si * 128:(si + 1) * 128],
                                c.ident_bf[:])
                        return ins
                    w = [(ve, scale_done[(4 * sup + 3, "norm")])]
                    for jj in range(jmin, 4):
                        w.append((ve, scale_done[(4 * sup + jj, "norm")]))
                    if si >= 2:
                        w.append(evacs[si - 2])
                    tr = p.step("tensor", w, mktr, inc=te)
                    eng = "vector" if si % 2 == 0 else "scalar"
                    evsem = ve if eng == "vector" else ac
                    if eng == "vector":
                        evfn = (lambda e, spar=spar, jmin=jmin: e.tensor_copy(
                            ptr[spar][:, jmin * 128:TT],
                            trp[spar][:, jmin * 128:TT]))
                    else:
                        evfn = (lambda e, spar=spar, jmin=jmin: e.copy(
                            ptr[spar][:, jmin * 128:TT],
                            trp[spar][:, jmin * 128:TT]))
                    ev = p.step(eng, [(te, tr)], evfn, inc=evsem)
                    evacs[si] = (evsem, ev)

                    def mkpv(e, si=si, spar=spar, jmin=jmin, par=par,
                             first=(si == 0), last=(si == nsi - 1)):
                        return e.matmul(pv[:, jmin * 128:TT],
                                        vt[par][:, si * 128:(si + 1) * 128],
                                        ptr[spar][:, jmin * 128:TT],
                                        start=first, stop=last)
                    w = [evacs[si]]
                    if si == 0:
                        w.append((ac, scale_done.get((sup, "oevac"), 0)))
                    pvmm = p.step("tensor", w, mkpv, inc=te)
                opar = sup % 2
                ow = [(te, pvmm)]
                if (sup - 2, "odma") in scale_done:
                    ow.append((st, scale_done[(sup - 2, "odma")]))
                oe = p.step("scalar", ow, lambda e, opar=opar: e.activation(
                    obuf[opar][:], pv[:], AF.Copy), inc=ac)
                scale_done[(sup + 1, "oevac")] = oe
                scale_done[(sup, "odma")] = p.step(
                    "sync", [(ac, oe)],
                    lambda e, h=h, sup=sup, opar=opar: e.dma_start(
                        out=out_ap[h * 128:(h + 1) * 128,
                                   sup * TT:(sup + 1) * TT],
                        in_=obuf[opar][:]), inc=st, inc_k=16)
        p.barrier()
    p.barrier()
    es.close()


def emit_l1aa(p, c, x_src, x_dst, xn_ap, q_ap, k_ap, vt_ap, ao_ap,
              wqkv_ap, wo_ap):
    ld, ld2, ve, ac, te, st = (p.s[k] for k in ("ld", "ld2", "ve", "ac", "te", "st"))
    es1 = ExitStack()
    emit_rope_tables(p, c, es1)
    xnres = sbuf(p, "sb_xn", [128, NCH * T], BF16, es1)
    bufs = {
        "qb": [sbuf(p, f"sb_qb{i}", [128, TT], BF16, es1) for i in range(2)],
        "rb": [sbuf(p, f"sb_rb{i}", [128, TT], F32, es1) for i in range(2)],
        "tmpa": [sbuf(p, f"sb_ta{i}", [128, TT], F32, es1) for i in range(2)],
        "qr": [sbuf(p, f"sb_qr{i}", [128, TT], BF16, es1) for i in range(2)],
        "vb": [sbuf(p, f"sb_vb{i}", [128, TT], BF16, es1) for i in range(2)],
        "trv": [psum(p, f"sb_trv{i}", [128, TT], BF16, es1) for i in range(2)],
        "trr": [psum(p, f"sb_trr{i}", [128, TT], F32, es1) for i in range(2)],
    }
    xn_ready = emit_resident(p, xn_ap, xnres)
    epi = rope_epi_factory(p, c, q_ap, k_ap, vt_ap, bufs)
    emit_projection(p, c, xnres, xn_ready, wqkv_ap, 48, NCH, epi, name="sbq")
    es1.close()
    p.barrier()

    emit_sparse_attn_core(p, c, q_ap, k_ap, vt_ap, ao_ap)

    es3 = ExitStack()
    aores = sbuf(p, "sb_ao", [128, NCH * T], BF16, es3)
    xo = [sbuf(p, f"sb_xo{i}", [128, TT], F32, es3) for i in range(2)]
    xnew = [sbuf(p, f"sb_xw{i}", [128, TT], F32, es3) for i in range(2)]
    ao_ready = emit_resident(p, ao_ap, aores)
    xl, ep, sd2 = {}, {}, {}

    def epi2(oc, t, par, mmt, ps):
        i = oc * NT + t
        xl[i] = p.step("sync", [(ve, ep.get(i - 2, 0))],
                       lambda e, par=par, oc=oc, t=t: e.dma_start(
                           out=xo[par][:], in_=x_src[oc * 128:(oc + 1) * 128,
                                                     t * TT:(t + 1) * TT]),
                       inc=ld, inc_k=16)
        w = [(te, mmt), (ld, xl[i])]
        if (i - 2) in sd2:
            w.append((st, sd2[i - 2]))
        ep[i] = p.step("vector", w, lambda e, par=par: e.tensor_add(
            xnew[par][:], xo[par][:], ps[:]), inc=ve)
        sd2[i] = p.step("scalar", [(ve, ep[i])],
                        lambda e, par=par, oc=oc, t=t: e.dma_start(
                            out=x_dst[oc * 128:(oc + 1) * 128,
                                      t * TT:(t + 1) * TT],
                            in_=xnew[par][:]), inc=st, inc_k=16)
        return [(ve, ep[i])]

    emit_projection(p, c, aores, ao_ready, wo_ap, NCH, NCH, epi2, name="sbo")
    es3.close()


def prep_l1aa(inp):
    wq = np.asarray(inp["L1_attn_a_w_q"])
    wk = np.asarray(inp["L1_attn_a_w_k"])
    wv = np.asarray(inp["L1_attn_a_w_v"])
    wo = np.asarray(inp["L1_attn_a_w_o"])
    wqkv = np.concatenate([wq.T, wk.T, wv.T], axis=1)
    return {
        "swqkv_t": tile_lhsT(np.ascontiguousarray(wqkv)),
        "swo_t": tile_lhsT(np.ascontiguousarray(wo.T)),
    }


def build_dev_l1aa(dbg=False):
    _uid[0] = 0
    nc = bass.Bass("TRN2", target_bir_lowering=False, debug=False)
    es = ExitStack()
    kind = "ExternalOutput" if dbg else "Internal"
    x_in = nc.dram_tensor("x", [D, T], F32, kind="ExternalInput").ap()
    wqkv = nc.dram_tensor("swqkv_t", [48, 128, 2048], F32, kind="ExternalInput").ap()
    wo = nc.dram_tensor("swo_t", [16, 128, 2048], F32, kind="ExternalInput").ap()
    x_out = nc.dram_tensor("x_out", [D, T], F32, kind="ExternalOutput").ap()
    XN = nc.dram_tensor("XN", [D, T], BF16, kind="Internal").ap()
    QF = nc.dram_tensor("QF", [D, T], BF16, kind=kind).ap()
    KF = nc.dram_tensor("KF", [D, T], BF16, kind="Internal").ap()
    VT = nc.dram_tensor("VT", [NHEAD, 32, 128, 128], BF16, kind="Internal").ap()
    AO = nc.dram_tensor("AO", [D, T], BF16, kind=kind).ap()
    WQKVB = nc.dram_tensor("WQKVB", [48, 128, 2048], BF16, kind="Internal").ap()
    WOB = nc.dram_tensor("WOB", [16, 128, 2048], BF16, kind="Internal").ap()

    p = Prog(nc, es)
    c = emit_consts(p)
    emit_convert(p, [(wqkv, WQKVB), (wo, WOB)])
    emit_ln(p, c, x_in, XN)
    emit_l1aa(p, c, x_in, x_out, XN, QF, KF, VT, AO, WQKVB, WOB)
    p.step("sync", [(p.s["st"], p.s["st"].n)], lambda e: None)
    p.finalize()
    return nc, es


def emit_l1ffn(p, c, x_src, x_dst, xn_ap, xt32_ap, xt1_ap,
               wg_ap, bg_ap, wup_ap, wsw_ap, wdn_ap, bdn_ap,
               wsu_ap, wsa_ap, wsd_ap, bias_aps):
    marks = {}
    ld, ld2, ve, ac, te, st, gp = (p.s[k] for k in
                                   ("ld", "ld2", "ve", "ac", "te", "st", "gp"))
    es = ExitStack()
    bias = sbuf(p, "mf_bias", [128, 144], F32, es)
    emit_vec_cols(p, c, bias_aps["bup"], bias, 0, 48)
    emit_vec_cols(p, c, bias_aps["bsw"], bias, 48, 48)
    emit_vec_cols(p, c, bias_aps["bsu"], bias, 96, 16)
    emit_vec_cols(p, c, bias_aps["bsa"], bias, 112, 8)
    emit_vec_cols(p, c, bias_aps["bsd"], bias, 120, 16)

    bdnf = sbuf(p, "mf_bdnf", [6, D], F32, es)
    bdnb = sbuf(p, "mf_bdnb", [6, D], BF16, es)
    bgf = sbuf(p, "mf_bgf", [1, 16], F32, es)
    bgb = sbuf(p, "mf_bgb", [128, 16], F32, es)
    sel = sbuf(p, "mf_sel", [128, 768], BF16, es)
    wg_sb = sbuf(p, "mf_wg", [128, 2 * 16 * 8], BF16, es)
    bcp = psum(p, "mf_bcp", [128, TT], F32, es)

    l1 = p.step("sync", [], lambda e: e.dma_start(out=bdnf[:], in_=bdn_ap),
                inc=ld, inc_k=16)
    l2 = p.step("sync", [], lambda e: e.dma_start(out=bgf[:], in_=bg_ap),
                inc=ld, inc_k=16)

    def mkwg(e):
        ins = None
        for g in range(32):
            if ins is not None:
                ins.then_inc(ld.h, 16)
            ins = e.dma_start(out=wg_sb[:, g * 8:(g + 1) * 8], in_=wg_ap[g])
        return ins
    _pre = ld.n
    ld.n += 16 * 31
    l3 = p.step("sync", [], mkwg, inc=ld, inc_k=16, dma_pre=_pre)
    v1 = p.step("vector", [(ld, l1)], lambda e: e.tensor_copy(bdnb[:], bdnf[:]),
                inc=ve)
    s1 = p.step("gpsimd", [], lambda e: (
        e.memset(sel[:], 0.0),
        e.affine_select(out=sel.rearrange("p (n m) -> p n m", m=128),
                        in_=sel.rearrange("p (n m) -> p n m", m=128),
                        compare_op=ALU.not_equal, fill=1.0, base=0,
                        pattern=[[-1, 6], [0, 128]], channel_multiplier=1))[-1],
        inc=gp)
    for j in range(2):
        b1 = p.step("tensor", [(ld, l2), (p.s["gp"], 1)],
                    lambda e, j=j: e.matmul(bcp[:, 0:8], c.ones_row_f[:],
                                            bgf[:, j * 8:(j + 1) * 8],
                                            start=True, stop=True), inc=te)
        p.step("vector", [(te, b1)], lambda e, j=j: e.tensor_copy(
            bgb[:, j * 8:(j + 1) * 8], bcp[:, 0:8]), inc=ve)
    p.barrier()

    xts = [sbuf(p, f"mf_x{i}", [128, NCH * TT], BF16, es) for i in range(2)]
    gt = sbuf(p, "mf_gt", [128, 32], F32, es)
    t8a = sbuf(p, "mf_t8a", [128, 8], F32, es)
    t8b = sbuf(p, "mf_t8b", [128, 8], F32, es)
    tcols = sbuf(p, "mf_tc", [128, 8], F32, es)
    wbf = sbuf(p, "mf_wbf", [128, 32], BF16, es)
    wr = sbuf(p, "mf_wr", [8, TT], BF16, es)
    wsl = [sbuf(p, f"mf_wu{i}", [128, 2048], BF16, es) for i in range(2)]
    wssl = [sbuf(p, f"mf_ws{i}", [128, 1024], BF16, es) for i in range(2)]
    hres = sbuf(p, "mf_h", [128, 8 * TT], BF16, es)
    ssb = [sbuf(p, f"mf_ss{i}", [128, TT], F32, es) for i in range(2)]
    gtmp = [sbuf(p, f"mf_gt{i}", [128, TT], BF16, es) for i in range(2)]
    g6 = sbuf(p, "mf_g6", [128, 6 * 8 * TT], BF16, es)
    shres = sbuf(p, "mf_sh", [128, 16 * TT], BF16, es)
    gsres = sbuf(p, "mf_gs", [128, 16 * TT], BF16, es)
    wdnsl = [[sbuf(p, f"mf_wd{i}{n}", [128, 1024], BF16, es) for n in range(6)]
             for i in range(2)]
    wsdsl = [sbuf(p, f"mf_wo{i}", [128, 2048], BF16, es) for i in range(2)]
    res32 = [sbuf(p, f"mf_r{i}", [128, TT], F32, es) for i in range(2)]
    xold = [sbuf(p, f"mf_xo{i}", [128, TT], F32, es) for i in range(2)]
    eout = [sbuf(p, f"mf_eo{i}", [128, TT], F32, es) for i in range(2)]
    ebf = [sbuf(p, f"mf_eb{i}", [128, TT], BF16, es) for i in range(2)]
    gps = psum(p, "mf_gps", [128, 32], F32, es)
    psA = [psum(p, f"mf_pA{i}", [128, TT], F32, es) for i in range(2)]
    psO = [psum(p, f"mf_pO{i}", [128, TT], F32, es) for i in range(2)]
    wtr = psum(p, "mf_wtr", [8, 128], BF16, es)

    for rnd in range(2):
        src_ap = xn_ap if rnd == 0 else xt1_ap
        for sup in range(NT):
            spar = sup % 2
            def mkxl(e, src_ap=src_ap, sup=sup, spar=spar):
                ins = None
                for kc in range(NCH):
                    if ins is not None:
                        ins.then_inc(ld.h, 16)
                    ins = e.dma_start(
                        out=xts[spar][:, kc * TT:(kc + 1) * TT],
                        in_=src_ap[kc * 128:(kc + 1) * 128,
                                   sup * TT:(sup + 1) * TT])
                return ins
            _pre = ld.n
            ld.n += 16 * (NCH - 1)
            xlw = []
            if (sup - 2, rnd, "lastmm") in marks:
                xlw.append((te, marks[(sup - 2, rnd, "lastmm")]))
            elif rnd > 0 and (NT - 2 + sup, rnd - 1, "lastmm") in marks:
                xlw.append((te, marks[(NT - 2 + sup, rnd - 1, "lastmm")]))
            xl = p.step("sync", xlw, mkxl, inc=ld, inc_k=16, dma_pre=_pre)

            gdone = []
            for gi in range(4):
                def mkg(e, gi=gi, spar=spar, rnd=rnd):
                    ins = None
                    for kc in range(NCH):
                        ins = e.matmul(
                            gps[:, gi * 8:(gi + 1) * 8],
                            xts[spar][:, kc * TT + gi * 128:kc * TT + gi * 128 + 128],
                            wg_sb[:, (rnd * 16 + kc) * 8:(rnd * 16 + kc + 1) * 8],
                            start=(kc == 0), stop=(kc == NCH - 1))
                    return ins
                w = [(ld, xl)]
                if gi > 0:
                    w.append((ve, gdone[gi - 1][0]))
                elif (sup - 1, rnd, "glast") in marks:
                    w.append((ve, marks[(sup - 1, rnd, "glast")]))
                gm = p.step("tensor", w, mkg, inc=te)
                ga = p.step("vector", [(te, gm)], lambda e, gi=gi, rnd=rnd:
                            e.tensor_add(gt[:, gi * 8:(gi + 1) * 8],
                                         gps[:, gi * 8:(gi + 1) * 8],
                                         bgb[:, rnd * 8:(rnd + 1) * 8]), inc=ve)
                gdone.append((ga, gm))
            wrdone = []
            for gi in range(4):
                sl = slice(gi * 8, (gi + 1) * 8)
                q0 = p.step("vector", [(ve, gdone[gi][0])],
                            lambda e, sl=sl: e.tensor_reduce(
                                tcols[:, 0:1], gt[:, sl], AX.X, ALU.min), inc=ve)
                q1 = p.step("vector", [(ve, q0)], lambda e, sl=sl: e.tensor_scalar(
                    t8a[:], gt[:, sl], tcols[:, 0:1], None, ALU.is_le), inc=ve)
                q2 = p.step("vector", [(ve, q1)], lambda e, sl=sl:
                            e.scalar_tensor_tensor(t8b[:], t8a[:], 1e9,
                                                   gt[:, sl], ALU.mult,
                                                   ALU.add), inc=ve)
                q3 = p.step("vector", [(ve, q2)], lambda e: e.tensor_reduce(
                    tcols[:, 1:2], t8b[:], AX.X, ALU.min), inc=ve)
                q4 = p.step("vector", [(ve, q3)], lambda e, sl=sl: e.tensor_scalar(
                    t8a[:], gt[:, sl], tcols[:, 1:2], None, ALU.is_gt), inc=ve)
                q5 = p.step("vector", [(ve, q4)], lambda e, sl=sl: e.tensor_reduce(
                    tcols[:, 2:3], gt[:, sl], AX.X, ALU.max), inc=ve)
                q6 = p.step("vector", [(ve, q5)], lambda e: e.tensor_scalar(
                    tcols[:, 3:4], tcols[:, 2:3], -1.0, None, ALU.mult), inc=ve)
                q7 = p.step("scalar", [(ve, q6)], lambda e, sl=sl: e.activation(
                    t8b[:], gt[:, sl], AF.Exp, bias=tcols[:, 3:4]), inc=ac)
                q8 = p.step("vector", [(ac, q7)], lambda e: e.tensor_mul(
                    t8b[:], t8b[:], t8a[:]), inc=ve)
                q9 = p.step("vector", [(ve, q8)], lambda e: e.tensor_reduce(
                    tcols[:, 4:5], t8b[:], AX.X, ALU.add), inc=ve)
                qa = p.step("vector", [(ve, q9)], lambda e: e.reciprocal(
                    tcols[:, 5:6], tcols[:, 4:5]), inc=ve)
                qb_ = p.step("vector", [(ve, qa)], lambda e, sl=sl: e.tensor_scalar(
                    wbf[:, sl], t8b[:], tcols[:, 5:6], None, ALU.mult), inc=ve)
                w = [(ve, qb_), (p.s["gp"], 1)]
                if wrdone:
                    w.append((ve, wrdone[-1]))
                tr = p.step("tensor", w, lambda e, sl=sl: e.transpose(
                    wtr[:], wbf[:, sl], c.ident_bf[:]), inc=te)
                ev = p.step("vector", [(te, tr)], lambda e, gi=gi: e.tensor_copy(
                    wr[:, gi * 128:(gi + 1) * 128], wtr[:]), inc=ve)
                wrdone.append(ev)
            marks[(sup, rnd, "glast")] = wrdone[-1]

            pA_free = {0: marks.get((sup - 1, rnd, "pA0"), 0),
                       1: marks.get((sup - 1, rnd, "pA1"), 0)}
            pA_sem = {0: marks.get((sup - 1, rnd, "pA0s"), ac),
                      1: marks.get((sup - 1, rnd, "pA1s"), ac)}
            bcw_free = (ve, 0)
            wl_prev = {}
            glast = 0
            for n in range(6):
                w = [(ve, wrdone[-1])]
                if bcw_free[1]:
                    w.append(bcw_free)
                bc1 = p.step("tensor", w, lambda e, n=n: e.matmul(
                    bcp[:], sel[0:8, n * 128:(n + 1) * 128], wr[:],
                    start=True, stop=True), inc=te)
                glast_prev = glast if n > 0 else marks.get(
                    (sup - 1, rnd, "g6done"), 0)
                glast = 0
                hdone = []
                for ec in range(8):
                    i = n * 8 + ec
                    epar = i % 2
                    w = []
                    if (i - 2) in wl_prev:
                        w.append((te, wl_prev[i - 2]))
                    wlt = p.step("scalar", w, lambda e, n=n, ec=ec, epar=epar:
                                 e.dma_start(out=wsl[epar][:],
                                             in_=wup_ap[n * 8 + ec]),
                                 inc=ld2, inc_k=16)

                    def mkup(e, ec=ec, epar=epar, spar=spar):
                        ins = None
                        for kc in range(NCH):
                            ins = e.matmul(psA[epar][:],
                                           wsl[epar][:, kc * 128:(kc + 1) * 128],
                                           xts[spar][:, kc * TT:(kc + 1) * TT],
                                           start=(kc == 0), stop=(kc == NCH - 1))
                        return ins
                    w = [(ld2, wlt), (ld, xl)]
                    w.append((pA_sem[epar], pA_free[epar]))
                    um = p.step("tensor", w, mkup, inc=te)
                    wl_prev[i] = um
                    he = p.step("scalar", [(te, um), (ve, glast_prev)],
                                lambda e, n=n, ec=ec, epar=epar: e.activation(
                                    hres[:, ec * TT:(ec + 1) * TT], psA[epar][:],
                                    AF.Identity, bias=bias[:, n * 8 + ec:
                                                          n * 8 + ec + 1]),
                                inc=ac)
                    pA_free[epar], pA_sem[epar] = he, ac
                    hdone.append(he)
                for fc in range(8):
                    i = n * 8 + fc + 1000
                    epar = fc % 2
                    w = []
                    if (i - 2) in wl_prev:
                        w.append((te, wl_prev[i - 2]))
                    wlt = p.step("scalar", w, lambda e, n=n, fc=fc, epar=epar:
                                 e.dma_start(out=wssl[epar][:],
                                             in_=wsw_ap[n * 8 + fc]),
                                 inc=ld2, inc_k=16)

                    def mksw(e, fc=fc, epar=epar):
                        ins = None
                        for ec in range(8):
                            ins = e.matmul(psA[epar][:],
                                           wssl[epar][:, ec * 128:(ec + 1) * 128],
                                           hres[:, ec * TT:(ec + 1) * TT],
                                           start=(ec == 0), stop=(ec == 7))
                        return ins
                    w = [(ld2, wlt), (ac, hdone[-1])]
                    w.append((pA_sem[epar], pA_free[epar]))
                    sm_ = p.step("tensor", w, mksw, inc=te)
                    wl_prev[i] = sm_
                    sle = p.step("scalar",
                                 [(te, sm_), (ve, wl_prev.get((n, fc - 2, "g1"), 0))],
                                 lambda e, n=n, fc=fc, epar=epar: e.activation(
                                     ssb[epar][:], psA[epar][:], AF.Silu,
                                     bias=bias[:, 48 + n * 8 + fc:
                                               49 + n * 8 + fc]), inc=ac)
                    pA_free[epar], pA_sem[epar] = sle, ac
                    g1 = p.step("vector", [(ac, sle)],
                                lambda e, fc=fc, epar=epar: e.tensor_mul(
                                    gtmp[epar][:], ssb[epar][:],
                                    hres[:, fc * TT:(fc + 1) * TT]), inc=ve)
                    wl_prev[(n, fc, "g1")] = g1
                    g2 = p.step("vector", [(ve, g1), (te, bc1)],
                                lambda e, n=n, fc=fc, epar=epar: e.tensor_mul(
                                    g6[:, (n * 8 + fc) * TT:(n * 8 + fc + 1) * TT],
                                    gtmp[epar][:], bcp[:]), inc=ve)
                    glast = g2
                bcw_free = (ve, glast)
            marks[(sup, rnd, "g6done")] = glast

            shdone = []
            for oc2 in range(16):
                i = oc2 + 2000
                epar = oc2 % 2
                w = []
                if (i - 2) in wl_prev:
                    w.append((te, wl_prev[i - 2]))
                wlt = p.step("scalar", w, lambda e, oc2=oc2, epar=epar:
                             e.dma_start(out=wsl[epar][:], in_=wsu_ap[oc2]),
                             inc=ld2, inc_k=16)

                def mksu(e, oc2=oc2, epar=epar, spar=spar):
                    ins = None
                    for kc in range(NCH):
                        ins = e.matmul(psA[epar][:],
                                       wsl[epar][:, kc * 128:(kc + 1) * 128],
                                       xts[spar][:, kc * TT:(kc + 1) * TT],
                                       start=(kc == 0), stop=(kc == NCH - 1))
                    return ins
                w = [(ld2, wlt), (ld, xl)]
                w.append((pA_sem[epar], pA_free[epar]))
                um = p.step("tensor", w, mksu, inc=te)
                wl_prev[i] = um
                he = p.step("scalar", [(te, um)],
                            lambda e, oc2=oc2, epar=epar: e.activation(
                                shres[:, oc2 * TT:(oc2 + 1) * TT], psA[epar][:],
                                AF.Identity, bias=bias[:, 96 + oc2:97 + oc2]),
                            inc=ac)
                pA_free[epar], pA_sem[epar] = he, ac
                shdone.append(he)
            gsl = 0
            for cc in range(2):
                for fc2 in range(8):
                    i = cc * 8 + fc2 + 3000
                    epar = (cc * 8 + fc2) % 2
                    w = []
                    if (i - 2) in wl_prev:
                        w.append((te, wl_prev[i - 2]))
                    wlt = p.step("scalar", w, lambda e, fc2=fc2, epar=epar:
                                 e.dma_start(out=wssl[epar][:], in_=wsa_ap[fc2]),
                                 inc=ld2, inc_k=16)

                    def mksa(e, cc=cc, fc2=fc2, epar=epar):
                        ins = None
                        for ec in range(8):
                            ins = e.matmul(
                                psA[epar][:],
                                wssl[epar][:, ec * 128:(ec + 1) * 128],
                                shres[:, (cc * 8 + ec) * TT:(cc * 8 + ec + 1) * TT],
                                start=(ec == 0), stop=(ec == 7))
                        return ins
                    w = [(ld2, wlt), (ac, shdone[-1])]
                    w.append((pA_sem[epar], pA_free[epar]))
                    sm_ = p.step("tensor", w, mksa, inc=te)
                    wl_prev[i] = sm_
                    sle = p.step("scalar",
                                 [(te, sm_),
                                  (ve, wl_prev.get((cc * 8 + fc2 - 2, "gsl"), 0))],
                                 lambda e, fc2=fc2, epar=epar: e.activation(
                                     ssb[epar][:], psA[epar][:], AF.Silu,
                                     bias=bias[:, 112 + fc2:113 + fc2]), inc=ac)
                    pA_free[epar], pA_sem[epar] = sle, ac
                    gsl = p.step("vector", [(ac, sle)],
                                 lambda e, cc=cc, fc2=fc2, epar=epar: e.tensor_mul(
                                     gsres[:, (cc * 8 + fc2) * TT:
                                           (cc * 8 + fc2 + 1) * TT],
                                     ssb[epar][:],
                                     shres[:, (cc * 8 + fc2) * TT:
                                           (cc * 8 + fc2 + 1) * TT]), inc=ve)
                    wl_prev[(cc * 8 + fc2, "gsl")] = gsl
            marks[(sup, rnd, "gsdone")] = gsl

            lastmm = 0
            for oc in range(NCH):
                opar = oc % 2

                def mkdl(e, oc=oc, opar=opar):
                    ins = None
                    for n in range(6):
                        if ins is not None:
                            ins.then_inc(ld2.h, 16)
                        ins = e.dma_start(out=wdnsl[opar][n][:],
                                          in_=wdn_ap[n * 16 + oc])
                    ins.then_inc(ld2.h, 16)
                    return e.dma_start(out=wsdsl[opar][:], in_=wsd_ap[oc])
                w = []
                if (oc - 2, "dl") in wl_prev:
                    w.append((te, wl_prev[(oc - 2, "dl")]))
                elif (sup - 1, rnd, "lastmm") in marks:
                    w.append((te, marks[(sup - 1, rnd, "lastmm")]))
                elif rnd > 0 and (NT - 1, rnd - 1, "lastmm") in marks:
                    w.append((te, marks[(NT - 1, rnd - 1, "lastmm")]))
                _pre = ld2.n
                ld2.n += 16 * 6
                dlt = p.step("scalar", w, mkdl, inc=ld2, inc_k=16, dma_pre=_pre)

                def mkacc(e, oc=oc, opar=opar):
                    first = True
                    for n in range(6):
                        for fc in range(8):
                            e.matmul(psO[opar][:],
                                     wdnsl[opar][n][:, fc * 128:(fc + 1) * 128],
                                     g6[:, (n * 8 + fc) * TT:(n * 8 + fc + 1) * TT],
                                     start=first, stop=False)
                            first = False
                    e.matmul(psO[opar][:], bdnb[0:6, oc * 128:(oc + 1) * 128],
                             wr[0:6, :], start=False, stop=False)
                    ins = None
                    for o2 in range(NCH):
                        ins = e.matmul(psO[opar][:],
                                       wsdsl[opar][:, o2 * 128:(o2 + 1) * 128],
                                       gsres[:, o2 * TT:(o2 + 1) * TT],
                                       start=False, stop=(o2 == NCH - 1))
                    return ins
                w = [(ld2, dlt), (ve, glast), (ve, gsl)]
                if (oc - 2, "epi") in wl_prev:
                    w.append(wl_prev[(oc - 2, "epi")])
                elif (sup - 1, rnd, f"epi{oc % 2}") in marks:
                    w.append(marks[(sup - 1, rnd, f"epi{oc % 2}")])
                elif rnd > 0 and (NT - 1, rnd - 1, f"epi{oc % 2}") in marks:
                    w.append(marks[(NT - 1, rnd - 1, f"epi{oc % 2}")])
                am = p.step("tensor", w, mkacc, inc=te)
                wl_prev[(oc, "dl")] = am
                lastmm = am

                resw = [(ve, wl_prev.get((oc - 2, "res"), 0))]
                if (oc - 2, "res") not in wl_prev:
                    for key in ((sup - 1, rnd, f"res{oc % 2}"),
                                (NT - 1, rnd - 1, f"res{oc % 2}")):
                        if key in marks:
                            resw = [(ve, marks[key])]
                            break
                rl_ = p.step("sync", resw,
                             lambda e, oc=oc, sup=sup, opar=opar: e.dma_start(
                                 out=res32[opar][:],
                                 in_=xt32_ap[oc * 128:(oc + 1) * 128,
                                             sup * TT:(sup + 1) * TT]),
                             inc=ld, inc_k=16)
                stw = wl_prev.get((oc - 2, "st"))
                if stw is None:
                    for key in ((sup - 1, rnd, f"st{oc % 2}"),
                                (NT - 1, rnd - 1, f"st{oc % 2}")):
                        if key in marks:
                            stw = (st, marks[key])
                            break
                    else:
                        stw = (ve, 0)
                if rnd == 0:
                    ep = p.step("vector", [(te, am), (ld, rl_), stw],
                                lambda e, oc=oc, opar=opar:
                                e.scalar_tensor_tensor(
                                    ebf[opar][:], psO[opar][:],
                                    bias[:, 120 + oc:121 + oc], res32[opar][:],
                                    ALU.add, ALU.add), inc=ve)
                    sd_ = p.step("scalar", [(ve, ep)],
                                 lambda e, oc=oc, sup=sup, opar=opar: e.dma_start(
                                     out=xt1_ap[oc * 128:(oc + 1) * 128,
                                                sup * TT:(sup + 1) * TT],
                                     in_=ebf[opar][:]), inc=st, inc_k=16)
                else:
                    xl2 = p.step("sync", resw,
                                 lambda e, oc=oc, sup=sup, opar=opar: e.dma_start(
                                     out=xold[opar][:],
                                     in_=x_src[oc * 128:(oc + 1) * 128,
                                               sup * TT:(sup + 1) * TT]),
                                 inc=ld, inc_k=16)
                    e1 = p.step("vector", [(te, am), (ld, rl_), (ld, xl2),
                                           stw],
                                lambda e, oc=oc, opar=opar:
                                e.scalar_tensor_tensor(
                                    eout[opar][:], psO[opar][:],
                                    bias[:, 120 + oc:121 + oc], res32[opar][:],
                                    ALU.add, ALU.add), inc=ve)
                    ep = p.step("vector", [(ve, e1)],
                                lambda e, opar=opar: e.tensor_add(
                                    eout[opar][:], eout[opar][:],
                                    xold[opar][:]), inc=ve)
                    sd_ = p.step("scalar", [(ve, ep)],
                                 lambda e, oc=oc, sup=sup, opar=opar: e.dma_start(
                                     out=x_dst[oc * 128:(oc + 1) * 128,
                                               sup * TT:(sup + 1) * TT],
                                     in_=eout[opar][:]), inc=st, inc_k=16)
                wl_prev[(oc, "res")] = ep
                wl_prev[(oc, "epi")] = (ve, ep)
                wl_prev[(oc, "st")] = (st, sd_)
                if oc >= NCH - 2:
                    marks[(sup, rnd, f"epi{oc % 2}")] = (ve, ep)
                    marks[(sup, rnd, f"res{oc % 2}")] = ep
                    marks[(sup, rnd, f"st{oc % 2}")] = sd_
            marks[(sup, rnd, "lastmm")] = lastmm
            marks[(sup, rnd, "pA0")] = pA_free[0]
            marks[(sup, rnd, "pA0s")] = pA_sem[0]
            marks[(sup, rnd, "pA1")] = pA_free[1]
            marks[(sup, rnd, "pA1s")] = pA_sem[1]
        p.barrier()
    p.barrier()
    es.close()


marks = {}

def prep_l1ffn(inp):
    Wg = np.asarray(inp["L1_ffn_Wg"])
    Wup = np.asarray(inp["L1_ffn_Wup"])
    Wsw = np.asarray(inp["L1_ffn_Wsw"])
    Wdn = np.asarray(inp["L1_ffn_Wdn"])
    Wsu = np.asarray(inp["L1_ffn_Wsu"])
    Wsa = np.asarray(inp["L1_ffn_Wsa"])
    Wsd = np.asarray(inp["L1_ffn_Wsd"])
    wg_t = np.ascontiguousarray(
        Wg.reshape(2, 16, 128, 8).reshape(32, 128, 8).astype(np.float32))
    wup_t = np.concatenate([tile_lhsT(Wup[n].astype(np.float32))
                            for n in range(6)], axis=0)
    wsw_t = np.concatenate([tile_lhsT(Wsw[n].astype(np.float32))
                            for n in range(6)], axis=0)
    wdn_t = np.concatenate([tile_lhsT(Wdn[n].astype(np.float32))
                            for n in range(6)], axis=0)
    return {
        "wg_t": wg_t, "wup_t": wup_t, "wsw_t": wsw_t, "wdn_t": wdn_t,
        "wsu_t": tile_lhsT(Wsu.astype(np.float32)),
        "wsa_t": tile_lhsT(Wsa.astype(np.float32)),
        "wsd_t": tile_lhsT(Wsd.astype(np.float32)),
        "bg": np.ascontiguousarray(
            np.asarray(inp["L1_ffn_bg"]).reshape(1, 16).astype(np.float32)),
        "bdn": np.ascontiguousarray(np.asarray(inp["L1_ffn_bdn"]).astype(np.float32)),
        "bup": np.ascontiguousarray(
            np.asarray(inp["L1_ffn_bup"]).reshape(48, 128).astype(np.float32)),
        "bsw": np.ascontiguousarray(
            np.asarray(inp["L1_ffn_bsw"]).reshape(48, 128).astype(np.float32)),
        "bsu": np.ascontiguousarray(
            np.asarray(inp["L1_ffn_bsu"]).reshape(16, 128).astype(np.float32)),
        "bsa": np.ascontiguousarray(
            np.asarray(inp["L1_ffn_bsa"]).reshape(8, 128).astype(np.float32)),
        "bsd": np.ascontiguousarray(
            np.asarray(inp["L1_ffn_bsd"]).reshape(16, 128).astype(np.float32)),
    }


def decl_l1ffn(nc, pref=""):
    d = {}
    d["wg_t"] = nc.dram_tensor("wg_t", [32, 128, 8], F32, kind="ExternalInput").ap()
    d["wup_t"] = nc.dram_tensor("wup_t", [48, 128, 2048], F32, kind="ExternalInput").ap()
    d["wsw_t"] = nc.dram_tensor("wsw_t", [48, 128, 1024], F32, kind="ExternalInput").ap()
    d["wdn_t"] = nc.dram_tensor("wdn_t", [96, 128, 1024], F32, kind="ExternalInput").ap()
    d["wsu_t"] = nc.dram_tensor("wsu_t", [16, 128, 2048], F32, kind="ExternalInput").ap()
    d["wsa_t"] = nc.dram_tensor("wsa_t", [8, 128, 1024], F32, kind="ExternalInput").ap()
    d["wsd_t"] = nc.dram_tensor("wsd_t", [16, 128, 2048], F32, kind="ExternalInput").ap()
    d["bg"] = nc.dram_tensor("bg", [1, 16], F32, kind="ExternalInput").ap()
    d["bdn"] = nc.dram_tensor("bdn", [6, 2048], F32, kind="ExternalInput").ap()
    for nm, g in (("bup", 48), ("bsw", 48), ("bsu", 16), ("bsa", 8), ("bsd", 16)):
        d[nm] = nc.dram_tensor(nm, [g, 128], F32, kind="ExternalInput").ap()
    d["WGB"] = nc.dram_tensor("WGB", [32, 128, 8], BF16, kind="Internal").ap()
    d["WUPB"] = nc.dram_tensor("WUPB", [48, 128, 2048], BF16, kind="Internal").ap()
    d["WSWB"] = nc.dram_tensor("WSWB", [48, 128, 1024], BF16, kind="Internal").ap()
    d["WDNB"] = nc.dram_tensor("WDNB", [96, 128, 1024], BF16, kind="Internal").ap()
    d["WSUB"] = nc.dram_tensor("WSUB", [16, 128, 2048], BF16, kind="Internal").ap()
    d["WSAB"] = nc.dram_tensor("WSAB", [8, 128, 1024], BF16, kind="Internal").ap()
    d["WSDB"] = nc.dram_tensor("WSDB", [16, 128, 2048], BF16, kind="Internal").ap()
    return d


def emit_l1ffn_sub(p, c, d, x_src, x_dst, XN, XT32, XT1):
    emit_l1ffn(p, c, x_src, x_dst, XN, XT32, XT1,
               d["WGB"], d["bg"], d["WUPB"], d["WSWB"], d["WDNB"], d["bdn"],
               d["WSUB"], d["WSAB"], d["WSDB"],
               {k: d[k] for k in ("bup", "bsw", "bsu", "bsa", "bsd")})


def build_dev_l1ffn(dbg=False):
    _uid[0] = 0
    nc = bass.Bass("TRN2", target_bir_lowering=False, debug=False)
    es = ExitStack()
    kind = "ExternalOutput" if dbg else "Internal"
    x_in = nc.dram_tensor("x", [D, T], F32, kind="ExternalInput").ap()
    x_out = nc.dram_tensor("x_out", [D, T], F32, kind="ExternalOutput").ap()
    d = decl_l1ffn(nc)
    XN = nc.dram_tensor("XN", [D, T], BF16, kind=kind).ap()
    XT32 = nc.dram_tensor("XT32", [D, T], F32, kind=kind).ap()
    XT1 = nc.dram_tensor("XT1", [D, T], BF16, kind=kind).ap()

    p = Prog(nc, es)
    c = emit_consts(p)
    emit_convert(p, [(d["wg_t"], d["WGB"]), (d["wup_t"], d["WUPB"]),
                     (d["wsw_t"], d["WSWB"]), (d["wdn_t"], d["WDNB"]),
                     (d["wsu_t"], d["WSUB"]), (d["wsa_t"], d["WSAB"]),
                     (d["wsd_t"], d["WSDB"])])
    emit_ln(p, c, x_in, XN, xt32_ap=XT32)
    emit_l1ffn_sub(p, c, d, x_in, x_out, XN, XT32, XT1)
    p.step("sync", [(p.s["st"], p.s["st"].n)], lambda e: None)
    p.finalize()
    return nc, es


def kernel_unfused(**inputs):
    xs = np.asarray(inputs["L0_attn_a_x"])
    ncores = xs.shape[0]
    shards = [np.ascontiguousarray(xs[i].T) for i in range(ncores)]
    t_aa = prep_l0aa(inputs)
    t_ff = prep_l0ffn(inputs)
    t_ab = prep_l0ab(inputs)
    t_sa = prep_l1aa(inputs)
    t_cb = prep_l1ab(inputs)
    t_mf = prep_l1ffn(inputs)
    seq = [
        (build_dev_l0aa, t_aa),
        (build_dev_l0ffn, t_ff),
        (build_dev_l0ab, t_ab),
        (build_dev_l0ffn, t_ff),
        (build_dev_l1aa, t_sa),
        (build_dev_l1ffn, t_mf),
        (build_dev_l1ab, t_cb),
        (build_dev_l1ffn, t_mf),
    ]
    cur = shards
    for build, tiles in seq:
        nc, es = build()
        in_maps = [{"x": cur[i], **tiles} for i in range(ncores)]
        res = run_bass_kernel_spmd(nc, in_maps, core_ids=list(range(ncores)))
        cur = [np.ascontiguousarray(res.results[i]["x_out"])
               for i in range(ncores)]
        es.close()
    return np.stack([c.T for c in cur], axis=0)


def build_fused():
    _uid[0] = 0
    nc = bass.Bass("TRN2", target_bir_lowering=False, debug=False)
    es = ExitStack()
    EIn = "ExternalInput"
    x_in = nc.dram_tensor("x", [D, T], F32, kind=EIn).ap()
    y_out = nc.dram_tensor("y_out", [D, T], F32, kind="ExternalOutput").ap()
    aw = nc.dram_tensor("awqkv_t", [48, 128, 2048], F32, kind=EIn).ap()
    awo = nc.dram_tensor("awo_t", [16, 128, 2048], F32, kind=EIn).ap()
    w1t = nc.dram_tensor("w1t", [NHC, 128, 2048], F32, kind=EIn).ap()
    w3t = nc.dram_tensor("w3t", [NHC, 128, 2048], F32, kind=EIn).ap()
    w2t = nc.dram_tensor("w2t", [NCH, 128, HPAD], F32, kind=EIn).ap()
    lbw = nc.dram_tensor("wqkv_t", [48, 128, 2048], F32, kind=EIn).ap()
    lbo = nc.dram_tensor("wo_t", [16, 128, 2048], F32, kind=EIn).ap()
    bqkv = nc.dram_tensor("bqkv", [48, 128], F32, kind=EIn).ap()
    bo = nc.dram_tensor("bo", [16, 128], F32, kind=EIn).ap()
    wmq = nc.dram_tensor("wmq_t", [1, 128, 64], F32, kind=EIn).ap()
    wmk = nc.dram_tensor("wmk_t", [1, 128, 64], F32, kind=EIn).ap()
    sw = nc.dram_tensor("swqkv_t", [48, 128, 2048], F32, kind=EIn).ap()
    swo = nc.dram_tensor("swo_t", [16, 128, 2048], F32, kind=EIn).ap()
    cattn = nc.dram_tensor("cattn_t", [48, 128, 2048], F32, kind=EIn).ap()
    cproj = nc.dram_tensor("cproj_t", [16, 128, 2048], F32, kind=EIn).ap()
    cab = nc.dram_tensor("cab", [48, 128], F32, kind=EIn).ap()
    cpb = nc.dram_tensor("cpb", [16, 128], F32, kind=EIn).ap()
    dmf = decl_l1ffn(nc)
    AWB = nc.dram_tensor("AWB", [48, 128, 2048], BF16, kind="Internal").ap()
    AWOB = nc.dram_tensor("AWOB", [16, 128, 2048], BF16, kind="Internal").ap()
    W1Q = nc.dram_tensor("W1Q", [NHC, 128, 2048], BF16, kind="Internal").ap()
    W3Q = nc.dram_tensor("W3Q", [NHC, 128, 2048], BF16, kind="Internal").ap()
    W2Q = nc.dram_tensor("W2Q", [NCH, 128, HPAD], BF16, kind="Internal").ap()
    LBWB = nc.dram_tensor("LBWB", [48, 128, 2048], BF16, kind="Internal").ap()
    LBOB = nc.dram_tensor("LBOB", [16, 128, 2048], BF16, kind="Internal").ap()
    WMQB = nc.dram_tensor("WMQB", [1, 128, 64], BF16, kind="Internal").ap()
    WMKB = nc.dram_tensor("WMKB", [1, 128, 64], BF16, kind="Internal").ap()
    SWB = nc.dram_tensor("SWB", [48, 128, 2048], BF16, kind="Internal").ap()
    SWOB = nc.dram_tensor("SWOB", [16, 128, 2048], BF16, kind="Internal").ap()
    CATB = nc.dram_tensor("CATB", [48, 128, 2048], BF16, kind="Internal").ap()
    CPRB = nc.dram_tensor("CPRB", [16, 128, 2048], BF16, kind="Internal").ap()
    XA = nc.dram_tensor("XA", [D, T], F32, kind="Internal").ap()
    XB = nc.dram_tensor("XB", [D, T], F32, kind="Internal").ap()
    XN = nc.dram_tensor("XN", [D, T], BF16, kind="Internal").ap()
    XT32 = nc.dram_tensor("XT32", [D, T], F32, kind="Internal").ap()
    XT1 = nc.dram_tensor("XT1", [D, T], BF16, kind="Internal").ap()
    QF = nc.dram_tensor("QF", [D, T], BF16, kind="Internal").ap()
    KF = nc.dram_tensor("KF", [D, T], BF16, kind="Internal").ap()
    QM = nc.dram_tensor("QM", [1024, T], BF16, kind="Internal").ap()
    KM = nc.dram_tensor("KM", [1024, T], BF16, kind="Internal").ap()
    VT = nc.dram_tensor("VT", [NHEAD, 32, 128, 128], BF16, kind="Internal").ap()
    AO = nc.dram_tensor("AO", [D, T], BF16, kind="Internal").ap()
    G = nc.dram_tensor("G", [NHC, 128, T], BF16, kind="Internal").ap()

    p = Prog(nc, es)
    c = emit_consts(p)
    emit_convert(p, [(aw, AWB), (awo, AWOB), (lbw, LBWB), (lbo, LBOB),
                     (wmq, WMQB), (wmk, WMKB), (sw, SWB), (swo, SWOB),
                     (cattn, CATB), (cproj, CPRB),
                     (dmf["wg_t"], dmf["WGB"]), (dmf["wup_t"], dmf["WUPB"]),
                     (dmf["wsw_t"], dmf["WSWB"]), (dmf["wdn_t"], dmf["WDNB"]),
                     (dmf["wsu_t"], dmf["WSUB"]), (dmf["wsa_t"], dmf["WSAB"]),
                     (dmf["wsd_t"], dmf["WSDB"])])
    emit_quant(p, c, [(w1t, W1Q, H0REAL * D, 0), (w3t, W3Q, H0REAL * D, 1),
                      (w2t, W2Q, H0REAL * D, 2)])

    emit_ln(p, c, x_in, XN)
    emit_l0aa(p, c, x_in, XA, XN, QF, KF, VT, AO, AWB, AWOB)
    emit_ln(p, c, XA, XN)
    emit_l0ffn(p, c, XA, XB, XN, G, W1Q, W3Q, W2Q)
    emit_ln(p, c, XB, XN)
    emit_l0ab(p, c, XB, XA, XN, QM, KM, VT, AO, LBWB, LBOB, bqkv, bo,
              WMQB, WMKB)
    emit_ln(p, c, XA, XN)
    emit_l0ffn(p, c, XA, XB, XN, G, W1Q, W3Q, W2Q)
    emit_ln(p, c, XB, XN)
    emit_l1aa(p, c, XB, XA, XN, QF, KF, VT, AO, SWB, SWOB)
    emit_ln(p, c, XA, XN, xt32_ap=XT32)
    emit_l1ffn_sub(p, c, dmf, XA, XB, XN, XT32, XT1)
    emit_ln(p, c, XB, XN)
    emit_l1ab(p, c, XB, XA, XN, QF, KF, VT, AO, CATB, CPRB, cab, cpb)
    emit_ln(p, c, XA, XN, xt32_ap=XT32)
    emit_l1ffn_sub(p, c, dmf, XA, y_out, XN, XT32, XT1)

    p.step("sync", [(p.s["st"], p.s["st"].n)], lambda e: None)
    p.finalize()
    return nc, es


def kernel(**inputs):
    xs = np.asarray(inputs["L0_attn_a_x"])
    ncores = xs.shape[0]
    shards = [np.ascontiguousarray(xs[i].T) for i in range(ncores)]
    tiles = {}
    for prep in (prep_l0aa, prep_l0ffn, prep_l0ab, prep_l1aa, prep_l1ab,
                 prep_l1ffn):
        tiles.update(prep(inputs))
    nc, es = build_fused()
    in_maps = [{"x": shards[i], **tiles} for i in range(ncores)]
    res = run_bass_kernel_spmd(nc, in_maps, core_ids=list(range(ncores)))
    out = np.stack([np.ascontiguousarray(res.results[i]["y_out"]).T
                    for i in range(ncores)], axis=0)
    es.close()
    return out
